# version 07
# baseline (speedup 1.0000x reference)
import contextlib
import numpy as np
import ml_dtypes
import concourse.bass as bass
import concourse.mybir as mybir
from concourse.bass_utils import run_bass_kernel_spmd

F32 = mybir.dt.float32
BF16 = mybir.dt.bfloat16
AF = mybir.ActivationFunctionType
ALU = mybir.AluOpType
AX = mybir.AxisListType

B, S, H, DM = 8, 4096, 1024, 256
E, DFF = 8, 4096
CAP = 5120.0
EPS_ATT = 1e-6
EPS_LN = 1e-5
CHUNK = 512
NCH = S // CHUNK
NT = CHUNK // 128
KT = H // 128
NCHT = B * NCH
NFT = 32
FFN_CHUNKS = [(i * 384, 3) for i in range(10)] + [(3840, 2)]
NCI = len(FFN_CHUNKS)
NU = NCI * E

PACE1 = 160
PACE2 = 224
PACEF = 640
CDEL = 1536
SDEL = 768
SEARCH_ITERS = 27
DEV_EXPOSE = False
DEV_SKIP_CONV = False
DEV_SKIP_SEARCH = False
_BASS_KW = {}

VX_LN1, VX_LNB1, VX_SIG, VX_MSK = 1, 2, 3, 4
VX_CMB0 = 5
VX_SREF, VX_X2LN, VX_LNB2, VX_OWN, VX_TOP1, VX_PROB = 9, 10, 11, 12, 13, 14
VX_A = 14
EV_RSTD1 = 1
EV_XLNT0 = 2
EV_QK, EV_V, EV_SCM, EV_SKTM = 6, 7, 8, 9
EV_NM0 = 10
EV_CMBT, EV_SUP = 14, 15
EV_AO0 = 16
EV_RSTD2 = 20
EV_X2T0 = 21
EV_LGT, EV_LGTM, EV_TEXP = 29, 30, 31
EV_A = 31
TP_XLN0 = 1
TP_QK, TP_V, TP_SC, TP_SKT = 5, 6, 7, 8
TP_NM0 = 9
TP_CMBT, TP_SUP = 13, 14
TP_AO0 = 15
TP_X2T0 = 19
TP_LG, TP_LGT = 27, 28
TP_A = 28
LN_A = 2
SETUP_TP = 12
SETUP_EV = 12
SET_TOTAL = 6
CV_TOTAL = (E * KT * 4 + E * 16 * 2) * 16


def build_nc():
    nc = bass.Bass("TRN2", target_bir_lowering=False, **_BASS_KW)
    x_all = nc.dram_tensor("x_all", [B, S, H], F32, kind="ExternalInput").ap()
    onehot = nc.dram_tensor("onehot", [1, B], F32, kind="ExternalInput").ap()
    w_q = nc.dram_tensor("w_q", [DM, H], F32, kind="ExternalInput").ap()
    w_k = nc.dram_tensor("w_k", [DM, H], F32, kind="ExternalInput").ap()
    w_v = nc.dram_tensor("w_v", [DM, H], F32, kind="ExternalInput").ap()
    w_o = nc.dram_tensor("w_o", [H, DM], F32, kind="ExternalInput").ap()
    gate = nc.dram_tensor("gate", [1, 1], F32, kind="ExternalInput").ap()
    memory = nc.dram_tensor("memory", [DM, DM], F32, kind="ExternalInput").ap()
    memnorm = nc.dram_tensor("memnorm", [1, DM], F32, kind="ExternalInput").ap()
    router = nc.dram_tensor("router", [E, H], F32, kind="ExternalInput").ap()
    W1 = nc.dram_tensor("W1", [E, H, DFF], F32, kind="ExternalInput").ap()
    b1in = nc.dram_tensor("b1in", [E, DFF], F32, kind="ExternalInput").ap()
    W2 = nc.dram_tensor("W2", [E, DFF, H], F32, kind="ExternalInput").ap()
    b2in = nc.dram_tensor("b2in", [E, H], F32, kind="ExternalInput").ap()
    identf = nc.dram_tensor("identf", [128, 128], F32, kind="ExternalInput").ap()
    identb = nc.dram_tensor("identb", [128, 128], BF16, kind="ExternalInput").ap()
    trimask_in = nc.dram_tensor("trimask_in", [128, 4 * 512], BF16, kind="ExternalInput").ap()
    iotac = nc.dram_tensor("iotac", [128, 16], F32, kind="ExternalInput").ap()
    out = nc.dram_tensor("out", [S, H], F32, kind="ExternalOutput").ap()
    x2_scr = nc.dram_tensor("x2_scr", [S, H], F32,
                            kind=("ExternalOutput" if DEV_EXPOSE else "Internal"))
    x2ln_dram32 = nc.dram_tensor("x2ln_dram", [KT, 128, S // 2], F32,
                                 kind=("ExternalOutput" if DEV_EXPOSE else "Internal"))
    x2ln_dram = x2ln_dram32.bitcast(BF16)
    W1bf = nc.dram_tensor("W1bf", [E, NFT, KT, 128, 128], BF16, kind="Internal")
    W2bf = nc.dram_tensor("W2bf", [E, NFT, 128, H], BF16, kind="Internal")

    _ctx1 = contextlib.ExitStack()
    id_f32 = _ctx1.enter_context(nc.sbuf_tensor("id_f32", [128, 128], F32))
    id_bf = _ctx1.enter_context(nc.sbuf_tensor("id_bf", [128, 128], BF16))
    iota8 = _ctx1.enter_context(nc.sbuf_tensor("iota8", [128, 16], F32))
    ones_row = _ctx1.enter_context(nc.sbuf_tensor("ones_row", [1, 128], F32))
    ones_col_bf = _ctx1.enter_context(nc.sbuf_tensor("ones_col_bf", [128, 1], BF16))
    b1s = _ctx1.enter_context(nc.sbuf_tensor("b1s", [128, 256], F32))
    b2s = _ctx1.enter_context(nc.sbuf_tensor("b2s", [8, H], F32))
    ohbc = _ctx1.enter_context(nc.sbuf_tensor("ohbc", [128, 8], F32))
    epsc = _ctx1.enter_context(nc.sbuf_tensor("epsc", [128, 1], F32))
    P_all = _ctx1.enter_context(nc.sbuf_tensor("P_all", [128, 256], F32))
    E_all = _ctx1.enter_context(nc.sbuf_tensor("E_all", [128, 256], F32))
    s_pre = _ctx1.enter_context(nc.semaphore("s_pre"))
    s_set = _ctx1.enter_context(nc.semaphore("s_set"))
    s_xld = _ctx1.enter_context(nc.semaphore("s_xld"))
    s_cv = _ctx1.enter_context(nc.semaphore("s_cv"))
    s_ln = _ctx1.enter_context(nc.semaphore("s_ln"))
    s_tp = _ctx1.enter_context(nc.semaphore("s_tp"))
    s_ev = _ctx1.enter_context(nc.semaphore("s_ev"))
    s_vx = _ctx1.enter_context(nc.semaphore("s_vx"))
    s_sc = _ctx1.enter_context(nc.semaphore("s_sc"))
    s_srch = _ctx1.enter_context(nc.semaphore("s_srch"))
    s_fw = _ctx1.enter_context(nc.semaphore("s_fw"))
    s_fx = _ctx1.enter_context(nc.semaphore("s_fx"))
    s_l1 = _ctx1.enter_context(nc.semaphore("s_l1"))
    s_rl = _ctx1.enter_context(nc.semaphore("s_rl"))
    s_yd = _ctx1.enter_context(nc.semaphore("s_yd"))
    s_yc = _ctx1.enter_context(nc.semaphore("s_yc"))
    s_ac = _ctx1.enter_context(nc.semaphore("s_ac"))
    s_st = _ctx1.enter_context(nc.semaphore("s_st"))
    s_flv = _ctx1.enter_context(nc.semaphore("s_flv"))
    s_fls = _ctx1.enter_context(nc.semaphore("s_fls"))
    _fv = [0]
    def vfl(v):
        _fv[0] += 1
        v.drain().then_inc(s_flv, 1)
        v.wait_ge(s_flv, _fv[0])
    _fs = [0]
    def sfl(s):
        _fs[0] += 1
        s.drain().then_inc(s_fls, 1)
        s.wait_ge(s_fls, _fs[0])
    block = _ctx1.enter_context(nc.Block())
    _ctx2 = contextlib.ExitStack()
    stage = _ctx2.enter_context(nc.sbuf_tensor("stage", [128, 4096], F32))
    gbc = _ctx2.enter_context(nc.sbuf_tensor("gbc", [128, 2], F32))
    rwT = _ctx2.enter_context(nc.sbuf_tensor("rwT", [128, 64], F32))
    maug = _ctx2.enter_context(nc.sbuf_tensor("maug", [128, 2 * 257], BF16))
    woeh = _ctx2.enter_context(nc.sbuf_tensor("woeh", [128, 2048], BF16))
    wvT = _ctx2.enter_context(nc.sbuf_tensor("wvT", [128, 2048], BF16))
    wkT = _ctx2.enter_context(nc.sbuf_tensor("wkT", [128, 2048], BF16))
    wqT = _ctx2.enter_context(nc.sbuf_tensor("wqT", [128, 2048], BF16))
    trimask = _ctx2.enter_context(nc.sbuf_tensor("trimask", [128, 4 * 512], BF16))
    cvA = _ctx2.enter_context(nc.sbuf_tensor("cvA", [128, 2048], BF16))
    cvB = _ctx2.enter_context(nc.sbuf_tensor("cvB", [128, 2048], BF16))
    xin = _ctx2.enter_context(nc.sbuf_tensor("xin", [128, 2 * 4096], F32))
    xln_tm = _ctx2.enter_context(nc.sbuf_tensor("xln_tm", [128, 4096], BF16))
    xlnT = _ctx2.enter_context(nc.sbuf_tensor("xlnT", [128, KT * 512], BF16))
    x2_t = _ctx2.enter_context(nc.sbuf_tensor("x2_t", [128, 4096], F32))
    x2ln_f = _ctx2.enter_context(nc.sbuf_tensor("x2ln_f", [128, 4096], F32))
    x2lnT_f = _ctx2.enter_context(nc.sbuf_tensor("x2lnT_f", [128, KT * 512], F32))
    sqT = _ctx2.enter_context(nc.sbuf_tensor("sqT", [128, 2 * 512], BF16))
    skT = _ctx2.enter_context(nc.sbuf_tensor("skT", [128, 2 * 512], BF16))
    sk_tm = _ctx2.enter_context(nc.sbuf_tensor("sk_tm", [128, 1024], BF16))
    v_aug = _ctx2.enter_context(nc.sbuf_tensor("v_aug", [128, 4 * 257], BF16))
    scm = _ctx2.enter_context(nc.sbuf_tensor("scm", [128, 4 * 512], BF16))
    eluR = _ctx2.enter_context(nc.sbuf_tensor("eluR", [128, 4 * 512], F32))
    x2sel = _ctx2.enter_context(nc.sbuf_tensor("x2sel", [128, KT * 512], BF16))
    nev = _ctx2.enter_context(nc.sbuf_tensor("nev", [128, 2 * 257], F32))
    mev = _ctx2.enter_context(nc.sbuf_tensor("mev", [128, 2 * 257], F32))
    tden = _ctx2.enter_context(nc.sbuf_tensor("tden", [128, 8], F32))
    comb = _ctx2.enter_context(nc.sbuf_tensor("comb", [128, 4 * 256], BF16))
    combT = _ctx2.enter_context(nc.sbuf_tensor("combT", [128, 1024], BF16))
    S_f = _ctx2.enter_context(nc.sbuf_tensor("S_f", [128, 2 * 257], F32))
    S_bf = _ctx2.enter_context(nc.sbuf_tensor("S_bf", [128, 2 * 257], BF16))
    Sup = _ctx2.enter_context(nc.sbuf_tensor("Sup", [128, 2 * 257], F32))
    lgT_sb = _ctx2.enter_context(nc.sbuf_tensor("lgT_sb", [8, 512], F32))
    lg_tm = _ctx2.enter_context(nc.sbuf_tensor("lg_tm", [128, 32], F32))
    stat_b = _ctx2.enter_context(nc.sbuf_tensor("stat_b", [128, 4], F32))
    stat_s = _ctx2.enter_context(nc.sbuf_tensor("stat_s", [128, 4], F32))
    t8a = _ctx2.enter_context(nc.sbuf_tensor("t8a", [128, 8], F32))
    sst = _ctx2.enter_context(nc.sbuf_tensor("sst", [128, 8], F32))
    vjunk = _ctx2.enter_context(nc.sbuf_tensor("vjunk", [128, 8], F32))
    t8b = _ctx2.enter_context(nc.sbuf_tensor("t8b", [128, 8], F32))
    t1a = _ctx2.enter_context(nc.sbuf_tensor("t1a", [128, 4], F32))
    t1b = _ctx2.enter_context(nc.sbuf_tensor("t1b", [128, 4], F32))
    TPP1 = _ctx2.enter_context(nc.psum_tensor("TPP1", [128, 1024], BF16))
    TPP2 = _ctx2.enter_context(nc.psum_tensor("TPP2", [128, 1024], BF16))
    PQ = _ctx2.enter_context(nc.psum_tensor("PQ", [128, 512], F32))
    PS = _ctx2.enter_context(nc.psum_tensor("PS", [128, 512], F32))
    PN = _ctx2.enter_context(nc.psum_tensor("PN", [128, 512], F32))
    PM = _ctx2.enter_context(nc.psum_tensor("PM", [128, 512], F32))
    PA = _ctx2.enter_context(nc.psum_tensor("PA", [128, 1024], F32))
    @block.sync
    def _(sync):
        sync.wait_ge(s_set, SET_TOTAL)
        for i in range(NCHT):
            b, c = divmod(i, NCH)
            half = i % 2
            if i >= 2:
                sync.wait_ge(s_vx, (i - 2) * VX_A + VX_X2LN)
            sync.dma_start(
                out=xin[:, half * 4096:(half + 1) * 4096]
                    .rearrange("p (n h) -> p n h", n=NT),
                in_=x_all[b, c * CHUNK:(c + 1) * CHUNK, :]
                    .rearrange("(n p) h -> p n h", p=128),
            ).then_inc(s_xld, 16)

    @block.gpsimd
    def _(g):
        pre = 0
        def gd(dst, src):
            nonlocal pre
            g.dma_start(out=dst, in_=src).then_inc(s_pre, 16)
            pre += 1
        gd(id_f32[:], identf[:])
        gd(id_bf[:], identb[:])
        gd(trimask[:], trimask_in[:])
        gd(iota8[:], iotac[:])
        gd(b2s[:], b2in[:, :])
        gd(stage[:, 0:1024], w_q[0:128, :])
        gd(stage[:, 1024:2048], w_q[128:256, :])
        gd(stage[:, 2048:3072], w_k[0:128, :])
        gd(stage[:, 3072:4096], w_k[128:256, :])
        g.wait_ge(s_set, 1)
        gd(stage[:, 0:1024], w_v[0:128, :])
        gd(stage[:, 1024:2048], w_v[128:256, :])
        for hb in range(KT):
            gd(stage[:, 2048 + hb * 256:2048 + (hb + 1) * 256],
               w_o[hb * 128:(hb + 1) * 128, :])
        g.wait_ge(s_set, 2)
        gd(stage[:, 0:256], memory[0:128, :])
        gd(stage[:, 256:512], memory[128:256, :])
        gd(stage[0:1, 512:768], memnorm[:, :])
        gd(stage[:, 768:769], memnorm[0, 0:128].rearrange("(p a) -> p a", a=1))
        gd(stage[:, 769:770], memnorm[0, 128:256].rearrange("(p a) -> p a", a=1))
        gd(stage[0:1, 770:771], gate[:, :])
        gd(stage[0:1, 772:780], onehot[:, :])
        gd(stage[0:8, 1024:2048], router[:, :])
        gd(stage[0:8, 2048:4096], b1in[:, 0:2048])
        g.wait_ge(s_tp, 10)
        gd(stage[0:8, 2048:4096], b1in[:, 2048:4096])
        g.wait_ge(s_set, 5)
        cvn = 0
        if DEV_SKIP_CONV:
            g.nop().then_inc(s_cv, CV_TOTAL)
        for e in range(E if not DEV_SKIP_CONV else 0):
            for hb in range(KT):
                for fh in range(2):
                    buf = cvA if cvn % 4 < 2 else cvB
                    g.dma_start(out=buf[:],
                                in_=W1[e, hb * 128:(hb + 1) * 128,
                                       fh * 2048:(fh + 1) * 2048]).then_inc(s_cv, 16)
                    cvn += 1
                    g.wait_ge(s_cv, cvn * 16)
                    g.dma_start(
                        out=W1bf[e, fh * 16:(fh + 1) * 16, hb, :, :]
                            .rearrange("ft r c -> r ft c"),
                        in_=buf[:].rearrange("r (ft c) -> r ft c", c=128),
                    ).then_inc(s_cv, 16)
                    cvn += 1
        for e in range(E if not DEV_SKIP_CONV else 0):
            for qb in range(16):
                buf = cvA if qb % 2 == 0 else cvB
                g.dma_start(
                    out=buf[:].rearrange("r (q m) -> r q m", q=2),
                    in_=W2[e, qb * 256:(qb + 1) * 256, :]
                        .rearrange("(q r) m -> r q m", r=128),
                ).then_inc(s_cv, 16)
                cvn += 1
                g.wait_ge(s_cv, cvn * 16)
                g.dma_start(
                    out=W2bf[e, qb * 2:(qb + 1) * 2, :, :].rearrange("q r m -> r q m"),
                    in_=buf[:].rearrange("r (q m) -> r q m", q=2),
                ).then_inc(s_cv, 16)
                cvn += 1

    @block.gpsimd
    def _(g):
        for i in range(NCHT):
            b, c = divmod(i, NCH)
            g.wait_ge(s_vx, i * VX_A + VX_OWN)
            g.nop(cycle_cnt=SDEL)
            g.dma_start(
                out=x2_scr[c * CHUNK:(c + 1) * CHUNK, :]
                    .rearrange("(n p) h -> p n h", p=128),
                in_=x2_t[:].rearrange("p (n h) -> p n h", n=NT),
                accum_op=(ALU.bypass if b == 0 else ALU.add),
            ).then_inc(s_sc, 16)
            g.dma_start(
                out=x2ln_dram[:, :, c * CHUNK:(c + 1) * CHUNK]
                    .rearrange("kt p t -> p kt t"),
                in_=x2sel[:].rearrange("p (kt t) -> p kt t", kt=KT),
                accum_op=(ALU.bypass if b == 0 else ALU.add),
            ).then_inc(s_sc, 16)

    if DEV_EXPOSE:
        _dumps = {}
        for _nm, _w in [
            ("d_xln_tm", 4096), ("d_xlnT", 4096), ("d_sqT", 1024), ("d_skT", 1024),
            ("d_sk_tm", 1024), ("d_v_aug", 1028), ("d_scm", 2048), ("d_comb", 1024),
            ("d_combT", 1024), ("d_x2_t", 4096), ("d_x2ln_f", 4096),
            ("d_x2lnT_f", 4096), ("d_eluR", 2048), ("d_Sf", 514),
            ("d_lgtm", 32), ("d_stats", 16), ("d_t8a", 16), ("d_nev", 514),
            ("d_mev", 514), ("d_gbc", 4), ("d_stg", 64),
        ]:
            _dumps[_nm] = nc.dram_tensor(_nm, [128, _w], F32, kind="ExternalOutput").ap()

        @block.gpsimd
        def _(g):
            g.wait_ge(s_vx, NCHT * VX_A)
            g.wait_ge(s_ev, SETUP_EV + NCHT * EV_A)
            g.nop(cycle_cnt=4096)
            _dc = [0]
            def dmp(nm, srcap, isbf, w):
                dst = _dumps[nm]
                g.dma_start(out=dst[:, 0:w], in_=srcap).then_inc(s_st, 16)
                _dc[0] += 1
            dmp("d_xln_tm", xln_tm[:], True, 4096)
            dmp("d_xlnT", xlnT[:], True, 4096)
            dmp("d_sqT", sqT[:], True, 1024)
            dmp("d_skT", skT[:], True, 1024)
            dmp("d_sk_tm", sk_tm[:], True, 1024)
            dmp("d_v_aug", v_aug[:], True, 1028)
            dmp("d_scm", scm[:], True, 2048)
            dmp("d_comb", comb[:], True, 1024)
            dmp("d_combT", combT[:], True, 1024)
            dmp("d_x2_t", x2_t[:], False, 4096)
            dmp("d_x2ln_f", x2ln_f[:], False, 4096)
            dmp("d_x2lnT_f", x2lnT_f[:], False, 4096)
            dmp("d_eluR", eluR[:], False, 2048)
            dmp("d_Sf", S_f[:], False, 514)
            dmp("d_lgtm", lg_tm[:], False, 32)
            dmp("d_stats", stat_b[:], False, 4)
            dmp("d_nev", nev[:], False, 514)
            dmp("d_mev", mev[:], False, 514)
            g.dma_start(out=_dumps["d_t8a"][:, 0:8], in_=t8a[:]).then_inc(s_st, 16)
            g.dma_start(out=_dumps["d_gbc"][:, 0:2], in_=gbc[:]).then_inc(s_st, 16)
            g.dma_start(out=_dumps["d_stg"][0:1, 0:64],
                        in_=stage[0:1, 744:808]).then_inc(s_st, 16)
            g.dma_start(out=_dumps["d_stats"][:, 4:8], in_=stat_s[:]).then_inc(s_st, 16)
            g.dma_start(out=_dumps["d_stats"][:, 8:12], in_=t1a[:]).then_inc(s_st, 16)
            g.dma_start(out=_dumps["d_stats"][:, 12:16], in_=t1b[:]).then_inc(s_st, 16)
            g.wait_ge(s_st, (_dc[0] + 6) * 16)

    @block.scalar
    def _(s):
        tp = [0]
        def sevac(dst_ap, src_ap, delay=CDEL):
            tp[0] += 1
            s.wait_ge(s_tp, tp[0])
            s.nop(cycle_cnt=delay)
            s.copy(out=dst_ap, in_=src_ap)
            s.drain().then_inc(s_ev, 1)
        s.wait_ge(s_pre, 9 * 16)
        s.copy(out=cvA[:, 0:1024], in_=stage[:, 0:1024])
        s.copy(out=cvB[:, 0:1024], in_=stage[:, 1024:2048])
        s.copy(out=cvA[:, 1024:2048], in_=stage[:, 2048:3072])
        s.copy(out=cvB[:, 1024:2048], in_=stage[:, 3072:4096])
        s.drain().then_inc(s_set, 1)
        sevac(wqT[:, 0:1024], TPP1[:, 0:1024])
        sevac(wqT[:, 1024:2048], TPP2[:, 0:1024])
        sevac(wkT[:, 0:1024], TPP1[:, 0:1024])
        sevac(wkT[:, 1024:2048], TPP2[:, 0:1024])
        s.wait_ge(s_pre, 19 * 16)
        s.copy(out=cvA[:, 0:1024], in_=stage[:, 0:1024])
        s.copy(out=cvB[:, 0:1024], in_=stage[:, 1024:2048])
        for hb in range(KT):
            s.copy(out=cvA[:, 1024 + hb * 128:1024 + (hb + 1) * 128],
                   in_=stage[:, 2048 + hb * 256:2048 + hb * 256 + 128])
            s.copy(out=cvB[:, 1024 + hb * 128:1024 + (hb + 1) * 128],
                   in_=stage[:, 2048 + hb * 256 + 128:2048 + (hb + 1) * 256])
        s.drain().then_inc(s_set, 1)
        sevac(wvT[:, 0:1024], TPP1[:, 0:1024])
        sevac(wvT[:, 1024:2048], TPP2[:, 0:1024])
        sevac(woeh[:, 0:1024], TPP1[:, 0:1024])
        sevac(woeh[:, 1024:2048], TPP2[:, 0:1024])
        s.wait_ge(s_pre, 28 * 16)
        s.copy(out=maug[:, 0:256], in_=stage[:, 0:256])
        s.copy(out=maug[:, 257:513], in_=stage[:, 256:512])
        s.copy(out=maug[:, 256:257], in_=stage[:, 768:769])
        s.copy(out=maug[:, 513:514], in_=stage[:, 769:770])
        s.activation(out=stage[0:1, 771:772], in_=stage[0:1, 770:771],
                     func=AF.Sigmoid)
        s.drain().then_inc(s_set, 1)
        sevac(rwT[:].rearrange("p (kt e) -> p kt e", e=8),
              PQ[:, 0:128].rearrange("p (kt e) -> p kt e", e=16)[:, :, 0:8])
        tp[0] += 1
        s.wait_ge(s_tp, tp[0])
        s.drain().then_inc(s_ev, 1)
        sevac(b1s[:].rearrange("p (e ft) -> p ft e", ft=32),
              PS[:, 0:512].rearrange("p (ft e) -> p ft e", e=16)[:, :, 0:8])
        tp[0] += 1
        s.wait_ge(s_tp, tp[0])
        s.nop(cycle_cnt=CDEL)
        s.copy(out=ohbc[:], in_=PN[:, 0:8])
        s.copy(out=gbc[:], in_=PN[:, 16:18])
        s.drain().then_inc(s_ev, 1)
        assert tp[0] == SETUP_TP
        s.wait_ge(s_set, 5)
        s.drain().then_inc(s_set, 1)
        for i in range(NCHT):
            b, c = divmod(i, NCH)
            half = i % 2
            xh = xin[:, half * 4096:(half + 1) * 4096]
            TPB = SETUP_TP + i * TP_A
            EVB = SETUP_EV + i * EV_A
            VXB = i * VX_A
            LNB = i * LN_A
            s.wait_ge(s_vx, VXB + VX_LN1)
            s.activation(out=sst[:, 0:4], in_=stat_s[:], func=AF.Ln,
                         bias=epsc[:, 0:1])
            sfl(s)
            s.activation(out=stat_s[:], in_=sst[:, 0:4], func=AF.Exp, scale=-0.5)
            s.drain().then_inc(s_ev, 1)
            s.wait_ge(s_vx, VXB + VX_LNB1)
            for tt in range(NT):
                s.activation(out=xln_tm[:, tt * 1024:(tt + 1) * 1024],
                             in_=xh[:, tt * 1024:(tt + 1) * 1024],
                             func=AF.Identity,
                             bias=stat_b[:, tt:tt + 1], scale=stat_s[:, tt:tt + 1])
            s.drain().then_inc(s_ln, 1)
            for gq in range(4):
                TPx = TPP1 if gq % 2 == 0 else TPP2
                s.wait_ge(s_tp, TPB + TP_XLN0 + gq)
                s.nop(cycle_cnt=SDEL)
                s.copy(out=xlnT[:, gq * 1024:(gq + 1) * 1024], in_=TPx[:, 0:1024])
                s.drain().then_inc(s_ev, 1)
            s.wait_ge(s_tp, TPB + TP_QK)
            s.nop(cycle_cnt=CDEL)
            for j, P in enumerate((PQ, PS, PN, PM)):
                s.activation(out=eluR[:, j * 512:(j + 1) * 512], in_=P[:],
                             func=AF.Relu)
                s.activation(out=x2ln_f[:, j * 512:(j + 1) * 512], in_=P[:],
                             func=AF.Relu, scale=-1.0)
            for j in range(4):
                dst = sqT if j < 2 else skT
                s.activation(out=dst[:, (j % 2) * 512:(j % 2 + 1) * 512],
                             in_=x2ln_f[:, j * 512:(j + 1) * 512],
                             func=AF.Exp, scale=-1.0)
            s.nop(cycle_cnt=SDEL)
            s.drain().then_inc(s_ev, 1)
            s.wait_ge(s_tp, TPB + TP_V)
            s.nop(cycle_cnt=CDEL)
            for j, P in enumerate((PQ, PS, PN, PM)):
                s.copy(out=v_aug[:, j * 257:j * 257 + 256], in_=P[:, 0:256])
            s.drain().then_inc(s_ev, 1)
            s.wait_ge(s_tp, TPB + TP_SC)
            s.nop(cycle_cnt=CDEL)
            for j, P in enumerate((PQ, PS, PN, PM)):
                s.copy(out=scm[:, j * 512:(j + 1) * 512], in_=P[:])
            s.drain().then_inc(s_ev, 1)
            s.wait_ge(s_tp, TPB + TP_SKT)
            s.nop(cycle_cnt=SDEL)
            s.copy(out=sk_tm[:], in_=TPP1[:, 0:1024])
            s.drain().then_inc(s_ev, 1)
            for it in range(NT):
                s.wait_ge(s_tp, TPB + TP_NM0 + it)
                if it >= 2:
                    s.wait_ge(s_vx, VXB + VX_CMB0 + it - 2)
                s.nop(cycle_cnt=CDEL)
                hN = it % 2
                s.copy(out=nev[:, hN * 257:hN * 257 + 257],
                       in_=(PQ if it % 2 == 0 else PS)[:, 0:257])
                s.copy(out=mev[:, hN * 257:hN * 257 + 257],
                       in_=(PN if it % 2 == 0 else PM)[:, 0:257])
                s.drain().then_inc(s_ev, 1)
            s.wait_ge(s_tp, TPB + TP_CMBT)
            s.nop(cycle_cnt=SDEL)
            s.copy(out=combT[:], in_=TPP2[:, 0:1024])
            s.drain().then_inc(s_ev, 1)
            s.wait_ge(s_tp, TPB + TP_SUP)
            s.nop(cycle_cnt=CDEL)
            s.copy(out=Sup[:, 0:257], in_=PQ[:, 0:257])
            s.copy(out=Sup[:, 257:514], in_=PS[:, 0:257])
            s.drain().then_inc(s_ev, 1)
            for it in range(NT):
                s.wait_ge(s_tp, TPB + TP_AO0 + it)
                if i > 0 and it == 0:
                    s.wait_ge(s_vx, (i - 1) * VX_A + VX_OWN)
                    s.wait_ge(s_sc, i * 32)
                s.nop(cycle_cnt=CDEL)
                s.copy(out=x2_t[:, it * 1024:(it + 1) * 1024], in_=PA[:])
                s.drain().then_inc(s_ev, 1)
            s.wait_ge(s_vx, VXB + VX_X2LN)
            s.activation(out=sst[:, 0:4], in_=stat_s[:], func=AF.Ln,
                         bias=epsc[:, 0:1])
            sfl(s)
            s.activation(out=stat_s[:], in_=sst[:, 0:4], func=AF.Exp, scale=-0.5)
            s.drain().then_inc(s_ev, 1)
            s.wait_ge(s_vx, VXB + VX_LNB2)
            for tt in range(NT):
                s.activation(out=x2ln_f[:, tt * 1024:(tt + 1) * 1024],
                             in_=x2_t[:, tt * 1024:(tt + 1) * 1024],
                             func=AF.Identity,
                             bias=stat_b[:, tt:tt + 1], scale=stat_s[:, tt:tt + 1])
            s.drain().then_inc(s_ln, 1)
            for gq in range(8):
                s.wait_ge(s_tp, TPB + TP_X2T0 + gq)
                s.nop(cycle_cnt=SDEL)
                s.copy(out=x2lnT_f[:, gq * 512:(gq + 1) * 512],
                       in_=(PQ if gq % 2 == 0 else PS)[:])
                s.drain().then_inc(s_ev, 1)
            s.wait_ge(s_tp, TPB + TP_LG)
            s.nop(cycle_cnt=CDEL)
            s.copy(out=lgT_sb[:], in_=PN[0:8, 0:512])
            s.drain().then_inc(s_ev, 1)
            s.wait_ge(s_tp, TPB + TP_LGT)
            s.nop(cycle_cnt=CDEL)
            s.copy(out=lg_tm[:].rearrange("p (it e) -> p it e", e=8),
                   in_=PM[:, 0:64].rearrange("p (it e) -> p it e", e=16)[:, :, 0:8])
            s.drain().then_inc(s_ev, 1)
            s.wait_ge(s_vx, VXB + VX_TOP1)
            s.nop(cycle_cnt=SDEL)
            for it in range(NT):
                s.activation(out=t8a[:], in_=lg_tm[:, it * 8:(it + 1) * 8],
                             func=AF.Exp, bias=t1b[:, it:it + 1],
                             accum_out=tden[:, it:it + 1])
            s.drain().then_inc(s_ev, 1)

    @block.vector
    def _(v):
        v.wait_ge(s_set, 3)
        v.nop(cycle_cnt=SDEL)
        v.memset(vjunk[:], 0.0)
        v.memset(epsc[:], EPS_LN)
        v.reduce_sum(stage[0:1, 781:782], stage[0:1, 512:768], axis=AX.X)
        v.memset(ones_row[:], 1.0)
        vfl(v)
        v.tensor_scalar(stage[0:1, 781:782], stage[0:1, 781:782], EPS_ATT, None,
                        op0=ALU.is_ge)
        v.memset(ones_col_bf[:], 1.0)
        vfl(v)
        v.tensor_tensor(stage[0:1, 782:783], stage[0:1, 771:772],
                        stage[0:1, 781:782], ALU.mult)
        v.tensor_scalar(stage[0:1, 783:784], stage[0:1, 771:772], -1.0, 1.0,
                        op0=ALU.mult, op1=ALU.add)
        v.memset(S_f[:], 0.0)
        v.memset(S_bf[:], 0.0)
        v.memset(P_all[:], 0.0)
        v.memset(E_all[:], 0.0)
        for j in range(4):
            v.memset(v_aug[:, j * 257 + 256:j * 257 + 257], 1.0)
        v.drain().then_inc(s_set, 1)
        for i in range(NCHT):
            b, c = divmod(i, NCH)
            half = i % 2
            xh = xin[:, half * 4096:(half + 1) * 4096]
            EVB = SETUP_EV + i * EV_A
            VXB = i * VX_A
            col0 = b * 32 + c * 4
            v.wait_ge(s_xld, (i + 1) * 16)
            v.nop(cycle_cnt=SDEL)
            for tt in range(NT):
                xt_ = xh[:, tt * 1024:(tt + 1) * 1024]
                v.reduce_sum(t1a[:, tt:tt + 1], xt_, axis=AX.X)
                v.tensor_tensor(x2ln_f[:, tt * 1024:(tt + 1) * 1024], xt_, xt_,
                                ALU.mult)
                v.reduce_sum(t1b[:, tt:tt + 1],
                             x2ln_f[:, tt * 1024:(tt + 1) * 1024], axis=AX.X)
            vfl(v)
            v.tensor_scalar(t1a[:], t1a[:], 1.0 / H, None, op0=ALU.mult)
            v.tensor_scalar(t1b[:], t1b[:], 1.0 / H, None, op0=ALU.mult)
            vfl(v)
            v.tensor_tensor(t8a[:, 0:4], t1a[:], t1a[:], ALU.mult)
            vfl(v)
            v.scalar_tensor_tensor(out=stat_s[:], in0=t8a[:, 0:4], scalar=-1.0,
                                   in1=t1b[:], op0=ALU.mult, op1=ALU.add)
            v.drain().then_inc(s_vx, 1)
            v.wait_ge(s_ev, EVB + EV_RSTD1)
            v.scalar_tensor_tensor(out=stat_b[:], in0=t1a[:], scalar=-1.0,
                                   in1=stat_s[:], op0=ALU.mult, op1=ALU.mult)
            v.drain().then_inc(s_vx, 1)
            v.wait_ge(s_ev, EVB + EV_QK)
            v.nop(cycle_cnt=SDEL)
            for j in range(2):
                v.tensor_tensor(sqT[:, j * 512:(j + 1) * 512],
                                sqT[:, j * 512:(j + 1) * 512],
                                eluR[:, j * 512:(j + 1) * 512], ALU.add)
            for j in range(2):
                v.tensor_tensor(skT[:, j * 512:(j + 1) * 512],
                                skT[:, j * 512:(j + 1) * 512],
                                eluR[:, (2 + j) * 512:(3 + j) * 512], ALU.add)
            v.drain().then_inc(s_vx, 1)
            v.wait_ge(s_ev, EVB + EV_SCM)
            v.nop(cycle_cnt=SDEL)
            for j in range(4):
                v.tensor_tensor(scm[:, j * 512:(j + 1) * 512],
                                scm[:, j * 512:(j + 1) * 512],
                                trimask[:, j * 512:(j + 1) * 512], ALU.mult)
            v.drain().then_inc(s_vx, 1)
            for it in range(NT):
                v.wait_ge(s_ev, EVB + EV_NM0 + it)
                v.nop(cycle_cnt=SDEL)
                hN = it % 2
                nv = nev[:, hN * 257:hN * 257 + 257]
                mv = mev[:, hN * 257:hN * 257 + 257]
                v.tensor_scalar(t8a[:, 0:1], nv[:, 256:257], EPS_ATT, None,
                                op0=ALU.max)
                v.tensor_scalar(t8a[:, 2:3], mv[:, 256:257], EPS_ATT, None,
                                op0=ALU.max)
                vfl(v)
                v.reciprocal(t8a[:, 1:2], t8a[:, 0:1])
                v.reciprocal(t8a[:, 3:4], t8a[:, 2:3])
                vfl(v)
                v.tensor_tensor(t8a[:, 5:6], t8a[:, 3:4], gbc[:, 0:1], ALU.mult)
                v.tensor_tensor(t8a[:, 4:5], t8a[:, 1:2], gbc[:, 1:2], ALU.mult)
                vfl(v)
                v.tensor_scalar(x2ln_f[:, 0:256], mv[:, 0:256], t8a[:, 5:6],
                                None, op0=ALU.mult)
                v.scalar_tensor_tensor(out=comb[:, it * 256:(it + 1) * 256],
                                       in0=nv[:, 0:256], scalar=t8a[:, 4:5],
                                       in1=x2ln_f[:, 0:256],
                                       op0=ALU.mult, op1=ALU.add)
                v.drain().then_inc(s_vx, 1)
            v.wait_ge(s_ev, EVB + EV_SUP)
            v.nop(cycle_cnt=SDEL)
            if c == NCH - 1:
                v.memset(S_f[:], 0.0)
                v.memset(S_bf[:], 0.0)
            else:
                v.tensor_tensor(S_f[:], S_f[:], Sup[:], ALU.add)
                v.tensor_copy(out=S_bf[:], in_=S_f[:])
            v.drain().then_inc(s_vx, 1)
            v.wait_ge(s_ev, EVB + EV_AO0 + 3)
            v.nop(cycle_cnt=SDEL)
            for tt in range(NT):
                v.tensor_tensor(x2_t[:, tt * 1024:(tt + 1) * 1024],
                                x2_t[:, tt * 1024:(tt + 1) * 1024],
                                xh[:, tt * 1024:(tt + 1) * 1024], ALU.add)
            for tt in range(NT):
                xt_ = x2_t[:, tt * 1024:(tt + 1) * 1024]
                v.reduce_sum(t1a[:, tt:tt + 1], xt_, axis=AX.X)
                v.tensor_tensor(x2ln_f[:, tt * 1024:(tt + 1) * 1024], xt_, xt_,
                                ALU.mult)
                v.reduce_sum(t1b[:, tt:tt + 1],
                             x2ln_f[:, tt * 1024:(tt + 1) * 1024], axis=AX.X)
            vfl(v)
            v.tensor_scalar(t1a[:], t1a[:], 1.0 / H, None, op0=ALU.mult)
            v.tensor_scalar(t1b[:], t1b[:], 1.0 / H, None, op0=ALU.mult)
            vfl(v)
            v.tensor_tensor(t8a[:, 0:4], t1a[:], t1a[:], ALU.mult)
            vfl(v)
            v.scalar_tensor_tensor(out=stat_s[:], in0=t8a[:, 0:4], scalar=-1.0,
                                   in1=t1b[:], op0=ALU.mult, op1=ALU.add)
            v.drain().then_inc(s_vx, 1)
            v.wait_ge(s_ev, EVB + EV_RSTD2)
            v.scalar_tensor_tensor(out=stat_b[:], in0=t1a[:], scalar=-1.0,
                                   in1=stat_s[:], op0=ALU.mult, op1=ALU.mult)
            v.drain().then_inc(s_vx, 1)
            v.wait_ge(s_ev, EVB + EV_X2T0 + 7)
            if i > 0:
                v.wait_ge(s_sc, i * 32)
            v.nop(cycle_cnt=SDEL)
            for kt in range(KT):
                v.tensor_scalar(x2sel[:, kt * 512:(kt + 1) * 512],
                                x2lnT_f[:, kt * 512:(kt + 1) * 512],
                                ohbc[:, b:b + 1], None, op0=ALU.mult)
            for tt in range(NT):
                v.tensor_scalar(x2_t[:, tt * 1024:(tt + 1) * 1024],
                                x2_t[:, tt * 1024:(tt + 1) * 1024],
                                ohbc[:, b:b + 1], None, op0=ALU.mult)
            v.drain().then_inc(s_vx, 1)
            v.wait_ge(s_ev, EVB + EV_LGTM)
            v.nop(cycle_cnt=SDEL)
            for it in range(NT):
                v.reduce_max(t1a[:, it:it + 1], lg_tm[:, it * 8:(it + 1) * 8],
                             axis=AX.X)
            vfl(v)
            for it in range(NT):
                v.tensor_scalar(x2ln_f[:, it * 8:(it + 1) * 8],
                                lg_tm[:, it * 8:(it + 1) * 8],
                                t1a[:, it:it + 1], None, op0=ALU.is_ge)
            vfl(v)
            for it in range(NT):
                v.scalar_tensor_tensor(out=x2ln_f[:, 64 + it * 8:64 + (it + 1) * 8],
                                       in0=x2ln_f[:, it * 8:(it + 1) * 8],
                                       scalar=-1000.0, in1=iota8[:, 8:16],
                                       op0=ALU.mult, op1=ALU.add)
            vfl(v)
            for it in range(NT):
                v.tensor_reduce(E_all[:, col0 + it:col0 + it + 1],
                                x2ln_f[:, 64 + it * 8:64 + (it + 1) * 8],
                                axis=AX.X, op=ALU.min)
            v.tensor_scalar(t1b[:], t1a[:], -1.0, None, op0=ALU.mult)
            v.drain().then_inc(s_vx, 1)
            v.wait_ge(s_ev, EVB + EV_TEXP)
            v.nop(cycle_cnt=SDEL)
            v.reciprocal(P_all[:, col0:col0 + 4], tden[:, 0:4])
            v.drain().then_inc(s_vx, 1)

    @block.tensor
    def _(t):
        t.wait_ge(s_set, 1)
        gidx = 0
        for srcoff, TPx in ((0, TPP1), (0, TPP2), (1024, TPP1), (1024, TPP2)):
            srcbuf = cvA if TPx is TPP1 else cvB
            if gidx >= 2:
                t.wait_ge(s_ev, gidx - 1)
            for kt in range(KT):
                t.transpose(TPx[:, kt * 128:(kt + 1) * 128],
                            srcbuf[:, srcoff + kt * 128:srcoff + (kt + 1) * 128],
                            id_bf[:])
            t.drain().then_inc(s_tp, 1)
            gidx += 1
        t.wait_ge(s_set, 2)
        for srcoff, TPx in ((0, TPP1), (0, TPP2), (1024, TPP1), (1024, TPP2)):
            srcbuf = cvA if TPx is TPP1 else cvB
            t.wait_ge(s_ev, gidx - 1)
            for kt in range(KT):
                t.transpose(TPx[:, kt * 128:(kt + 1) * 128],
                            srcbuf[:, srcoff + kt * 128:srcoff + (kt + 1) * 128],
                            id_bf[:])
            t.drain().then_inc(s_tp, 1)
            gidx += 1
        t.wait_ge(s_set, 3)
        t.wait_ge(s_pre, 28 * 16)
        for kt in range(KT):
            t.transpose(PQ[:, kt * 16:kt * 16 + 8],
                        stage[0:8, 1024 + kt * 128:1024 + (kt + 1) * 128],
                        id_f32[0:8, 0:8])
        t.drain().then_inc(s_tp, 1)
        gidx += 1
        for ft in range(16):
            t.transpose(PS[:, ft * 16:ft * 16 + 8],
                        stage[0:8, 2048 + ft * 128:2048 + (ft + 1) * 128],
                        id_f32[0:8, 0:8])
        t.drain().then_inc(s_tp, 1)
        gidx += 1
        t.wait_ge(s_pre, 29 * 16)
        for ft in range(16):
            t.transpose(PS[:, (16 + ft) * 16:(16 + ft) * 16 + 8],
                        stage[0:8, 2048 + ft * 128:2048 + (ft + 1) * 128],
                        id_f32[0:8, 0:8])
        t.drain().then_inc(s_tp, 1)
        gidx += 1
        t.wait_ge(s_set, 4)
        t.matmul(PN[:, 0:8], ones_row[0:1, :], stage[0:1, 772:780],
                 start=True, stop=True)
        t.matmul(PN[:, 16:18], ones_row[0:1, :], stage[0:1, 782:784],
                 start=True, stop=True)
        t.drain().then_inc(s_tp, 1)
        gidx += 1
        assert gidx == SETUP_TP
        t.drain().then_inc(s_set, 1)
        t.wait_ge(s_set, SET_TOTAL)
        for _ in range(64):
            t.matmul(PA[:, 0:128], id_bf[:], id_bf[:], start=True, stop=True)
        for i in range(NCHT):
            b, c = divmod(i, NCH)
            half = i % 2
            TPB = SETUP_TP + i * TP_A
            EVB = SETUP_EV + i * EV_A
            EVBp = SETUP_EV + (i - 1) * EV_A
            VXB = i * VX_A
            LNB = i * LN_A
            t.wait_ge(s_ln, LNB + 1)
            if i > 0:
                t.wait_ge(s_ev, EVBp + EV_CMBT)
            for gq in range(4):
                TPx = TPP1 if gq % 2 == 0 else TPP2
                if gq >= 2:
                    t.wait_ge(s_ev, EVB + EV_XLNT0 + gq - 2)
                for ktp in range(2):
                    kt = gq * 2 + ktp
                    for tt in range(NT):
                        t.transpose(TPx[:, ktp * 512 + tt * 128:ktp * 512 + (tt + 1) * 128],
                                    xln_tm[:, tt * 1024 + kt * 128:tt * 1024 + (kt + 1) * 128],
                                    id_bf[:])
                t.drain().then_inc(s_tp, 1)
            t.wait_ge(s_ev, EVB + EV_XLNT0 + 3)
            if i > 0:
                t.wait_ge(s_ev, EVBp + EV_LGTM)
            for dh, P in ((0, PQ), (1, PS)):
                for kt in range(KT):
                    t.matmul(P[:], wqT[:, dh * 1024 + kt * 128:dh * 1024 + (kt + 1) * 128],
                             xlnT[:, kt * 512:(kt + 1) * 512],
                             start=(kt == 0), stop=(kt == KT - 1))
                    t.nop(cycle_cnt=PACE2)
            for dh, P in ((0, PN), (1, PM)):
                for kt in range(KT):
                    t.matmul(P[:], wkT[:, dh * 1024 + kt * 128:dh * 1024 + (kt + 1) * 128],
                             xlnT[:, kt * 512:(kt + 1) * 512],
                             start=(kt == 0), stop=(kt == KT - 1))
                    t.nop(cycle_cnt=PACE2)
            t.drain().then_inc(s_tp, 1)
            t.wait_ge(s_ev, EVB + EV_QK)
            for jt, P in enumerate((PQ, PS, PN, PM)):
                for kt in range(KT):
                    t.matmul(P[:, 0:256],
                             xlnT[:, kt * 512 + jt * 128:kt * 512 + (jt + 1) * 128],
                             wvT[:].rearrange("p (dh kt2 c) -> p kt2 dh c",
                                              dh=2, c=128)[:, kt, :, :],
                             start=(kt == 0), stop=(kt == KT - 1))
                    t.nop(cycle_cnt=PACE1)
            t.drain().then_inc(s_tp, 1)
            t.wait_ge(s_vx, VXB + VX_SIG)
            t.wait_ge(s_ev, EVB + EV_V)
            t.nop(cycle_cnt=SDEL)
            for jt, P in enumerate((PQ, PS, PN, PM)):
                for dh in range(2):
                    t.matmul(P[:],
                             skT[:, dh * 512 + jt * 128:dh * 512 + (jt + 1) * 128],
                             sqT[:, dh * 512:(dh + 1) * 512],
                             start=(dh == 0), stop=(dh == 1))
                    t.nop(cycle_cnt=PACE2)
            t.drain().then_inc(s_tp, 1)
            for jt in range(NT):
                for dh in range(2):
                    t.transpose(TPP1[:, jt * 256 + dh * 128:jt * 256 + (dh + 1) * 128],
                                skT[:, dh * 512 + jt * 128:dh * 512 + (jt + 1) * 128],
                                id_bf[:])
            t.drain().then_inc(s_tp, 1)
            t.wait_ge(s_vx, VXB + VX_MSK)
            if i > 0:
                t.wait_ge(s_vx, (i - 1) * VX_A + VX_SREF)
            t.wait_ge(s_ev, EVB + EV_SCM)
            for it in range(NT):
                if it >= 2:
                    t.wait_ge(s_ev, EVB + EV_NM0 + it - 2)
                Pn = PQ if it % 2 == 0 else PS
                Pm = PN if it % 2 == 0 else PM
                for jt in range(NT):
                    t.matmul(Pn[:, 0:257],
                             scm[:, jt * 512 + it * 128:jt * 512 + (it + 1) * 128],
                             v_aug[:, jt * 257:(jt + 1) * 257],
                             start=(jt == 0), stop=False)
                    t.nop(cycle_cnt=PACE1)
                for dh in range(2):
                    t.matmul(Pn[:, 0:257],
                             sqT[:, dh * 512 + it * 128:dh * 512 + (it + 1) * 128],
                             S_bf[:, dh * 257:(dh + 1) * 257],
                             start=False, stop=(dh == 1))
                    t.nop(cycle_cnt=PACE1)
                for dh in range(2):
                    t.matmul(Pm[:, 0:257],
                             sqT[:, dh * 512 + it * 128:dh * 512 + (it + 1) * 128],
                             maug[:, dh * 257:(dh + 1) * 257],
                             start=(dh == 0), stop=(dh == 1))
                    t.nop(cycle_cnt=PACE1)
                t.drain().then_inc(s_tp, 1)
            t.wait_ge(s_vx, VXB + VX_CMB0 + 3)
            for it in range(NT):
                for eh in range(2):
                    t.transpose(TPP2[:, it * 256 + eh * 128:it * 256 + (eh + 1) * 128],
                                comb[:, it * 256 + eh * 128:it * 256 + (eh + 1) * 128],
                                id_bf[:])
            t.drain().then_inc(s_tp, 1)
            t.wait_ge(s_ev, EVB + EV_SKTM)
            t.wait_ge(s_ev, EVB + EV_NM0 + 3)
            for dh, P in ((0, PQ), (1, PS)):
                for jt in range(NT):
                    t.matmul(P[:, 0:257],
                             sk_tm[:, jt * 256 + dh * 128:jt * 256 + (dh + 1) * 128],
                             v_aug[:, jt * 257:(jt + 1) * 257],
                             start=(jt == 0), stop=(jt == NT - 1))
                    t.nop(cycle_cnt=PACE1)
            t.drain().then_inc(s_tp, 1)
            t.wait_ge(s_ev, EVB + EV_CMBT)
            for it in range(NT):
                if it >= 1:
                    t.wait_ge(s_ev, EVB + EV_AO0 + it - 1)
                for hhalf in range(2):
                    for eh in range(2):
                        t.matmul(PA[:, hhalf * 512:(hhalf + 1) * 512],
                                 combT[:, it * 256 + eh * 128:it * 256 + (eh + 1) * 128],
                                 woeh[:, eh * 1024 + hhalf * 512:eh * 1024 + (hhalf + 1) * 512],
                                 start=(eh == 0), stop=(eh == 1))
                        t.nop(cycle_cnt=PACE2)
                t.drain().then_inc(s_tp, 1)
            t.wait_ge(s_ln, LNB + 2)
            t.wait_ge(s_ev, EVB + EV_SUP)
            for gq in range(KT):
                P = PQ if gq % 2 == 0 else PS
                if gq >= 2:
                    t.wait_ge(s_ev, EVB + EV_X2T0 + gq - 2)
                for tt in range(NT):
                    t.transpose(P[:, tt * 128:(tt + 1) * 128],
                                x2ln_f[:, tt * 1024 + gq * 128:tt * 1024 + (gq + 1) * 128],
                                id_f32[:])
                t.drain().then_inc(s_tp, 1)
            t.wait_ge(s_ev, EVB + EV_X2T0 + 7)
            for kt in range(KT):
                t.matmul(PN[0:8, 0:512], rwT[:, kt * 8:(kt + 1) * 8],
                         x2lnT_f[:, kt * 512:(kt + 1) * 512],
                         start=(kt == 0), stop=(kt == KT - 1))
                t.nop(cycle_cnt=PACEF)
            t.drain().then_inc(s_tp, 1)
            t.wait_ge(s_ev, EVB + EV_LGT)
            t.nop(cycle_cnt=SDEL)
            for it in range(NT):
                t.transpose(PM[:, it * 16:it * 16 + 8],
                            lgT_sb[0:8, it * 128:(it + 1) * 128],
                            id_f32[0:8, 0:8])
            t.drain().then_inc(s_tp, 1)

    _ctx2.close()
    _ctx3 = contextlib.ExitStack()
    Pe = _ctx3.enter_context(nc.sbuf_tensor("Pe", [128, 2048], F32))
    geb = _ctx3.enter_context(nc.sbuf_tensor("geb", [128, 2048], BF16))
    cnt_sb = _ctx3.enter_context(nc.sbuf_tensor("cnt_sb", [1, 2048], F32))
    th = _ctx3.enter_context(nc.sbuf_tensor("th", [128, 32], F32))
    lohi = _ctx3.enter_context(nc.sbuf_tensor("lohi", [1, 32], F32))
    cnts = _ctx3.enter_context(nc.sbuf_tensor("cnts", [1, 16], F32))
    neg1 = _ctx3.enter_context(nc.sbuf_tensor("neg1", [128, 256], F32))
    eqtmp = _ctx3.enter_context(nc.sbuf_tensor("eqtmp", [128, 256], F32))
    keep = _ctx3.enter_context(nc.sbuf_tensor("keep", [128, 256], F32))
    fE = _ctx3.enter_context(nc.sbuf_tensor("fE", [128, 256], F32))
    own_e = _ctx3.enter_context(nc.sbuf_tensor("own_e", [128, 32], F32))
    masksC = _ctx3.enter_context(nc.sbuf_tensor("masksC", [128, 256], F32))
    eqb = _ctx3.enter_context(nc.sbuf_tensor("eqb", [128, 256], F32))
    onehotT = _ctx3.enter_context(nc.sbuf_tensor("onehotT", [8, 4096], F32))
    w1s = _ctx3.enter_context(nc.sbuf_tensor("w1s", [128, 8192], BF16))
    x2u = _ctx3.enter_context(nc.sbuf_tensor("x2u", [128, 2 * 3072], BF16))
    w2s = _ctx3.enter_context(nc.sbuf_tensor("w2s", [128, 8192], BF16))
    htr = _ctx3.enter_context(nc.sbuf_tensor("htr", [128, 6 * 384], BF16))
    ystage = _ctx3.enter_context(nc.sbuf_tensor("ystage", [128, 3072], F32))
    acc = _ctx3.enter_context(nc.sbuf_tensor("acc", [128, 3072], F32))
    xres = _ctx3.enter_context(nc.sbuf_tensor("xres", [128, 3072], F32))
    fpa = _ctx3.enter_context(nc.psum_tensor("fpa", [128, 512], F32))
    fpb = _ctx3.enter_context(nc.psum_tensor("fpb", [128, 512], F32))
    fpy = _ctx3.enter_context(nc.psum_tensor("fpy", [128, 3072], F32))
    SRCH_STEPS = []
    if not DEV_SKIP_SEARCH:
        SRCH_STEPS.append("v_init")
    if not DEV_SKIP_SEARCH:
        for k in range(SEARCH_ITERS):
            SRCH_STEPS += ["v_mid", "t_bc", "s_th", "v_ge", "t_cnt", "s_cnt", "v_upd"]
        SRCH_STEPS += ["v_midhi", "t_bc", "s_th", "v_fin"]
        for grp in range(8):
            SRCH_STEPS += ["t_eqT%d" % grp, "s_eqv%d" % grp]
    SRCH_TOTAL = len(SRCH_STEPS)

    @block.sync
    def _(sync):
        sync.wait_ge(s_cv, CV_TOTAL)
        sync.wait_ge(s_sc, NCHT * 32)
        sync.wait_ge(s_vx, NCHT * VX_A)
        sync.wait_ge(s_ev, SETUP_EV + NCHT * EV_A)
        for u in range(NU):
            ci, e = divmod(u, E)
            tk0u, nttu = FFN_CHUNKS[ci]
            if u >= 2:
                sync.wait_ge(s_l1, (u - 2) * NFT + NFT + 2)
            sync.dma_start(
                out=x2u[:, (u % 2) * 3072:(u % 2) * 3072 + KT * nttu * 128]
                    .rearrange("r (kt t) -> r kt t", kt=KT),
                in_=x2ln_dram[:, :, tk0u:tk0u + nttu * 128]
                    .rearrange("kt p t -> p kt t"),
            ).then_inc(s_fw, 16)
            for p in range(8):
                G = u * 8 + p
                if G >= 2:
                    sync.wait_ge(s_l1, 4 * G - 2)
                sync.dma_start(
                    out=w1s[:, (G % 2) * 4096:(G % 2 + 1) * 4096]
                        .rearrange("r (ft kt c) -> r ft kt c", ft=4, kt=KT),
                    in_=W1bf[e, p * 4:(p + 1) * 4, :, :, :]
                        .rearrange("ft kt r c -> r ft kt c"),
                ).then_inc(s_fw, 16)
                if G >= 2:
                    sync.wait_ge(s_l1, 4 * G)
                sync.dma_start(
                    out=w2s[:, (G % 2) * 4096:(G % 2 + 1) * 4096]
                        .rearrange("r (ft m) -> r ft m", ft=4),
                    in_=W2bf[e, p * 4:(p + 1) * 4, :, :]
                        .rearrange("ft r m -> r ft m"),
                ).then_inc(s_fw, 16)

    @block.gpsimd
    def _(g):
        g.wait_ge(s_vx, NCHT * VX_A)
        g.wait_ge(s_ev, SETUP_EV + NCHT * EV_A)
        g.wait_ge(s_cv, CV_TOTAL)
        for ci in range(NCI):
            tk0, ntt = FFN_CHUNKS[ci]
            if ci >= 1:
                g.wait_ge(s_ac, (ci - 1) * 9 + 1)
            g.dma_start(
                out=xres[:, 0:ntt * 1024].rearrange("p (n h) -> p n h", n=ntt),
                in_=x2_scr[tk0:tk0 + ntt * 128, :]
                    .rearrange("(n p) h -> p n h", p=128),
            ).then_inc(s_fx, 16)
            g.wait_ge(s_ac, ci * 9 + 9)
            g.nop(cycle_cnt=2048)
            g.dma_start(
                out=out[tk0:tk0 + ntt * 128, :]
                    .rearrange("(n p) h -> p n h", p=128),
                in_=acc[:, 0:ntt * 1024].rearrange("p (n h) -> p n h", n=ntt),
            ).then_inc(s_st, 16)
        g.wait_ge(s_st, NCI * 16)

    @block.scalar
    def _(s):
        s.wait_ge(s_cv, CV_TOTAL)
        s.wait_ge(s_vx, NCHT * VX_A)
        s.wait_ge(s_sc, NCHT * 32)
        for k, step in enumerate(SRCH_STEPS):
            if step == "s_th":
                s.wait_ge(s_srch, k)
                s.nop(cycle_cnt=CDEL)
                s.copy(out=th[:, 0:8], in_=fpa[:, 0:8])
                s.drain().then_inc(s_srch, 1)
            elif step == "s_cnt":
                s.wait_ge(s_srch, k)
                s.nop(cycle_cnt=CDEL)
                s.copy(out=cnt_sb[:], in_=fpy[0:1, 0:2048])
                s.drain().then_inc(s_srch, 1)
            elif step.startswith("s_eqv"):
                grp = int(step[5:])
                s.wait_ge(s_srch, k)
                s.nop(cycle_cnt=CDEL)
                s.copy(out=onehotT[0:8, grp * 512:(grp + 1) * 512],
                       in_=(fpa if grp % 2 == 0 else fpb)[0:8, 0:512])
                s.drain().then_inc(s_srch, 1)
        for u in range(NU):
            ci, e = divmod(u, E)
            tk0, ntt = FFN_CHUNKS[ci]
            if e == 0:
                s.wait_ge(s_yd, ci * 9 + 1)
                if ci > 0:
                    s.wait_ge(s_ac, ci * 9)
                s.nop(cycle_cnt=CDEL)
                for tt in range(ntt):
                    s.copy(out=ystage[:, tt * 1024:(tt + 1) * 1024],
                           in_=fpy[:, tt * 1024:(tt + 1) * 1024])
                s.drain().then_inc(s_yc, 1)
            for f in range(NFT):
                fg = u * NFT + f
                s.wait_ge(s_l1, fg + 1)
                s.nop(cycle_cnt=SDEL)
                s.activation(out=htr[:, (fg % 6) * 384:(fg % 6) * 384 + ntt * 128],
                             in_=(fpa if fg % 2 == 0 else fpb)[:, 0:ntt * 128],
                             func=AF.Relu, bias=b1s[:, e * 32 + f:e * 32 + f + 1])
                s.nop(cycle_cnt=512)
                s.drain().then_inc(s_rl, 1)
            s.wait_ge(s_yd, ci * 9 + 1 + e + 1)
            s.wait_ge(s_ac, ci * 9 + 1 + e)
            s.nop(cycle_cnt=CDEL)
            for tt in range(ntt):
                s.copy(out=ystage[:, tt * 1024:(tt + 1) * 1024],
                       in_=fpy[:, tt * 1024:(tt + 1) * 1024])
            s.drain().then_inc(s_yc, 1)

    @block.vector
    def _(v):
        v.wait_ge(s_cv, CV_TOTAL)
        v.wait_ge(s_ev, SETUP_EV + NCHT * EV_A)
        v.wait_ge(s_sc, NCHT * 32)
        for k, step in enumerate(SRCH_STEPS):
            if not step.startswith("v"):
                continue
            v.wait_ge(s_srch, k)
            v.nop(cycle_cnt=SDEL)
            if step == "v_init":
                v.memset(neg1[:], -1.0)
                v.memset(lohi[0:1, 0:8], 0.05)
                v.memset(lohi[0:1, 8:16], 1.01)
                v.memset(own_e[:], 0.0)
                v.tensor_scalar(keep[:], P_all[:], 1.0, None, op0=ALU.add)
                for e_ in range(E):
                    v.tensor_scalar(eqtmp[:], E_all[:], float(e_), None,
                                    op0=ALU.is_equal)
                    v.tensor_tensor(eqtmp[:], eqtmp[:], keep[:], ALU.mult)
                    v.tensor_scalar(Pe[:, e_ * 256:(e_ + 1) * 256], eqtmp[:], 1.0,
                                    None, op0=ALU.subtract)
            elif step == "v_mid":
                v.tensor_tensor(cnts[0:1, 8:16], lohi[0:1, 0:8],
                                lohi[0:1, 8:16], ALU.add)
                vfl(v)
                v.tensor_scalar(lohi[0:1, 16:24], cnts[0:1, 8:16], 0.5, None,
                                op0=ALU.mult)
            elif step == "v_ge":
                for e_ in range(E):
                    v.tensor_scalar(geb[:, e_ * 256:(e_ + 1) * 256],
                                    Pe[:, e_ * 256:(e_ + 1) * 256],
                                    th[:, e_:e_ + 1], None, op0=ALU.is_ge)
            elif step == "v_upd":
                for e_ in range(E):
                    v.reduce_sum(cnts[0:1, e_:e_ + 1],
                                 cnt_sb[0:1, e_ * 256:(e_ + 1) * 256], axis=AX.X)
                vfl(v)
                v.tensor_scalar(cnts[0:1, 8:16], cnts[0:1, 0:8], CAP, None,
                                op0=ALU.is_gt)
                vfl(v)
                v.scalar_tensor_tensor(out=th[0:1, 0:8], in0=lohi[0:1, 0:8],
                                       scalar=-1.0, in1=lohi[0:1, 16:24],
                                       op0=ALU.mult, op1=ALU.add)
                v.scalar_tensor_tensor(out=th[0:1, 8:16], in0=lohi[0:1, 8:16],
                                       scalar=-1.0, in1=lohi[0:1, 16:24],
                                       op0=ALU.mult, op1=ALU.add)
                v.tensor_scalar(cnts[0:1, 0:8], cnts[0:1, 8:16], -1.0, 1.0,
                                op0=ALU.mult, op1=ALU.add)
                vfl(v)
                v.tensor_tensor(th[0:1, 16:24], th[0:1, 0:8], cnts[0:1, 8:16],
                                ALU.mult)
                v.tensor_tensor(th[0:1, 24:32], th[0:1, 8:16], cnts[0:1, 0:8],
                                ALU.mult)
                vfl(v)
                v.tensor_tensor(lohi[0:1, 0:8], lohi[0:1, 0:8], th[0:1, 16:24],
                                ALU.add)
                v.tensor_tensor(lohi[0:1, 8:16], lohi[0:1, 8:16], th[0:1, 24:32],
                                ALU.add)
            elif step == "v_midhi":
                v.tensor_copy(out=lohi[0:1, 16:24], in_=lohi[0:1, 8:16])
            elif step == "v_fin":
                for e_ in range(E):
                    if e_ == 0:
                        v.tensor_scalar(keep[:], Pe[:, 0:256], th[:, 0:1],
                                        None, op0=ALU.is_ge)
                    else:
                        v.tensor_scalar(eqtmp[:], Pe[:, e_ * 256:(e_ + 1) * 256],
                                        th[:, e_:e_ + 1], None, op0=ALU.is_ge)
                        v.tensor_tensor(keep[:], keep[:], eqtmp[:], ALU.add)
                v.tensor_tensor(fE[:], E_all[:], keep[:], ALU.mult)
                for b_ in range(B):
                    v.scalar_tensor_tensor(out=own_e[:],
                                           in0=fE[:, b_ * 32:(b_ + 1) * 32],
                                           scalar=ohbc[:, b_:b_ + 1],
                                           in1=own_e[:], op0=ALU.mult,
                                           op1=ALU.add)
                    vfl(v)
                for e_ in range(E):
                    v.tensor_scalar(masksC[:, e_ * 32:(e_ + 1) * 32], own_e[:],
                                    float(e_), None, op0=ALU.is_equal)
                for tg in range(32):
                    v.tensor_scalar(eqb[:, tg * 8:(tg + 1) * 8], iota8[:, 0:8],
                                    own_e[:, tg:tg + 1], None, op0=ALU.is_equal)
            v.drain().then_inc(s_srch, 1)
        for ci in range(NCI):
            tk0, ntt = FFN_CHUNKS[ci]
            v.wait_ge(s_yc, ci * 9 + 1)
            v.wait_ge(s_fx, (ci + 1) * 16)
            if ci > 0:
                v.wait_ge(s_st, ci * 16)
            v.nop(cycle_cnt=SDEL)
            for tt in range(ntt):
                v.tensor_tensor(acc[:, tt * 1024:(tt + 1) * 1024],
                                ystage[:, tt * 1024:(tt + 1) * 1024],
                                xres[:, tt * 1024:(tt + 1) * 1024], ALU.add)
            v.drain().then_inc(s_ac, 1)
            for e in range(E):
                v.wait_ge(s_yc, ci * 9 + 2 + e)
                v.nop(cycle_cnt=SDEL)
                for tt in range(ntt):
                    v.scalar_tensor_tensor(
                        out=acc[:, tt * 1024:(tt + 1) * 1024],
                        in0=ystage[:, tt * 1024:(tt + 1) * 1024],
                        scalar=masksC[:, e * 32 + tk0 // 128 + tt:
                                      e * 32 + tk0 // 128 + tt + 1],
                        in1=acc[:, tt * 1024:(tt + 1) * 1024],
                        op0=ALU.mult, op1=ALU.add)
                v.drain().then_inc(s_ac, 1)

    @block.tensor
    def _(t):
        t.wait_ge(s_ev, SETUP_EV + NCHT * EV_A)
        t.wait_ge(s_vx, NCHT * VX_A)
        t.wait_ge(s_cv, CV_TOTAL)
        t.wait_ge(s_sc, NCHT * 32)
        for k, step in enumerate(SRCH_STEPS):
            if not step.startswith("t"):
                continue
            t.wait_ge(s_srch, k)
            if step == "t_bc":
                t.matmul(fpa[:, 0:8], ones_row[0:1, :], lohi[0:1, 16:24],
                         start=True, stop=True)
            elif step == "t_cnt":
                for e_ in range(E):
                    t.matmul(fpy[0:1, e_ * 256:(e_ + 1) * 256],
                             ones_col_bf[:], geb[:, e_ * 256:(e_ + 1) * 256],
                             start=True, stop=True)
                    t.nop(cycle_cnt=PACE1)
            elif step.startswith("t_eqT"):
                grp = int(step[5:])
                P = fpa if grp % 2 == 0 else fpb
                for tgl in range(4):
                    tg = grp * 4 + tgl
                    t.transpose(P[0:8, tgl * 128:(tgl + 1) * 128],
                                eqb[:, tg * 8:(tg + 1) * 8], id_f32[:])
            t.drain().then_inc(s_srch, 1)
        t.wait_ge(s_srch, SRCH_TOTAL)
        for _ in range(64):
            t.matmul(fpa[:, 0:128], id_bf[:], id_bf[:], start=True, stop=True)
        def emit_l2(u, ci, e, tk0, ntt, fl):
            fgl = u * NFT + fl
            if fl % 4 == 0:
                t.wait_ge(s_fw, (u * 16 + 2 * (fl // 4) + 2) * 16)
            if fl == 0:
                t.wait_ge(s_yc, ci * 9 + 1 + e)
            t.wait_ge(s_rl, min(u * NFT + fl + 2, u * NFT + NFT))
            slot = fgl % 6
            pw = (fl // 4) % 2
            for tt in range(ntt):
                for half in range(2):
                    t.matmul(fpy[:, tt * 1024 + half * 512:tt * 1024 + (half + 1) * 512],
                             htr[:, slot * 384 + tt * 128:slot * 384 + (tt + 1) * 128],
                             w2s[:, pw * 4096 + (fl % 4) * 1024 + half * 512:
                                 pw * 4096 + (fl % 4) * 1024 + (half + 1) * 512],
                             start=(fl == 0), stop=(fl == NFT - 1))
                    t.nop(cycle_cnt=PACE2)
        for u in range(NU):
            ci, e = divmod(u, E)
            tk0, ntt = FFN_CHUNKS[ci]
            if e == 0:
                if ci > 0:
                    t.wait_ge(s_yc, ci * 9)
                for tt in range(ntt):
                    tg = tk0 // 128 + tt
                    for half in range(2):
                        t.matmul(fpy[:, tt * 1024 + half * 512:tt * 1024 + (half + 1) * 512],
                                 onehotT[0:8, tg * 128:(tg + 1) * 128],
                                 b2s[0:8, half * 512:(half + 1) * 512],
                                 start=True, stop=True)
                        t.nop(cycle_cnt=PACEF)
                t.drain().then_inc(s_yd, 1)
            for f in range(NFT):
                fg = u * NFT + f
                if f == 0:
                    t.wait_ge(s_fw, (u * 17 + 1) * 16)
                if f % 4 == 0:
                    t.wait_ge(s_fw, (u * 17 + 2 * (f // 4) + 2) * 16)
                bank = fpa if fg % 2 == 0 else fpb
                if fg >= 2:
                    t.wait_ge(s_rl, fg - 1)
                pw = (f // 4) % 2
                for kt in range(KT):
                    t.matmul(bank[:, 0:ntt * 128],
                             w1s[:, pw * 4096 + (f % 4) * 1024 + kt * 128:
                                 pw * 4096 + (f % 4) * 1024 + (kt + 1) * 128],
                             x2u[:, (u % 2) * 3072 + kt * ntt * 128:
                                 (u % 2) * 3072 + (kt + 1) * ntt * 128],
                             start=(kt == 0), stop=(kt == KT - 1))
                    t.nop(cycle_cnt=PACE1)
                t.drain().then_inc(s_l1, 1)
                if f >= 2:
                    emit_l2(u, ci, e, tk0, ntt, f - 2)
            emit_l2(u, ci, e, tk0, ntt, NFT - 2)
            emit_l2(u, ci, e, tk0, ntt, NFT - 1)
            t.drain().then_inc(s_yd, 1)
    _ctx3.close()
    _ctx1.close()
    return nc


_CONSTS = None


def _host_consts():
    global _CONSTS
    if _CONSTS is None:
        identf = np.eye(128, dtype=np.float32)
        identb = np.eye(128, dtype=np.float32).astype(ml_dtypes.bfloat16)
        tri = np.zeros((128, 4 * 512), dtype=np.float32)
        for jt in range(4):
            p = np.arange(128)[:, None] + jt * 128
            i = np.arange(512)[None, :]
            tri[:, jt * 512:(jt + 1) * 512] = (p <= i).astype(np.float32)
        trimask = tri.astype(ml_dtypes.bfloat16)
        iotac = np.zeros((128, 16), dtype=np.float32)
        iotac[:, 0:8] = np.arange(8)[None, :]
        iotac[:, 8:16] = np.arange(8)[None, :] + 1000.0
        _CONSTS = (identf, identb, trimask, iotac)
    return _CONSTS


_NC = None


def _get_nc():
    global _NC
    if _NC is None:
        _NC = build_nc()
    return _NC


def kernel(L0_attn_hidden_states, L0_attn_w_q, L0_attn_w_k, L0_attn_w_v, L0_attn_w_o,
           L0_attn_gate, L0_attn_memory, L0_attn_memory_norm, L0_ffn_router_w,
           L0_ffn_W1, L0_ffn_b1, L0_ffn_W2, L0_ffn_b2):
    identf, identb, trimask, iotac = _host_consts()
    x = np.ascontiguousarray(np.asarray(L0_attn_hidden_states, dtype=np.float32))
    base = {
        "x_all": x,
        "w_q": np.ascontiguousarray(np.asarray(L0_attn_w_q, dtype=np.float32)),
        "w_k": np.ascontiguousarray(np.asarray(L0_attn_w_k, dtype=np.float32)),
        "w_v": np.ascontiguousarray(np.asarray(L0_attn_w_v, dtype=np.float32)),
        "w_o": np.ascontiguousarray(np.asarray(L0_attn_w_o, dtype=np.float32)),
        "gate": np.asarray(L0_attn_gate, dtype=np.float32).reshape(1, 1),
        "memory": np.ascontiguousarray(np.asarray(L0_attn_memory, dtype=np.float32)),
        "memnorm": np.asarray(L0_attn_memory_norm, dtype=np.float32).reshape(1, DM),
        "router": np.ascontiguousarray(np.asarray(L0_ffn_router_w, dtype=np.float32)),
        "W1": np.ascontiguousarray(np.asarray(L0_ffn_W1, dtype=np.float32)),
        "b1in": np.ascontiguousarray(np.asarray(L0_ffn_b1, dtype=np.float32)),
        "W2": np.ascontiguousarray(np.asarray(L0_ffn_W2, dtype=np.float32)),
        "b2in": np.ascontiguousarray(np.asarray(L0_ffn_b2, dtype=np.float32)),
        "identf": identf,
        "identb": identb,
        "trimask_in": trimask,
        "iotac": iotac,
    }
    in_maps = []
    for c in range(B):
        m = dict(base)
        oh = np.zeros((1, B), dtype=np.float32)
        oh[0, c] = 1.0
        m["onehot"] = oh
        in_maps.append(m)
    nc = _get_nc()
    res = run_bass_kernel_spmd(nc, in_maps, core_ids=list(range(B)))
    return np.stack([res.results[c]["out"] for c in range(B)], axis=0)


# version 08
# speedup vs baseline: 1.0327x; 1.0327x over previous
import contextlib
import numpy as np
import ml_dtypes
import concourse.bass as bass
import concourse.mybir as mybir
from concourse.bass_utils import run_bass_kernel_spmd

F32 = mybir.dt.float32
BF16 = mybir.dt.bfloat16
AF = mybir.ActivationFunctionType
ALU = mybir.AluOpType
AX = mybir.AxisListType

B, S, H, DM = 8, 4096, 1024, 256
E, DFF = 8, 4096
CAP = 5120.0
EPS_ATT = 1e-6
EPS_LN = 1e-5
CHUNK = 512
NCH = S // CHUNK
NT = CHUNK // 128
KT = H // 128
NCHT = B * NCH
NFT = 32
FFN_CHUNKS = [(i * 384, 3) for i in range(10)] + [(3840, 2)]
NCI = len(FFN_CHUNKS)
NU = NCI * E

PACE1 = 144
PACE2 = 192
PACEF = 640
CDEL = 1536
SDEL = 768
SEARCH_ITERS = 27
DEV_EXPOSE = False
DEV_SKIP_CONV = False
DEV_SKIP_SEARCH = False
_BASS_KW = {}

VX_LN1, VX_LNB1, VX_SIG, VX_MSK = 1, 2, 3, 4
VX_CMB0 = 5
VX_SREF, VX_X2LN, VX_LNB2, VX_OWN, VX_TOP1, VX_PROB = 9, 10, 11, 12, 13, 14
VX_A = 14
EV_RSTD1 = 1
EV_XLNT0 = 2
EV_QK, EV_V, EV_SCM, EV_SKTM = 6, 7, 8, 9
EV_NM0 = 10
EV_CMBT, EV_SUP = 14, 15
EV_AO0 = 16
EV_RSTD2 = 20
EV_X2T0 = 21
EV_LGT, EV_LGTM, EV_TEXP = 29, 30, 31
EV_A = 31
TP_XLN0 = 1
TP_QK, TP_V, TP_SC, TP_SKT = 5, 6, 7, 8
TP_NM0 = 9
TP_CMBT, TP_SUP = 13, 14
TP_AO0 = 15
TP_X2T0 = 19
TP_LG, TP_LGT = 27, 28
TP_A = 28
LN_A = 2
SETUP_TP = 12
SETUP_EV = 12
SET_TOTAL = 6
CV_TOTAL = (E * KT * 4 + E * 16 * 2) * 16


def build_nc():
    nc = bass.Bass("TRN2", target_bir_lowering=False, **_BASS_KW)
    x_all = nc.dram_tensor("x_all", [B, S, H], F32, kind="ExternalInput").ap()
    onehot = nc.dram_tensor("onehot", [1, B], F32, kind="ExternalInput").ap()
    w_q = nc.dram_tensor("w_q", [DM, H], F32, kind="ExternalInput").ap()
    w_k = nc.dram_tensor("w_k", [DM, H], F32, kind="ExternalInput").ap()
    w_v = nc.dram_tensor("w_v", [DM, H], F32, kind="ExternalInput").ap()
    w_o = nc.dram_tensor("w_o", [H, DM], F32, kind="ExternalInput").ap()
    gate = nc.dram_tensor("gate", [1, 1], F32, kind="ExternalInput").ap()
    memory = nc.dram_tensor("memory", [DM, DM], F32, kind="ExternalInput").ap()
    memnorm = nc.dram_tensor("memnorm", [1, DM], F32, kind="ExternalInput").ap()
    router = nc.dram_tensor("router", [E, H], F32, kind="ExternalInput").ap()
    W1 = nc.dram_tensor("W1", [E, H, DFF], F32, kind="ExternalInput").ap()
    b1in = nc.dram_tensor("b1in", [E, DFF], F32, kind="ExternalInput").ap()
    W2 = nc.dram_tensor("W2", [E, DFF, H], F32, kind="ExternalInput").ap()
    b2in = nc.dram_tensor("b2in", [E, H], F32, kind="ExternalInput").ap()
    identf = nc.dram_tensor("identf", [128, 128], F32, kind="ExternalInput").ap()
    identb = nc.dram_tensor("identb", [128, 128], BF16, kind="ExternalInput").ap()
    trimask_in = nc.dram_tensor("trimask_in", [128, 4 * 512], BF16, kind="ExternalInput").ap()
    iotac = nc.dram_tensor("iotac", [128, 16], F32, kind="ExternalInput").ap()
    out = nc.dram_tensor("out", [S, H], F32, kind="ExternalOutput").ap()
    x2_scr = nc.dram_tensor("x2_scr", [S, H], F32,
                            kind=("ExternalOutput" if DEV_EXPOSE else "Internal"))
    x2ln_dram32 = nc.dram_tensor("x2ln_dram", [KT, 128, S // 2], F32,
                                 kind=("ExternalOutput" if DEV_EXPOSE else "Internal"))
    x2ln_dram = x2ln_dram32.bitcast(BF16)
    W1bf = nc.dram_tensor("W1bf", [E, NFT, KT, 128, 128], BF16, kind="Internal")
    W2bf = nc.dram_tensor("W2bf", [E, NFT, 128, H], BF16, kind="Internal")

    _ctx1 = contextlib.ExitStack()
    id_f32 = _ctx1.enter_context(nc.sbuf_tensor("id_f32", [128, 128], F32))
    id_bf = _ctx1.enter_context(nc.sbuf_tensor("id_bf", [128, 128], BF16))
    iota8 = _ctx1.enter_context(nc.sbuf_tensor("iota8", [128, 16], F32))
    ones_row = _ctx1.enter_context(nc.sbuf_tensor("ones_row", [1, 128], F32))
    ones_col_bf = _ctx1.enter_context(nc.sbuf_tensor("ones_col_bf", [128, 1], BF16))
    b1s = _ctx1.enter_context(nc.sbuf_tensor("b1s", [128, 256], F32))
    b2s = _ctx1.enter_context(nc.sbuf_tensor("b2s", [8, H], F32))
    ohbc = _ctx1.enter_context(nc.sbuf_tensor("ohbc", [128, 8], F32))
    epsc = _ctx1.enter_context(nc.sbuf_tensor("epsc", [128, 1], F32))
    P_all = _ctx1.enter_context(nc.sbuf_tensor("P_all", [128, 256], F32))
    E_all = _ctx1.enter_context(nc.sbuf_tensor("E_all", [128, 256], F32))
    s_pre = _ctx1.enter_context(nc.semaphore("s_pre"))
    s_set = _ctx1.enter_context(nc.semaphore("s_set"))
    s_xld = _ctx1.enter_context(nc.semaphore("s_xld"))
    s_cv = _ctx1.enter_context(nc.semaphore("s_cv"))
    s_ln = _ctx1.enter_context(nc.semaphore("s_ln"))
    s_tp = _ctx1.enter_context(nc.semaphore("s_tp"))
    s_ev = _ctx1.enter_context(nc.semaphore("s_ev"))
    s_vx = _ctx1.enter_context(nc.semaphore("s_vx"))
    s_sc = _ctx1.enter_context(nc.semaphore("s_sc"))
    s_srch = _ctx1.enter_context(nc.semaphore("s_srch"))
    s_fw = _ctx1.enter_context(nc.semaphore("s_fw"))
    s_fx = _ctx1.enter_context(nc.semaphore("s_fx"))
    s_l1 = _ctx1.enter_context(nc.semaphore("s_l1"))
    s_rl = _ctx1.enter_context(nc.semaphore("s_rl"))
    s_yd = _ctx1.enter_context(nc.semaphore("s_yd"))
    s_yc = _ctx1.enter_context(nc.semaphore("s_yc"))
    s_ac = _ctx1.enter_context(nc.semaphore("s_ac"))
    s_st = _ctx1.enter_context(nc.semaphore("s_st"))
    s_flv = _ctx1.enter_context(nc.semaphore("s_flv"))
    s_fls = _ctx1.enter_context(nc.semaphore("s_fls"))
    _fv = [0]
    def vfl(v):
        _fv[0] += 1
        v.drain().then_inc(s_flv, 1)
        v.wait_ge(s_flv, _fv[0])
    _fs = [0]
    def sfl(s):
        _fs[0] += 1
        s.drain().then_inc(s_fls, 1)
        s.wait_ge(s_fls, _fs[0])
    block = _ctx1.enter_context(nc.Block())
    _ctx2 = contextlib.ExitStack()
    stage = _ctx2.enter_context(nc.sbuf_tensor("stage", [128, 4096], F32))
    gbc = _ctx2.enter_context(nc.sbuf_tensor("gbc", [128, 2], F32))
    rwT = _ctx2.enter_context(nc.sbuf_tensor("rwT", [128, 64], F32))
    maug = _ctx2.enter_context(nc.sbuf_tensor("maug", [128, 2 * 257], BF16))
    woeh = _ctx2.enter_context(nc.sbuf_tensor("woeh", [128, 2048], BF16))
    wvT = _ctx2.enter_context(nc.sbuf_tensor("wvT", [128, 2048], BF16))
    wkT = _ctx2.enter_context(nc.sbuf_tensor("wkT", [128, 2048], BF16))
    wqT = _ctx2.enter_context(nc.sbuf_tensor("wqT", [128, 2048], BF16))
    trimask = _ctx2.enter_context(nc.sbuf_tensor("trimask", [128, 4 * 512], BF16))
    cvA = _ctx2.enter_context(nc.sbuf_tensor("cvA", [128, 2048], BF16))
    cvB = _ctx2.enter_context(nc.sbuf_tensor("cvB", [128, 2048], BF16))
    xin = _ctx2.enter_context(nc.sbuf_tensor("xin", [128, 2 * 4096], F32))
    xln_tm = _ctx2.enter_context(nc.sbuf_tensor("xln_tm", [128, 4096], BF16))
    xlnT = _ctx2.enter_context(nc.sbuf_tensor("xlnT", [128, KT * 512], BF16))
    x2_t = _ctx2.enter_context(nc.sbuf_tensor("x2_t", [128, 4096], F32))
    x2ln_f = _ctx2.enter_context(nc.sbuf_tensor("x2ln_f", [128, 4096], F32))
    x2lnT_f = _ctx2.enter_context(nc.sbuf_tensor("x2lnT_f", [128, KT * 512], F32))
    sqT = _ctx2.enter_context(nc.sbuf_tensor("sqT", [128, 2 * 512], BF16))
    skT = _ctx2.enter_context(nc.sbuf_tensor("skT", [128, 2 * 512], BF16))
    sk_tm = _ctx2.enter_context(nc.sbuf_tensor("sk_tm", [128, 1024], BF16))
    v_aug = _ctx2.enter_context(nc.sbuf_tensor("v_aug", [128, 4 * 257], BF16))
    scm = _ctx2.enter_context(nc.sbuf_tensor("scm", [128, 4 * 512], BF16))
    eluR = _ctx2.enter_context(nc.sbuf_tensor("eluR", [128, 4 * 512], F32))
    x2sel = _ctx2.enter_context(nc.sbuf_tensor("x2sel", [128, KT * 512], BF16))
    nev = _ctx2.enter_context(nc.sbuf_tensor("nev", [128, 2 * 257], F32))
    mev = _ctx2.enter_context(nc.sbuf_tensor("mev", [128, 2 * 257], F32))
    tden = _ctx2.enter_context(nc.sbuf_tensor("tden", [128, 8], F32))
    comb = _ctx2.enter_context(nc.sbuf_tensor("comb", [128, 4 * 256], BF16))
    combT = _ctx2.enter_context(nc.sbuf_tensor("combT", [128, 1024], BF16))
    S_f = _ctx2.enter_context(nc.sbuf_tensor("S_f", [128, 2 * 257], F32))
    S_bf = _ctx2.enter_context(nc.sbuf_tensor("S_bf", [128, 2 * 257], BF16))
    Sup = _ctx2.enter_context(nc.sbuf_tensor("Sup", [128, 2 * 257], F32))
    lgT_sb = _ctx2.enter_context(nc.sbuf_tensor("lgT_sb", [8, 512], F32))
    lg_tm = _ctx2.enter_context(nc.sbuf_tensor("lg_tm", [128, 32], F32))
    stat_b = _ctx2.enter_context(nc.sbuf_tensor("stat_b", [128, 4], F32))
    stat_s = _ctx2.enter_context(nc.sbuf_tensor("stat_s", [128, 4], F32))
    t8a = _ctx2.enter_context(nc.sbuf_tensor("t8a", [128, 8], F32))
    sst = _ctx2.enter_context(nc.sbuf_tensor("sst", [128, 8], F32))
    vjunk = _ctx2.enter_context(nc.sbuf_tensor("vjunk", [128, 8], F32))
    t8b = _ctx2.enter_context(nc.sbuf_tensor("t8b", [128, 8], F32))
    t1a = _ctx2.enter_context(nc.sbuf_tensor("t1a", [128, 4], F32))
    t1b = _ctx2.enter_context(nc.sbuf_tensor("t1b", [128, 4], F32))
    TPP1 = _ctx2.enter_context(nc.psum_tensor("TPP1", [128, 1024], BF16))
    TPP2 = _ctx2.enter_context(nc.psum_tensor("TPP2", [128, 1024], BF16))
    PQ = _ctx2.enter_context(nc.psum_tensor("PQ", [128, 512], F32))
    PS = _ctx2.enter_context(nc.psum_tensor("PS", [128, 512], F32))
    PN = _ctx2.enter_context(nc.psum_tensor("PN", [128, 512], F32))
    PM = _ctx2.enter_context(nc.psum_tensor("PM", [128, 512], F32))
    PA = _ctx2.enter_context(nc.psum_tensor("PA", [128, 1024], F32))
    @block.sync
    def _(sync):
        sync.wait_ge(s_set, SET_TOTAL)
        for i in range(NCHT):
            b, c = divmod(i, NCH)
            half = i % 2
            if i >= 2:
                sync.wait_ge(s_vx, (i - 2) * VX_A + VX_X2LN)
            sync.dma_start(
                out=xin[:, half * 4096:(half + 1) * 4096]
                    .rearrange("p (n h) -> p n h", n=NT),
                in_=x_all[b, c * CHUNK:(c + 1) * CHUNK, :]
                    .rearrange("(n p) h -> p n h", p=128),
            ).then_inc(s_xld, 16)

    @block.gpsimd
    def _(g):
        pre = 0
        def gd(dst, src):
            nonlocal pre
            g.dma_start(out=dst, in_=src).then_inc(s_pre, 16)
            pre += 1
        gd(id_f32[:], identf[:])
        gd(id_bf[:], identb[:])
        gd(trimask[:], trimask_in[:])
        gd(iota8[:], iotac[:])
        gd(b2s[:], b2in[:, :])
        gd(stage[:, 0:1024], w_q[0:128, :])
        gd(stage[:, 1024:2048], w_q[128:256, :])
        gd(stage[:, 2048:3072], w_k[0:128, :])
        gd(stage[:, 3072:4096], w_k[128:256, :])
        g.wait_ge(s_set, 1)
        gd(stage[:, 0:1024], w_v[0:128, :])
        gd(stage[:, 1024:2048], w_v[128:256, :])
        for hb in range(KT):
            gd(stage[:, 2048 + hb * 256:2048 + (hb + 1) * 256],
               w_o[hb * 128:(hb + 1) * 128, :])
        g.wait_ge(s_set, 2)
        gd(stage[:, 0:256], memory[0:128, :])
        gd(stage[:, 256:512], memory[128:256, :])
        gd(stage[0:1, 512:768], memnorm[:, :])
        gd(stage[:, 768:769], memnorm[0, 0:128].rearrange("(p a) -> p a", a=1))
        gd(stage[:, 769:770], memnorm[0, 128:256].rearrange("(p a) -> p a", a=1))
        gd(stage[0:1, 770:771], gate[:, :])
        gd(stage[0:1, 772:780], onehot[:, :])
        gd(stage[0:8, 1024:2048], router[:, :])
        gd(stage[0:8, 2048:4096], b1in[:, 0:2048])
        g.wait_ge(s_tp, 10)
        gd(stage[0:8, 2048:4096], b1in[:, 2048:4096])
        g.wait_ge(s_set, 5)
        cvn = 0
        if DEV_SKIP_CONV:
            g.nop().then_inc(s_cv, CV_TOTAL)
        for e in range(E if not DEV_SKIP_CONV else 0):
            for hb in range(KT):
                for fh in range(2):
                    buf = cvA if cvn % 4 < 2 else cvB
                    g.dma_start(out=buf[:],
                                in_=W1[e, hb * 128:(hb + 1) * 128,
                                       fh * 2048:(fh + 1) * 2048]).then_inc(s_cv, 16)
                    cvn += 1
                    g.wait_ge(s_cv, cvn * 16)
                    g.dma_start(
                        out=W1bf[e, fh * 16:(fh + 1) * 16, hb, :, :]
                            .rearrange("ft r c -> r ft c"),
                        in_=buf[:].rearrange("r (ft c) -> r ft c", c=128),
                    ).then_inc(s_cv, 16)
                    cvn += 1
        for e in range(E if not DEV_SKIP_CONV else 0):
            for qb in range(16):
                buf = cvA if qb % 2 == 0 else cvB
                g.dma_start(
                    out=buf[:].rearrange("r (q m) -> r q m", q=2),
                    in_=W2[e, qb * 256:(qb + 1) * 256, :]
                        .rearrange("(q r) m -> r q m", r=128),
                ).then_inc(s_cv, 16)
                cvn += 1
                g.wait_ge(s_cv, cvn * 16)
                g.dma_start(
                    out=W2bf[e, qb * 2:(qb + 1) * 2, :, :].rearrange("q r m -> r q m"),
                    in_=buf[:].rearrange("r (q m) -> r q m", q=2),
                ).then_inc(s_cv, 16)
                cvn += 1

    @block.gpsimd
    def _(g):
        for i in range(NCHT):
            b, c = divmod(i, NCH)
            g.wait_ge(s_vx, i * VX_A + VX_OWN)
            g.nop(cycle_cnt=SDEL)
            g.dma_start(
                out=x2_scr[c * CHUNK:(c + 1) * CHUNK, :]
                    .rearrange("(n p) h -> p n h", p=128),
                in_=x2_t[:].rearrange("p (n h) -> p n h", n=NT),
                accum_op=(ALU.bypass if b == 0 else ALU.add),
            ).then_inc(s_sc, 16)
            g.dma_start(
                out=x2ln_dram[:, :, c * CHUNK:(c + 1) * CHUNK]
                    .rearrange("kt p t -> p kt t"),
                in_=x2sel[:].rearrange("p (kt t) -> p kt t", kt=KT),
                accum_op=(ALU.bypass if b == 0 else ALU.add),
            ).then_inc(s_sc, 16)

    if DEV_EXPOSE:
        _dumps = {}
        for _nm, _w in [
            ("d_xln_tm", 4096), ("d_xlnT", 4096), ("d_sqT", 1024), ("d_skT", 1024),
            ("d_sk_tm", 1024), ("d_v_aug", 1028), ("d_scm", 2048), ("d_comb", 1024),
            ("d_combT", 1024), ("d_x2_t", 4096), ("d_x2ln_f", 4096),
            ("d_x2lnT_f", 4096), ("d_eluR", 2048), ("d_Sf", 514),
            ("d_lgtm", 32), ("d_stats", 16), ("d_t8a", 16), ("d_nev", 514),
            ("d_mev", 514), ("d_gbc", 4), ("d_stg", 64),
        ]:
            _dumps[_nm] = nc.dram_tensor(_nm, [128, _w], F32, kind="ExternalOutput").ap()

        @block.gpsimd
        def _(g):
            g.wait_ge(s_vx, NCHT * VX_A)
            g.wait_ge(s_ev, SETUP_EV + NCHT * EV_A)
            g.nop(cycle_cnt=4096)
            _dc = [0]
            def dmp(nm, srcap, isbf, w):
                dst = _dumps[nm]
                g.dma_start(out=dst[:, 0:w], in_=srcap).then_inc(s_st, 16)
                _dc[0] += 1
            dmp("d_xln_tm", xln_tm[:], True, 4096)
            dmp("d_xlnT", xlnT[:], True, 4096)
            dmp("d_sqT", sqT[:], True, 1024)
            dmp("d_skT", skT[:], True, 1024)
            dmp("d_sk_tm", sk_tm[:], True, 1024)
            dmp("d_v_aug", v_aug[:], True, 1028)
            dmp("d_scm", scm[:], True, 2048)
            dmp("d_comb", comb[:], True, 1024)
            dmp("d_combT", combT[:], True, 1024)
            dmp("d_x2_t", x2_t[:], False, 4096)
            dmp("d_x2ln_f", x2ln_f[:], False, 4096)
            dmp("d_x2lnT_f", x2lnT_f[:], False, 4096)
            dmp("d_eluR", eluR[:], False, 2048)
            dmp("d_Sf", S_f[:], False, 514)
            dmp("d_lgtm", lg_tm[:], False, 32)
            dmp("d_stats", stat_b[:], False, 4)
            dmp("d_nev", nev[:], False, 514)
            dmp("d_mev", mev[:], False, 514)
            g.dma_start(out=_dumps["d_t8a"][:, 0:8], in_=t8a[:]).then_inc(s_st, 16)
            g.dma_start(out=_dumps["d_gbc"][:, 0:2], in_=gbc[:]).then_inc(s_st, 16)
            g.dma_start(out=_dumps["d_stg"][0:1, 0:64],
                        in_=stage[0:1, 744:808]).then_inc(s_st, 16)
            g.dma_start(out=_dumps["d_stats"][:, 4:8], in_=stat_s[:]).then_inc(s_st, 16)
            g.dma_start(out=_dumps["d_stats"][:, 8:12], in_=t1a[:]).then_inc(s_st, 16)
            g.dma_start(out=_dumps["d_stats"][:, 12:16], in_=t1b[:]).then_inc(s_st, 16)
            g.wait_ge(s_st, (_dc[0] + 6) * 16)

    @block.scalar
    def _(s):
        tp = [0]
        def sevac(dst_ap, src_ap, delay=CDEL):
            tp[0] += 1
            s.wait_ge(s_tp, tp[0])
            s.nop(cycle_cnt=delay)
            s.copy(out=dst_ap, in_=src_ap)
            s.drain().then_inc(s_ev, 1)
        s.wait_ge(s_pre, 9 * 16)
        s.copy(out=cvA[:, 0:1024], in_=stage[:, 0:1024])
        s.copy(out=cvB[:, 0:1024], in_=stage[:, 1024:2048])
        s.copy(out=cvA[:, 1024:2048], in_=stage[:, 2048:3072])
        s.copy(out=cvB[:, 1024:2048], in_=stage[:, 3072:4096])
        s.drain().then_inc(s_set, 1)
        sevac(wqT[:, 0:1024], TPP1[:, 0:1024])
        sevac(wqT[:, 1024:2048], TPP2[:, 0:1024])
        sevac(wkT[:, 0:1024], TPP1[:, 0:1024])
        sevac(wkT[:, 1024:2048], TPP2[:, 0:1024])
        s.wait_ge(s_pre, 19 * 16)
        s.copy(out=cvA[:, 0:1024], in_=stage[:, 0:1024])
        s.copy(out=cvB[:, 0:1024], in_=stage[:, 1024:2048])
        for hb in range(KT):
            s.copy(out=cvA[:, 1024 + hb * 128:1024 + (hb + 1) * 128],
                   in_=stage[:, 2048 + hb * 256:2048 + hb * 256 + 128])
            s.copy(out=cvB[:, 1024 + hb * 128:1024 + (hb + 1) * 128],
                   in_=stage[:, 2048 + hb * 256 + 128:2048 + (hb + 1) * 256])
        s.drain().then_inc(s_set, 1)
        sevac(wvT[:, 0:1024], TPP1[:, 0:1024])
        sevac(wvT[:, 1024:2048], TPP2[:, 0:1024])
        sevac(woeh[:, 0:1024], TPP1[:, 0:1024])
        sevac(woeh[:, 1024:2048], TPP2[:, 0:1024])
        s.wait_ge(s_pre, 28 * 16)
        s.copy(out=maug[:, 0:256], in_=stage[:, 0:256])
        s.copy(out=maug[:, 257:513], in_=stage[:, 256:512])
        s.copy(out=maug[:, 256:257], in_=stage[:, 768:769])
        s.copy(out=maug[:, 513:514], in_=stage[:, 769:770])
        s.activation(out=stage[0:1, 771:772], in_=stage[0:1, 770:771],
                     func=AF.Sigmoid)
        s.drain().then_inc(s_set, 1)
        sevac(rwT[:].rearrange("p (kt e) -> p kt e", e=8),
              PQ[:, 0:128].rearrange("p (kt e) -> p kt e", e=16)[:, :, 0:8])
        tp[0] += 1
        s.wait_ge(s_tp, tp[0])
        s.drain().then_inc(s_ev, 1)
        sevac(b1s[:].rearrange("p (e ft) -> p ft e", ft=32),
              PS[:, 0:512].rearrange("p (ft e) -> p ft e", e=16)[:, :, 0:8])
        tp[0] += 1
        s.wait_ge(s_tp, tp[0])
        s.nop(cycle_cnt=CDEL)
        s.copy(out=ohbc[:], in_=PN[:, 0:8])
        s.copy(out=gbc[:], in_=PN[:, 16:18])
        s.drain().then_inc(s_ev, 1)
        assert tp[0] == SETUP_TP
        s.wait_ge(s_set, 5)
        s.drain().then_inc(s_set, 1)
        for i in range(NCHT):
            b, c = divmod(i, NCH)
            half = i % 2
            xh = xin[:, half * 4096:(half + 1) * 4096]
            TPB = SETUP_TP + i * TP_A
            EVB = SETUP_EV + i * EV_A
            VXB = i * VX_A
            LNB = i * LN_A
            s.wait_ge(s_vx, VXB + VX_LN1)
            s.activation(out=sst[:, 0:4], in_=stat_s[:], func=AF.Ln,
                         bias=epsc[:, 0:1])
            sfl(s)
            s.activation(out=stat_s[:], in_=sst[:, 0:4], func=AF.Exp, scale=-0.5)
            s.drain().then_inc(s_ev, 1)
            s.wait_ge(s_vx, VXB + VX_LNB1)
            for tt in range(NT):
                s.activation(out=xln_tm[:, tt * 1024:(tt + 1) * 1024],
                             in_=xh[:, tt * 1024:(tt + 1) * 1024],
                             func=AF.Identity,
                             bias=stat_b[:, tt:tt + 1], scale=stat_s[:, tt:tt + 1])
            s.drain().then_inc(s_ln, 1)
            for gq in range(4):
                TPx = TPP1 if gq % 2 == 0 else TPP2
                s.wait_ge(s_tp, TPB + TP_XLN0 + gq)
                s.nop(cycle_cnt=SDEL)
                s.copy(out=xlnT[:, gq * 1024:(gq + 1) * 1024], in_=TPx[:, 0:1024])
                s.drain().then_inc(s_ev, 1)
            s.wait_ge(s_tp, TPB + TP_QK)
            s.nop(cycle_cnt=CDEL)
            for j, P in enumerate((PQ, PS, PN, PM)):
                s.activation(out=eluR[:, j * 512:(j + 1) * 512], in_=P[:],
                             func=AF.Relu)
                s.activation(out=x2ln_f[:, j * 512:(j + 1) * 512], in_=P[:],
                             func=AF.Relu, scale=-1.0)
            for j in range(4):
                dst = sqT if j < 2 else skT
                s.activation(out=dst[:, (j % 2) * 512:(j % 2 + 1) * 512],
                             in_=x2ln_f[:, j * 512:(j + 1) * 512],
                             func=AF.Exp, scale=-1.0)
            s.nop(cycle_cnt=SDEL)
            s.drain().then_inc(s_ev, 1)
            s.wait_ge(s_tp, TPB + TP_V)
            s.nop(cycle_cnt=CDEL)
            for j, P in enumerate((PQ, PS, PN, PM)):
                s.copy(out=v_aug[:, j * 257:j * 257 + 256], in_=P[:, 0:256])
            s.drain().then_inc(s_ev, 1)
            s.wait_ge(s_tp, TPB + TP_SC)
            s.nop(cycle_cnt=CDEL)
            for j, P in enumerate((PQ, PS, PN, PM)):
                s.copy(out=scm[:, j * 512:(j + 1) * 512], in_=P[:])
            s.drain().then_inc(s_ev, 1)
            s.wait_ge(s_tp, TPB + TP_SKT)
            s.nop(cycle_cnt=SDEL)
            s.copy(out=sk_tm[:], in_=TPP1[:, 0:1024])
            s.drain().then_inc(s_ev, 1)
            for it in range(NT):
                s.wait_ge(s_tp, TPB + TP_NM0 + it)
                if it >= 2:
                    s.wait_ge(s_vx, VXB + VX_CMB0 + it - 2)
                s.nop(cycle_cnt=CDEL)
                hN = it % 2
                s.copy(out=nev[:, hN * 257:hN * 257 + 257],
                       in_=(PQ if it % 2 == 0 else PS)[:, 0:257])
                s.copy(out=mev[:, hN * 257:hN * 257 + 257],
                       in_=(PN if it % 2 == 0 else PM)[:, 0:257])
                s.drain().then_inc(s_ev, 1)
            s.wait_ge(s_tp, TPB + TP_CMBT)
            s.nop(cycle_cnt=SDEL)
            s.copy(out=combT[:], in_=TPP2[:, 0:1024])
            s.drain().then_inc(s_ev, 1)
            s.wait_ge(s_tp, TPB + TP_SUP)
            s.nop(cycle_cnt=CDEL)
            s.copy(out=Sup[:, 0:257], in_=PQ[:, 0:257])
            s.copy(out=Sup[:, 257:514], in_=PS[:, 0:257])
            s.drain().then_inc(s_ev, 1)
            for it in range(NT):
                s.wait_ge(s_tp, TPB + TP_AO0 + it)
                if i > 0 and it == 0:
                    s.wait_ge(s_vx, (i - 1) * VX_A + VX_OWN)
                    s.wait_ge(s_sc, i * 32)
                s.nop(cycle_cnt=CDEL)
                s.copy(out=x2_t[:, it * 1024:(it + 1) * 1024], in_=PA[:])
                s.drain().then_inc(s_ev, 1)
            s.wait_ge(s_vx, VXB + VX_X2LN)
            s.activation(out=sst[:, 0:4], in_=stat_s[:], func=AF.Ln,
                         bias=epsc[:, 0:1])
            sfl(s)
            s.activation(out=stat_s[:], in_=sst[:, 0:4], func=AF.Exp, scale=-0.5)
            s.drain().then_inc(s_ev, 1)
            s.wait_ge(s_vx, VXB + VX_LNB2)
            for tt in range(NT):
                s.activation(out=x2ln_f[:, tt * 1024:(tt + 1) * 1024],
                             in_=x2_t[:, tt * 1024:(tt + 1) * 1024],
                             func=AF.Identity,
                             bias=stat_b[:, tt:tt + 1], scale=stat_s[:, tt:tt + 1])
            s.drain().then_inc(s_ln, 1)
            for gq in range(8):
                s.wait_ge(s_tp, TPB + TP_X2T0 + gq)
                s.nop(cycle_cnt=SDEL)
                s.copy(out=x2lnT_f[:, gq * 512:(gq + 1) * 512],
                       in_=(PQ if gq % 2 == 0 else PS)[:])
                s.drain().then_inc(s_ev, 1)
            s.wait_ge(s_tp, TPB + TP_LG)
            s.nop(cycle_cnt=CDEL)
            s.copy(out=lgT_sb[:], in_=PN[0:8, 0:512])
            s.drain().then_inc(s_ev, 1)
            s.wait_ge(s_tp, TPB + TP_LGT)
            s.nop(cycle_cnt=CDEL)
            s.copy(out=lg_tm[:].rearrange("p (it e) -> p it e", e=8),
                   in_=PM[:, 0:64].rearrange("p (it e) -> p it e", e=16)[:, :, 0:8])
            s.drain().then_inc(s_ev, 1)
            s.wait_ge(s_vx, VXB + VX_TOP1)
            s.nop(cycle_cnt=SDEL)
            for it in range(NT):
                s.activation(out=t8a[:], in_=lg_tm[:, it * 8:(it + 1) * 8],
                             func=AF.Exp, bias=t1b[:, it:it + 1],
                             accum_out=tden[:, it:it + 1])
            s.drain().then_inc(s_ev, 1)

    @block.vector
    def _(v):
        v.wait_ge(s_set, 3)
        v.nop(cycle_cnt=SDEL)
        v.memset(vjunk[:], 0.0)
        v.memset(epsc[:], EPS_LN)
        v.reduce_sum(stage[0:1, 781:782], stage[0:1, 512:768], axis=AX.X)
        v.memset(ones_row[:], 1.0)
        vfl(v)
        v.tensor_scalar(stage[0:1, 781:782], stage[0:1, 781:782], EPS_ATT, None,
                        op0=ALU.is_ge)
        v.memset(ones_col_bf[:], 1.0)
        vfl(v)
        v.tensor_tensor(stage[0:1, 782:783], stage[0:1, 771:772],
                        stage[0:1, 781:782], ALU.mult)
        v.tensor_scalar(stage[0:1, 783:784], stage[0:1, 771:772], -1.0, 1.0,
                        op0=ALU.mult, op1=ALU.add)
        v.memset(S_f[:], 0.0)
        v.memset(S_bf[:], 0.0)
        v.memset(P_all[:], 0.0)
        v.memset(E_all[:], 0.0)
        for j in range(4):
            v.memset(v_aug[:, j * 257 + 256:j * 257 + 257], 1.0)
        v.drain().then_inc(s_set, 1)
        for i in range(NCHT):
            b, c = divmod(i, NCH)
            half = i % 2
            xh = xin[:, half * 4096:(half + 1) * 4096]
            EVB = SETUP_EV + i * EV_A
            VXB = i * VX_A
            col0 = b * 32 + c * 4
            v.wait_ge(s_xld, (i + 1) * 16)
            v.nop(cycle_cnt=SDEL)
            for tt in range(NT):
                xt_ = xh[:, tt * 1024:(tt + 1) * 1024]
                v.reduce_sum(t1a[:, tt:tt + 1], xt_, axis=AX.X)
                v.tensor_tensor(x2ln_f[:, tt * 1024:(tt + 1) * 1024], xt_, xt_,
                                ALU.mult)
                v.reduce_sum(t1b[:, tt:tt + 1],
                             x2ln_f[:, tt * 1024:(tt + 1) * 1024], axis=AX.X)
            vfl(v)
            v.tensor_scalar(t1a[:], t1a[:], 1.0 / H, None, op0=ALU.mult)
            v.tensor_scalar(t1b[:], t1b[:], 1.0 / H, None, op0=ALU.mult)
            vfl(v)
            v.tensor_tensor(t8a[:, 0:4], t1a[:], t1a[:], ALU.mult)
            vfl(v)
            v.scalar_tensor_tensor(out=stat_s[:], in0=t8a[:, 0:4], scalar=-1.0,
                                   in1=t1b[:], op0=ALU.mult, op1=ALU.add)
            v.drain().then_inc(s_vx, 1)
            v.wait_ge(s_ev, EVB + EV_RSTD1)
            v.scalar_tensor_tensor(out=stat_b[:], in0=t1a[:], scalar=-1.0,
                                   in1=stat_s[:], op0=ALU.mult, op1=ALU.mult)
            v.drain().then_inc(s_vx, 1)
            v.wait_ge(s_ev, EVB + EV_QK)
            v.nop(cycle_cnt=SDEL)
            for j in range(2):
                v.tensor_tensor(sqT[:, j * 512:(j + 1) * 512],
                                sqT[:, j * 512:(j + 1) * 512],
                                eluR[:, j * 512:(j + 1) * 512], ALU.add)
            for j in range(2):
                v.tensor_tensor(skT[:, j * 512:(j + 1) * 512],
                                skT[:, j * 512:(j + 1) * 512],
                                eluR[:, (2 + j) * 512:(3 + j) * 512], ALU.add)
            v.drain().then_inc(s_vx, 1)
            v.wait_ge(s_ev, EVB + EV_SCM)
            v.nop(cycle_cnt=SDEL)
            for j in range(4):
                v.tensor_tensor(scm[:, j * 512:(j + 1) * 512],
                                scm[:, j * 512:(j + 1) * 512],
                                trimask[:, j * 512:(j + 1) * 512], ALU.mult)
            v.drain().then_inc(s_vx, 1)
            for it in range(NT):
                v.wait_ge(s_ev, EVB + EV_NM0 + it)
                v.nop(cycle_cnt=SDEL)
                hN = it % 2
                nv = nev[:, hN * 257:hN * 257 + 257]
                mv = mev[:, hN * 257:hN * 257 + 257]
                v.tensor_scalar(t8a[:, 0:1], nv[:, 256:257], EPS_ATT, None,
                                op0=ALU.max)
                v.tensor_scalar(t8a[:, 2:3], mv[:, 256:257], EPS_ATT, None,
                                op0=ALU.max)
                vfl(v)
                v.reciprocal(t8a[:, 1:2], t8a[:, 0:1])
                v.reciprocal(t8a[:, 3:4], t8a[:, 2:3])
                vfl(v)
                v.tensor_tensor(t8a[:, 5:6], t8a[:, 3:4], gbc[:, 0:1], ALU.mult)
                v.tensor_tensor(t8a[:, 4:5], t8a[:, 1:2], gbc[:, 1:2], ALU.mult)
                vfl(v)
                v.tensor_scalar(x2ln_f[:, 0:256], mv[:, 0:256], t8a[:, 5:6],
                                None, op0=ALU.mult)
                v.scalar_tensor_tensor(out=comb[:, it * 256:(it + 1) * 256],
                                       in0=nv[:, 0:256], scalar=t8a[:, 4:5],
                                       in1=x2ln_f[:, 0:256],
                                       op0=ALU.mult, op1=ALU.add)
                v.drain().then_inc(s_vx, 1)
            v.wait_ge(s_ev, EVB + EV_SUP)
            v.nop(cycle_cnt=SDEL)
            if c == NCH - 1:
                v.memset(S_f[:], 0.0)
                v.memset(S_bf[:], 0.0)
            else:
                v.tensor_tensor(S_f[:], S_f[:], Sup[:], ALU.add)
                v.tensor_copy(out=S_bf[:], in_=S_f[:])
            v.drain().then_inc(s_vx, 1)
            v.wait_ge(s_ev, EVB + EV_AO0 + 3)
            v.nop(cycle_cnt=SDEL)
            for tt in range(NT):
                v.tensor_tensor(x2_t[:, tt * 1024:(tt + 1) * 1024],
                                x2_t[:, tt * 1024:(tt + 1) * 1024],
                                xh[:, tt * 1024:(tt + 1) * 1024], ALU.add)
            for tt in range(NT):
                xt_ = x2_t[:, tt * 1024:(tt + 1) * 1024]
                v.reduce_sum(t1a[:, tt:tt + 1], xt_, axis=AX.X)
                v.tensor_tensor(x2ln_f[:, tt * 1024:(tt + 1) * 1024], xt_, xt_,
                                ALU.mult)
                v.reduce_sum(t1b[:, tt:tt + 1],
                             x2ln_f[:, tt * 1024:(tt + 1) * 1024], axis=AX.X)
            vfl(v)
            v.tensor_scalar(t1a[:], t1a[:], 1.0 / H, None, op0=ALU.mult)
            v.tensor_scalar(t1b[:], t1b[:], 1.0 / H, None, op0=ALU.mult)
            vfl(v)
            v.tensor_tensor(t8a[:, 0:4], t1a[:], t1a[:], ALU.mult)
            vfl(v)
            v.scalar_tensor_tensor(out=stat_s[:], in0=t8a[:, 0:4], scalar=-1.0,
                                   in1=t1b[:], op0=ALU.mult, op1=ALU.add)
            v.drain().then_inc(s_vx, 1)
            v.wait_ge(s_ev, EVB + EV_RSTD2)
            v.scalar_tensor_tensor(out=stat_b[:], in0=t1a[:], scalar=-1.0,
                                   in1=stat_s[:], op0=ALU.mult, op1=ALU.mult)
            v.drain().then_inc(s_vx, 1)
            v.wait_ge(s_ev, EVB + EV_X2T0 + 7)
            if i > 0:
                v.wait_ge(s_sc, i * 32)
            v.nop(cycle_cnt=SDEL)
            for kt in range(KT):
                v.tensor_scalar(x2sel[:, kt * 512:(kt + 1) * 512],
                                x2lnT_f[:, kt * 512:(kt + 1) * 512],
                                ohbc[:, b:b + 1], None, op0=ALU.mult)
            for tt in range(NT):
                v.tensor_scalar(x2_t[:, tt * 1024:(tt + 1) * 1024],
                                x2_t[:, tt * 1024:(tt + 1) * 1024],
                                ohbc[:, b:b + 1], None, op0=ALU.mult)
            v.drain().then_inc(s_vx, 1)
            v.wait_ge(s_ev, EVB + EV_LGTM)
            v.nop(cycle_cnt=SDEL)
            for it in range(NT):
                v.reduce_max(t1a[:, it:it + 1], lg_tm[:, it * 8:(it + 1) * 8],
                             axis=AX.X)
            vfl(v)
            for it in range(NT):
                v.tensor_scalar(x2ln_f[:, it * 8:(it + 1) * 8],
                                lg_tm[:, it * 8:(it + 1) * 8],
                                t1a[:, it:it + 1], None, op0=ALU.is_ge)
            vfl(v)
            for it in range(NT):
                v.scalar_tensor_tensor(out=x2ln_f[:, 64 + it * 8:64 + (it + 1) * 8],
                                       in0=x2ln_f[:, it * 8:(it + 1) * 8],
                                       scalar=-1000.0, in1=iota8[:, 8:16],
                                       op0=ALU.mult, op1=ALU.add)
            vfl(v)
            for it in range(NT):
                v.tensor_reduce(E_all[:, col0 + it:col0 + it + 1],
                                x2ln_f[:, 64 + it * 8:64 + (it + 1) * 8],
                                axis=AX.X, op=ALU.min)
            v.tensor_scalar(t1b[:], t1a[:], -1.0, None, op0=ALU.mult)
            v.drain().then_inc(s_vx, 1)
            v.wait_ge(s_ev, EVB + EV_TEXP)
            v.nop(cycle_cnt=SDEL)
            v.reciprocal(P_all[:, col0:col0 + 4], tden[:, 0:4])
            v.drain().then_inc(s_vx, 1)

    @block.tensor
    def _(t):
        t.wait_ge(s_set, 1)
        gidx = 0
        for srcoff, TPx in ((0, TPP1), (0, TPP2), (1024, TPP1), (1024, TPP2)):
            srcbuf = cvA if TPx is TPP1 else cvB
            if gidx >= 2:
                t.wait_ge(s_ev, gidx - 1)
            for kt in range(KT):
                t.transpose(TPx[:, kt * 128:(kt + 1) * 128],
                            srcbuf[:, srcoff + kt * 128:srcoff + (kt + 1) * 128],
                            id_bf[:])
            t.drain().then_inc(s_tp, 1)
            gidx += 1
        t.wait_ge(s_set, 2)
        for srcoff, TPx in ((0, TPP1), (0, TPP2), (1024, TPP1), (1024, TPP2)):
            srcbuf = cvA if TPx is TPP1 else cvB
            t.wait_ge(s_ev, gidx - 1)
            for kt in range(KT):
                t.transpose(TPx[:, kt * 128:(kt + 1) * 128],
                            srcbuf[:, srcoff + kt * 128:srcoff + (kt + 1) * 128],
                            id_bf[:])
            t.drain().then_inc(s_tp, 1)
            gidx += 1
        t.wait_ge(s_set, 3)
        t.wait_ge(s_pre, 28 * 16)
        for kt in range(KT):
            t.transpose(PQ[:, kt * 16:kt * 16 + 8],
                        stage[0:8, 1024 + kt * 128:1024 + (kt + 1) * 128],
                        id_f32[0:8, 0:8])
        t.drain().then_inc(s_tp, 1)
        gidx += 1
        for ft in range(16):
            t.transpose(PS[:, ft * 16:ft * 16 + 8],
                        stage[0:8, 2048 + ft * 128:2048 + (ft + 1) * 128],
                        id_f32[0:8, 0:8])
        t.drain().then_inc(s_tp, 1)
        gidx += 1
        t.wait_ge(s_pre, 29 * 16)
        for ft in range(16):
            t.transpose(PS[:, (16 + ft) * 16:(16 + ft) * 16 + 8],
                        stage[0:8, 2048 + ft * 128:2048 + (ft + 1) * 128],
                        id_f32[0:8, 0:8])
        t.drain().then_inc(s_tp, 1)
        gidx += 1
        t.wait_ge(s_set, 4)
        t.matmul(PN[:, 0:8], ones_row[0:1, :], stage[0:1, 772:780],
                 start=True, stop=True)
        t.matmul(PN[:, 16:18], ones_row[0:1, :], stage[0:1, 782:784],
                 start=True, stop=True)
        t.drain().then_inc(s_tp, 1)
        gidx += 1
        assert gidx == SETUP_TP
        t.drain().then_inc(s_set, 1)
        t.wait_ge(s_set, SET_TOTAL)
        for _ in range(64):
            t.matmul(PA[:, 0:128], id_bf[:], id_bf[:], start=True, stop=True)
        for i in range(NCHT):
            b, c = divmod(i, NCH)
            half = i % 2
            TPB = SETUP_TP + i * TP_A
            EVB = SETUP_EV + i * EV_A
            EVBp = SETUP_EV + (i - 1) * EV_A
            VXB = i * VX_A
            LNB = i * LN_A
            t.wait_ge(s_ln, LNB + 1)
            if i > 0:
                t.wait_ge(s_ev, EVBp + EV_CMBT)
            for gq in range(4):
                TPx = TPP1 if gq % 2 == 0 else TPP2
                if gq >= 2:
                    t.wait_ge(s_ev, EVB + EV_XLNT0 + gq - 2)
                for ktp in range(2):
                    kt = gq * 2 + ktp
                    for tt in range(NT):
                        t.transpose(TPx[:, ktp * 512 + tt * 128:ktp * 512 + (tt + 1) * 128],
                                    xln_tm[:, tt * 1024 + kt * 128:tt * 1024 + (kt + 1) * 128],
                                    id_bf[:])
                t.drain().then_inc(s_tp, 1)
            t.wait_ge(s_ev, EVB + EV_XLNT0 + 3)
            if i > 0:
                t.wait_ge(s_ev, EVBp + EV_LGTM)
            for dh, P in ((0, PQ), (1, PS)):
                for kt in range(KT):
                    t.matmul(P[:], wqT[:, dh * 1024 + kt * 128:dh * 1024 + (kt + 1) * 128],
                             xlnT[:, kt * 512:(kt + 1) * 512],
                             start=(kt == 0), stop=(kt == KT - 1))
                    t.nop(cycle_cnt=PACE2)
            for dh, P in ((0, PN), (1, PM)):
                for kt in range(KT):
                    t.matmul(P[:], wkT[:, dh * 1024 + kt * 128:dh * 1024 + (kt + 1) * 128],
                             xlnT[:, kt * 512:(kt + 1) * 512],
                             start=(kt == 0), stop=(kt == KT - 1))
                    t.nop(cycle_cnt=PACE2)
            t.drain().then_inc(s_tp, 1)
            t.wait_ge(s_ev, EVB + EV_QK)
            for jt, P in enumerate((PQ, PS, PN, PM)):
                for kt in range(KT):
                    t.matmul(P[:, 0:256],
                             xlnT[:, kt * 512 + jt * 128:kt * 512 + (jt + 1) * 128],
                             wvT[:].rearrange("p (dh kt2 c) -> p kt2 dh c",
                                              dh=2, c=128)[:, kt, :, :],
                             start=(kt == 0), stop=(kt == KT - 1))
                    t.nop(cycle_cnt=PACE1)
            t.drain().then_inc(s_tp, 1)
            t.wait_ge(s_vx, VXB + VX_SIG)
            t.wait_ge(s_ev, EVB + EV_V)
            t.nop(cycle_cnt=SDEL)
            for jt, P in enumerate((PQ, PS, PN, PM)):
                for dh in range(2):
                    t.matmul(P[:],
                             skT[:, dh * 512 + jt * 128:dh * 512 + (jt + 1) * 128],
                             sqT[:, dh * 512:(dh + 1) * 512],
                             start=(dh == 0), stop=(dh == 1))
                    t.nop(cycle_cnt=PACE2)
            t.drain().then_inc(s_tp, 1)
            for jt in range(NT):
                for dh in range(2):
                    t.transpose(TPP1[:, jt * 256 + dh * 128:jt * 256 + (dh + 1) * 128],
                                skT[:, dh * 512 + jt * 128:dh * 512 + (jt + 1) * 128],
                                id_bf[:])
            t.drain().then_inc(s_tp, 1)
            t.wait_ge(s_vx, VXB + VX_MSK)
            if i > 0:
                t.wait_ge(s_vx, (i - 1) * VX_A + VX_SREF)
            t.wait_ge(s_ev, EVB + EV_SCM)
            for it in range(NT):
                if it >= 2:
                    t.wait_ge(s_ev, EVB + EV_NM0 + it - 2)
                Pn = PQ if it % 2 == 0 else PS
                Pm = PN if it % 2 == 0 else PM
                for jt in range(NT):
                    t.matmul(Pn[:, 0:257],
                             scm[:, jt * 512 + it * 128:jt * 512 + (it + 1) * 128],
                             v_aug[:, jt * 257:(jt + 1) * 257],
                             start=(jt == 0), stop=False)
                    t.nop(cycle_cnt=PACE1)
                for dh in range(2):
                    t.matmul(Pn[:, 0:257],
                             sqT[:, dh * 512 + it * 128:dh * 512 + (it + 1) * 128],
                             S_bf[:, dh * 257:(dh + 1) * 257],
                             start=False, stop=(dh == 1))
                    t.nop(cycle_cnt=PACE1)
                for dh in range(2):
                    t.matmul(Pm[:, 0:257],
                             sqT[:, dh * 512 + it * 128:dh * 512 + (it + 1) * 128],
                             maug[:, dh * 257:(dh + 1) * 257],
                             start=(dh == 0), stop=(dh == 1))
                    t.nop(cycle_cnt=PACE1)
                t.drain().then_inc(s_tp, 1)
            t.wait_ge(s_vx, VXB + VX_CMB0 + 3)
            for it in range(NT):
                for eh in range(2):
                    t.transpose(TPP2[:, it * 256 + eh * 128:it * 256 + (eh + 1) * 128],
                                comb[:, it * 256 + eh * 128:it * 256 + (eh + 1) * 128],
                                id_bf[:])
            t.drain().then_inc(s_tp, 1)
            t.wait_ge(s_ev, EVB + EV_SKTM)
            t.wait_ge(s_ev, EVB + EV_NM0 + 3)
            for dh, P in ((0, PQ), (1, PS)):
                for jt in range(NT):
                    t.matmul(P[:, 0:257],
                             sk_tm[:, jt * 256 + dh * 128:jt * 256 + (dh + 1) * 128],
                             v_aug[:, jt * 257:(jt + 1) * 257],
                             start=(jt == 0), stop=(jt == NT - 1))
                    t.nop(cycle_cnt=PACE1)
            t.drain().then_inc(s_tp, 1)
            t.wait_ge(s_ev, EVB + EV_CMBT)
            for it in range(NT):
                if it >= 1:
                    t.wait_ge(s_ev, EVB + EV_AO0 + it - 1)
                for hhalf in range(2):
                    for eh in range(2):
                        t.matmul(PA[:, hhalf * 512:(hhalf + 1) * 512],
                                 combT[:, it * 256 + eh * 128:it * 256 + (eh + 1) * 128],
                                 woeh[:, eh * 1024 + hhalf * 512:eh * 1024 + (hhalf + 1) * 512],
                                 start=(eh == 0), stop=(eh == 1))
                        t.nop(cycle_cnt=PACE2)
                t.drain().then_inc(s_tp, 1)
            t.wait_ge(s_ln, LNB + 2)
            t.wait_ge(s_ev, EVB + EV_SUP)
            for gq in range(KT):
                P = PQ if gq % 2 == 0 else PS
                if gq >= 2:
                    t.wait_ge(s_ev, EVB + EV_X2T0 + gq - 2)
                for tt in range(NT):
                    t.transpose(P[:, tt * 128:(tt + 1) * 128],
                                x2ln_f[:, tt * 1024 + gq * 128:tt * 1024 + (gq + 1) * 128],
                                id_f32[:])
                t.drain().then_inc(s_tp, 1)
            t.wait_ge(s_ev, EVB + EV_X2T0 + 7)
            for kt in range(KT):
                t.matmul(PN[0:8, 0:512], rwT[:, kt * 8:(kt + 1) * 8],
                         x2lnT_f[:, kt * 512:(kt + 1) * 512],
                         start=(kt == 0), stop=(kt == KT - 1))
                t.nop(cycle_cnt=PACEF)
            t.drain().then_inc(s_tp, 1)
            t.wait_ge(s_ev, EVB + EV_LGT)
            t.nop(cycle_cnt=SDEL)
            for it in range(NT):
                t.transpose(PM[:, it * 16:it * 16 + 8],
                            lgT_sb[0:8, it * 128:(it + 1) * 128],
                            id_f32[0:8, 0:8])
            t.drain().then_inc(s_tp, 1)

    _ctx2.close()
    _ctx3 = contextlib.ExitStack()
    Pe = _ctx3.enter_context(nc.sbuf_tensor("Pe", [128, 2048], F32))
    geb = _ctx3.enter_context(nc.sbuf_tensor("geb", [128, 2048], BF16))
    cnt_sb = _ctx3.enter_context(nc.sbuf_tensor("cnt_sb", [1, 2048], F32))
    th = _ctx3.enter_context(nc.sbuf_tensor("th", [128, 32], F32))
    lohi = _ctx3.enter_context(nc.sbuf_tensor("lohi", [1, 32], F32))
    cnts = _ctx3.enter_context(nc.sbuf_tensor("cnts", [1, 16], F32))
    neg1 = _ctx3.enter_context(nc.sbuf_tensor("neg1", [128, 256], F32))
    eqtmp = _ctx3.enter_context(nc.sbuf_tensor("eqtmp", [128, 256], F32))
    keep = _ctx3.enter_context(nc.sbuf_tensor("keep", [128, 256], F32))
    fE = _ctx3.enter_context(nc.sbuf_tensor("fE", [128, 256], F32))
    own_e = _ctx3.enter_context(nc.sbuf_tensor("own_e", [128, 32], F32))
    masksC = _ctx3.enter_context(nc.sbuf_tensor("masksC", [128, 256], F32))
    eqb = _ctx3.enter_context(nc.sbuf_tensor("eqb", [128, 256], F32))
    onehotT = _ctx3.enter_context(nc.sbuf_tensor("onehotT", [8, 4096], F32))
    w1s = _ctx3.enter_context(nc.sbuf_tensor("w1s", [128, 8192], BF16))
    x2u = _ctx3.enter_context(nc.sbuf_tensor("x2u", [128, 2 * 3072], BF16))
    w2s = _ctx3.enter_context(nc.sbuf_tensor("w2s", [128, 8192], BF16))
    htr = _ctx3.enter_context(nc.sbuf_tensor("htr", [128, 6 * 384], BF16))
    ystage = _ctx3.enter_context(nc.sbuf_tensor("ystage", [128, 3072], F32))
    acc = _ctx3.enter_context(nc.sbuf_tensor("acc", [128, 3072], F32))
    xres = _ctx3.enter_context(nc.sbuf_tensor("xres", [128, 3072], F32))
    fpa = _ctx3.enter_context(nc.psum_tensor("fpa", [128, 512], F32))
    fpb = _ctx3.enter_context(nc.psum_tensor("fpb", [128, 512], F32))
    fpy = _ctx3.enter_context(nc.psum_tensor("fpy", [128, 3072], F32))
    SRCH_STEPS = []
    if not DEV_SKIP_SEARCH:
        SRCH_STEPS.append("v_init")
    if not DEV_SKIP_SEARCH:
        for k in range(SEARCH_ITERS):
            SRCH_STEPS += ["v_mid", "t_bc", "s_th", "v_ge", "t_cnt", "s_cnt", "v_upd"]
        SRCH_STEPS += ["v_midhi", "t_bc", "s_th", "v_fin"]
        for grp in range(8):
            SRCH_STEPS += ["t_eqT%d" % grp, "s_eqv%d" % grp]
    SRCH_TOTAL = len(SRCH_STEPS)

    @block.sync
    def _(sync):
        sync.wait_ge(s_cv, CV_TOTAL)
        sync.wait_ge(s_sc, NCHT * 32)
        sync.wait_ge(s_vx, NCHT * VX_A)
        sync.wait_ge(s_ev, SETUP_EV + NCHT * EV_A)
        for u in range(NU):
            ci, e = divmod(u, E)
            tk0u, nttu = FFN_CHUNKS[ci]
            if u >= 2:
                sync.wait_ge(s_l1, (u - 2) * NFT + NFT + 2)
            sync.dma_start(
                out=x2u[:, (u % 2) * 3072:(u % 2) * 3072 + KT * nttu * 128]
                    .rearrange("r (kt t) -> r kt t", kt=KT),
                in_=x2ln_dram[:, :, tk0u:tk0u + nttu * 128]
                    .rearrange("kt p t -> p kt t"),
            ).then_inc(s_fw, 16)
            for p in range(8):
                G = u * 8 + p
                if G >= 2:
                    sync.wait_ge(s_l1, 4 * G - 2)
                sync.dma_start(
                    out=w1s[:, (G % 2) * 4096:(G % 2 + 1) * 4096]
                        .rearrange("r (ft kt c) -> r ft kt c", ft=4, kt=KT),
                    in_=W1bf[e, p * 4:(p + 1) * 4, :, :, :]
                        .rearrange("ft kt r c -> r ft kt c"),
                ).then_inc(s_fw, 16)
                if G >= 2:
                    sync.wait_ge(s_l1, 4 * G)
                sync.dma_start(
                    out=w2s[:, (G % 2) * 4096:(G % 2 + 1) * 4096]
                        .rearrange("r (ft m) -> r ft m", ft=4),
                    in_=W2bf[e, p * 4:(p + 1) * 4, :, :]
                        .rearrange("ft r m -> r ft m"),
                ).then_inc(s_fw, 16)

    @block.gpsimd
    def _(g):
        g.wait_ge(s_vx, NCHT * VX_A)
        g.wait_ge(s_ev, SETUP_EV + NCHT * EV_A)
        g.wait_ge(s_cv, CV_TOTAL)
        for ci in range(NCI):
            tk0, ntt = FFN_CHUNKS[ci]
            if ci >= 1:
                g.wait_ge(s_ac, (ci - 1) * 9 + 1)
            g.dma_start(
                out=xres[:, 0:ntt * 1024].rearrange("p (n h) -> p n h", n=ntt),
                in_=x2_scr[tk0:tk0 + ntt * 128, :]
                    .rearrange("(n p) h -> p n h", p=128),
            ).then_inc(s_fx, 16)
            g.wait_ge(s_ac, ci * 9 + 9)
            g.nop(cycle_cnt=2048)
            g.dma_start(
                out=out[tk0:tk0 + ntt * 128, :]
                    .rearrange("(n p) h -> p n h", p=128),
                in_=acc[:, 0:ntt * 1024].rearrange("p (n h) -> p n h", n=ntt),
            ).then_inc(s_st, 16)
        g.wait_ge(s_st, NCI * 16)

    @block.scalar
    def _(s):
        s.wait_ge(s_cv, CV_TOTAL)
        s.wait_ge(s_vx, NCHT * VX_A)
        s.wait_ge(s_sc, NCHT * 32)
        for k, step in enumerate(SRCH_STEPS):
            if step == "s_th":
                s.wait_ge(s_srch, k)
                s.nop(cycle_cnt=CDEL)
                s.copy(out=th[:, 0:8], in_=fpa[:, 0:8])
                s.drain().then_inc(s_srch, 1)
            elif step == "s_cnt":
                s.wait_ge(s_srch, k)
                s.nop(cycle_cnt=CDEL)
                s.copy(out=cnt_sb[:], in_=fpy[0:1, 0:2048])
                s.drain().then_inc(s_srch, 1)
            elif step.startswith("s_eqv"):
                grp = int(step[5:])
                s.wait_ge(s_srch, k)
                s.nop(cycle_cnt=CDEL)
                s.copy(out=onehotT[0:8, grp * 512:(grp + 1) * 512],
                       in_=(fpa if grp % 2 == 0 else fpb)[0:8, 0:512])
                s.drain().then_inc(s_srch, 1)
        for u in range(NU):
            ci, e = divmod(u, E)
            tk0, ntt = FFN_CHUNKS[ci]
            if e == 0:
                s.wait_ge(s_yd, ci * 9 + 1)
                if ci > 0:
                    s.wait_ge(s_ac, ci * 9)
                s.nop(cycle_cnt=CDEL)
                for tt in range(ntt):
                    s.copy(out=ystage[:, tt * 1024:(tt + 1) * 1024],
                           in_=fpy[:, tt * 1024:(tt + 1) * 1024])
                s.drain().then_inc(s_yc, 1)
            for f in range(NFT):
                fg = u * NFT + f
                s.wait_ge(s_l1, fg + 1)
                s.nop(cycle_cnt=SDEL)
                s.activation(out=htr[:, (fg % 6) * 384:(fg % 6) * 384 + ntt * 128],
                             in_=(fpa if fg % 2 == 0 else fpb)[:, 0:ntt * 128],
                             func=AF.Relu, bias=b1s[:, e * 32 + f:e * 32 + f + 1])
                s.nop(cycle_cnt=512)
                s.drain().then_inc(s_rl, 1)
            s.wait_ge(s_yd, ci * 9 + 1 + e + 1)
            s.wait_ge(s_ac, ci * 9 + 1 + e)
            s.nop(cycle_cnt=CDEL)
            for tt in range(ntt):
                s.copy(out=ystage[:, tt * 1024:(tt + 1) * 1024],
                       in_=fpy[:, tt * 1024:(tt + 1) * 1024])
            s.drain().then_inc(s_yc, 1)

    @block.vector
    def _(v):
        v.wait_ge(s_cv, CV_TOTAL)
        v.wait_ge(s_ev, SETUP_EV + NCHT * EV_A)
        v.wait_ge(s_sc, NCHT * 32)
        for k, step in enumerate(SRCH_STEPS):
            if not step.startswith("v"):
                continue
            v.wait_ge(s_srch, k)
            v.nop(cycle_cnt=SDEL)
            if step == "v_init":
                v.memset(neg1[:], -1.0)
                v.memset(lohi[0:1, 0:8], 0.05)
                v.memset(lohi[0:1, 8:16], 1.01)
                v.memset(own_e[:], 0.0)
                v.tensor_scalar(keep[:], P_all[:], 1.0, None, op0=ALU.add)
                for e_ in range(E):
                    v.tensor_scalar(eqtmp[:], E_all[:], float(e_), None,
                                    op0=ALU.is_equal)
                    v.tensor_tensor(eqtmp[:], eqtmp[:], keep[:], ALU.mult)
                    v.tensor_scalar(Pe[:, e_ * 256:(e_ + 1) * 256], eqtmp[:], 1.0,
                                    None, op0=ALU.subtract)
            elif step == "v_mid":
                v.tensor_tensor(cnts[0:1, 8:16], lohi[0:1, 0:8],
                                lohi[0:1, 8:16], ALU.add)
                vfl(v)
                v.tensor_scalar(lohi[0:1, 16:24], cnts[0:1, 8:16], 0.5, None,
                                op0=ALU.mult)
            elif step == "v_ge":
                for e_ in range(E):
                    v.tensor_scalar(geb[:, e_ * 256:(e_ + 1) * 256],
                                    Pe[:, e_ * 256:(e_ + 1) * 256],
                                    th[:, e_:e_ + 1], None, op0=ALU.is_ge)
            elif step == "v_upd":
                for e_ in range(E):
                    v.reduce_sum(cnts[0:1, e_:e_ + 1],
                                 cnt_sb[0:1, e_ * 256:(e_ + 1) * 256], axis=AX.X)
                vfl(v)
                v.tensor_scalar(cnts[0:1, 8:16], cnts[0:1, 0:8], CAP, None,
                                op0=ALU.is_gt)
                vfl(v)
                v.scalar_tensor_tensor(out=th[0:1, 0:8], in0=lohi[0:1, 0:8],
                                       scalar=-1.0, in1=lohi[0:1, 16:24],
                                       op0=ALU.mult, op1=ALU.add)
                v.scalar_tensor_tensor(out=th[0:1, 8:16], in0=lohi[0:1, 8:16],
                                       scalar=-1.0, in1=lohi[0:1, 16:24],
                                       op0=ALU.mult, op1=ALU.add)
                v.tensor_scalar(cnts[0:1, 0:8], cnts[0:1, 8:16], -1.0, 1.0,
                                op0=ALU.mult, op1=ALU.add)
                vfl(v)
                v.tensor_tensor(th[0:1, 16:24], th[0:1, 0:8], cnts[0:1, 8:16],
                                ALU.mult)
                v.tensor_tensor(th[0:1, 24:32], th[0:1, 8:16], cnts[0:1, 0:8],
                                ALU.mult)
                vfl(v)
                v.tensor_tensor(lohi[0:1, 0:8], lohi[0:1, 0:8], th[0:1, 16:24],
                                ALU.add)
                v.tensor_tensor(lohi[0:1, 8:16], lohi[0:1, 8:16], th[0:1, 24:32],
                                ALU.add)
            elif step == "v_midhi":
                v.tensor_copy(out=lohi[0:1, 16:24], in_=lohi[0:1, 8:16])
            elif step == "v_fin":
                for e_ in range(E):
                    if e_ == 0:
                        v.tensor_scalar(keep[:], Pe[:, 0:256], th[:, 0:1],
                                        None, op0=ALU.is_ge)
                    else:
                        v.tensor_scalar(eqtmp[:], Pe[:, e_ * 256:(e_ + 1) * 256],
                                        th[:, e_:e_ + 1], None, op0=ALU.is_ge)
                        v.tensor_tensor(keep[:], keep[:], eqtmp[:], ALU.add)
                v.tensor_tensor(fE[:], E_all[:], keep[:], ALU.mult)
                for b_ in range(B):
                    v.scalar_tensor_tensor(out=own_e[:],
                                           in0=fE[:, b_ * 32:(b_ + 1) * 32],
                                           scalar=ohbc[:, b_:b_ + 1],
                                           in1=own_e[:], op0=ALU.mult,
                                           op1=ALU.add)
                    vfl(v)
                for e_ in range(E):
                    v.tensor_scalar(masksC[:, e_ * 32:(e_ + 1) * 32], own_e[:],
                                    float(e_), None, op0=ALU.is_equal)
                for tg in range(32):
                    v.tensor_scalar(eqb[:, tg * 8:(tg + 1) * 8], iota8[:, 0:8],
                                    own_e[:, tg:tg + 1], None, op0=ALU.is_equal)
            v.drain().then_inc(s_srch, 1)
        for ci in range(NCI):
            tk0, ntt = FFN_CHUNKS[ci]
            v.wait_ge(s_yc, ci * 9 + 1)
            v.wait_ge(s_fx, (ci + 1) * 16)
            if ci > 0:
                v.wait_ge(s_st, ci * 16)
            v.nop(cycle_cnt=SDEL)
            for tt in range(ntt):
                v.tensor_tensor(acc[:, tt * 1024:(tt + 1) * 1024],
                                ystage[:, tt * 1024:(tt + 1) * 1024],
                                xres[:, tt * 1024:(tt + 1) * 1024], ALU.add)
            v.drain().then_inc(s_ac, 1)
            for e in range(E):
                v.wait_ge(s_yc, ci * 9 + 2 + e)
                v.nop(cycle_cnt=SDEL)
                for tt in range(ntt):
                    v.scalar_tensor_tensor(
                        out=acc[:, tt * 1024:(tt + 1) * 1024],
                        in0=ystage[:, tt * 1024:(tt + 1) * 1024],
                        scalar=masksC[:, e * 32 + tk0 // 128 + tt:
                                      e * 32 + tk0 // 128 + tt + 1],
                        in1=acc[:, tt * 1024:(tt + 1) * 1024],
                        op0=ALU.mult, op1=ALU.add)
                v.drain().then_inc(s_ac, 1)

    @block.tensor
    def _(t):
        t.wait_ge(s_ev, SETUP_EV + NCHT * EV_A)
        t.wait_ge(s_vx, NCHT * VX_A)
        t.wait_ge(s_cv, CV_TOTAL)
        t.wait_ge(s_sc, NCHT * 32)
        for k, step in enumerate(SRCH_STEPS):
            if not step.startswith("t"):
                continue
            t.wait_ge(s_srch, k)
            if step == "t_bc":
                t.matmul(fpa[:, 0:8], ones_row[0:1, :], lohi[0:1, 16:24],
                         start=True, stop=True)
            elif step == "t_cnt":
                for e_ in range(E):
                    t.matmul(fpy[0:1, e_ * 256:(e_ + 1) * 256],
                             ones_col_bf[:], geb[:, e_ * 256:(e_ + 1) * 256],
                             start=True, stop=True)
                    t.nop(cycle_cnt=PACE1)
            elif step.startswith("t_eqT"):
                grp = int(step[5:])
                P = fpa if grp % 2 == 0 else fpb
                for tgl in range(4):
                    tg = grp * 4 + tgl
                    t.transpose(P[0:8, tgl * 128:(tgl + 1) * 128],
                                eqb[:, tg * 8:(tg + 1) * 8], id_f32[:])
            t.drain().then_inc(s_srch, 1)
        t.wait_ge(s_srch, SRCH_TOTAL)
        for _ in range(64):
            t.matmul(fpa[:, 0:128], id_bf[:], id_bf[:], start=True, stop=True)
        def emit_l2(u, ci, e, tk0, ntt, fl):
            fgl = u * NFT + fl
            if fl % 4 == 0:
                t.wait_ge(s_fw, (u * 16 + 2 * (fl // 4) + 2) * 16)
            if fl == 0:
                t.wait_ge(s_yc, ci * 9 + 1 + e)
            t.wait_ge(s_rl, min(u * NFT + fl + 2, u * NFT + NFT))
            slot = fgl % 6
            pw = (fl // 4) % 2
            for tt in range(ntt):
                for half in range(2):
                    t.matmul(fpy[:, tt * 1024 + half * 512:tt * 1024 + (half + 1) * 512],
                             htr[:, slot * 384 + tt * 128:slot * 384 + (tt + 1) * 128],
                             w2s[:, pw * 4096 + (fl % 4) * 1024 + half * 512:
                                 pw * 4096 + (fl % 4) * 1024 + (half + 1) * 512],
                             start=(fl == 0), stop=(fl == NFT - 1))
                    t.nop(cycle_cnt=PACE2)
        for u in range(NU):
            ci, e = divmod(u, E)
            tk0, ntt = FFN_CHUNKS[ci]
            if e == 0:
                if ci > 0:
                    t.wait_ge(s_yc, ci * 9)
                for tt in range(ntt):
                    tg = tk0 // 128 + tt
                    for half in range(2):
                        t.matmul(fpy[:, tt * 1024 + half * 512:tt * 1024 + (half + 1) * 512],
                                 onehotT[0:8, tg * 128:(tg + 1) * 128],
                                 b2s[0:8, half * 512:(half + 1) * 512],
                                 start=True, stop=True)
                        t.nop(cycle_cnt=PACEF)
                t.drain().then_inc(s_yd, 1)
            for f in range(NFT):
                fg = u * NFT + f
                if f == 0:
                    t.wait_ge(s_fw, (u * 17 + 1) * 16)
                if f % 4 == 0:
                    t.wait_ge(s_fw, (u * 17 + 2 * (f // 4) + 2) * 16)
                bank = fpa if fg % 2 == 0 else fpb
                if fg >= 2:
                    t.wait_ge(s_rl, fg - 1)
                pw = (f // 4) % 2
                for kt in range(KT):
                    t.matmul(bank[:, 0:ntt * 128],
                             w1s[:, pw * 4096 + (f % 4) * 1024 + kt * 128:
                                 pw * 4096 + (f % 4) * 1024 + (kt + 1) * 128],
                             x2u[:, (u % 2) * 3072 + kt * ntt * 128:
                                 (u % 2) * 3072 + (kt + 1) * ntt * 128],
                             start=(kt == 0), stop=(kt == KT - 1))
                    t.nop(cycle_cnt=PACE1)
                t.drain().then_inc(s_l1, 1)
                if f >= 2:
                    emit_l2(u, ci, e, tk0, ntt, f - 2)
            emit_l2(u, ci, e, tk0, ntt, NFT - 2)
            emit_l2(u, ci, e, tk0, ntt, NFT - 1)
            t.drain().then_inc(s_yd, 1)
    _ctx3.close()
    _ctx1.close()
    return nc


_CONSTS = None


def _host_consts():
    global _CONSTS
    if _CONSTS is None:
        identf = np.eye(128, dtype=np.float32)
        identb = np.eye(128, dtype=np.float32).astype(ml_dtypes.bfloat16)
        tri = np.zeros((128, 4 * 512), dtype=np.float32)
        for jt in range(4):
            p = np.arange(128)[:, None] + jt * 128
            i = np.arange(512)[None, :]
            tri[:, jt * 512:(jt + 1) * 512] = (p <= i).astype(np.float32)
        trimask = tri.astype(ml_dtypes.bfloat16)
        iotac = np.zeros((128, 16), dtype=np.float32)
        iotac[:, 0:8] = np.arange(8)[None, :]
        iotac[:, 8:16] = np.arange(8)[None, :] + 1000.0
        _CONSTS = (identf, identb, trimask, iotac)
    return _CONSTS


_NC = None


def _get_nc():
    global _NC
    if _NC is None:
        _NC = build_nc()
    return _NC


def kernel(L0_attn_hidden_states, L0_attn_w_q, L0_attn_w_k, L0_attn_w_v, L0_attn_w_o,
           L0_attn_gate, L0_attn_memory, L0_attn_memory_norm, L0_ffn_router_w,
           L0_ffn_W1, L0_ffn_b1, L0_ffn_W2, L0_ffn_b2):
    identf, identb, trimask, iotac = _host_consts()
    x = np.ascontiguousarray(np.asarray(L0_attn_hidden_states, dtype=np.float32))
    base = {
        "x_all": x,
        "w_q": np.ascontiguousarray(np.asarray(L0_attn_w_q, dtype=np.float32)),
        "w_k": np.ascontiguousarray(np.asarray(L0_attn_w_k, dtype=np.float32)),
        "w_v": np.ascontiguousarray(np.asarray(L0_attn_w_v, dtype=np.float32)),
        "w_o": np.ascontiguousarray(np.asarray(L0_attn_w_o, dtype=np.float32)),
        "gate": np.asarray(L0_attn_gate, dtype=np.float32).reshape(1, 1),
        "memory": np.ascontiguousarray(np.asarray(L0_attn_memory, dtype=np.float32)),
        "memnorm": np.asarray(L0_attn_memory_norm, dtype=np.float32).reshape(1, DM),
        "router": np.ascontiguousarray(np.asarray(L0_ffn_router_w, dtype=np.float32)),
        "W1": np.ascontiguousarray(np.asarray(L0_ffn_W1, dtype=np.float32)),
        "b1in": np.ascontiguousarray(np.asarray(L0_ffn_b1, dtype=np.float32)),
        "W2": np.ascontiguousarray(np.asarray(L0_ffn_W2, dtype=np.float32)),
        "b2in": np.ascontiguousarray(np.asarray(L0_ffn_b2, dtype=np.float32)),
        "identf": identf,
        "identb": identb,
        "trimask_in": trimask,
        "iotac": iotac,
    }
    in_maps = []
    for c in range(B):
        m = dict(base)
        oh = np.zeros((1, B), dtype=np.float32)
        oh[0, c] = 1.0
        m["onehot"] = oh
        in_maps.append(m)
    nc = _get_nc()
    res = run_bass_kernel_spmd(nc, in_maps, core_ids=list(range(B)))
    return np.stack([res.results[c]["out"] for c in range(B)], axis=0)


# version 09 (final)
# speedup vs baseline: 1.1044x; 1.0694x over previous
import contextlib
import numpy as np
import ml_dtypes
import concourse.bass as bass
import concourse.mybir as mybir
from concourse.bass_utils import run_bass_kernel_spmd

F32 = mybir.dt.float32
BF16 = mybir.dt.bfloat16
AF = mybir.ActivationFunctionType
ALU = mybir.AluOpType
AX = mybir.AxisListType

B, S, H, DM = 8, 4096, 1024, 256
E, DFF = 8, 4096
CAP = 5120.0
EPS_ATT = 1e-6
EPS_LN = 1e-5
CHUNK = 512
NCH = S // CHUNK
NT = CHUNK // 128
KT = H // 128
NCHT = B * NCH
NFT = 32
FFN_CHUNKS = [(i * 384, 3) for i in range(10)] + [(3840, 2)]
NCI = len(FFN_CHUNKS)
NU = NCI * E

PACE1 = 144
PACE2 = 192
PACEF = 640
CDEL = 384
SDEL = 192
SEARCH_ITERS = 27
DEV_EXPOSE = False
DEV_SKIP_CONV = False
DEV_SKIP_SEARCH = False
_BASS_KW = {}

VX_LN1, VX_LNB1, VX_SIG, VX_MSK = 1, 2, 3, 4
VX_CMB0 = 5
VX_SREF, VX_X2LN, VX_LNB2, VX_OWN, VX_TOP1, VX_PROB = 9, 10, 11, 12, 13, 14
VX_A = 14
EV_RSTD1 = 1
EV_XLNT0 = 2
EV_QK, EV_V, EV_SCM, EV_SKTM = 6, 7, 8, 9
EV_NM0 = 10
EV_CMBT, EV_SUP = 14, 15
EV_AO0 = 16
EV_RSTD2 = 20
EV_X2T0 = 21
EV_LGT, EV_LGTM, EV_TEXP = 29, 30, 31
EV_A = 31
TP_XLN0 = 1
TP_QK, TP_V, TP_SC, TP_SKT = 5, 6, 7, 8
TP_NM0 = 9
TP_CMBT, TP_SUP = 13, 14
TP_AO0 = 15
TP_X2T0 = 19
TP_LG, TP_LGT = 27, 28
TP_A = 28
LN_A = 2
SETUP_TP = 12
SETUP_EV = 12
SET_TOTAL = 6
CV_TOTAL = (E * KT * 4 + E * 16 * 2) * 16


def build_nc():
    nc = bass.Bass("TRN2", target_bir_lowering=False, **_BASS_KW)
    x_all = nc.dram_tensor("x_all", [B, S, H], F32, kind="ExternalInput").ap()
    onehot = nc.dram_tensor("onehot", [1, B], F32, kind="ExternalInput").ap()
    w_q = nc.dram_tensor("w_q", [DM, H], F32, kind="ExternalInput").ap()
    w_k = nc.dram_tensor("w_k", [DM, H], F32, kind="ExternalInput").ap()
    w_v = nc.dram_tensor("w_v", [DM, H], F32, kind="ExternalInput").ap()
    w_o = nc.dram_tensor("w_o", [H, DM], F32, kind="ExternalInput").ap()
    gate = nc.dram_tensor("gate", [1, 1], F32, kind="ExternalInput").ap()
    memory = nc.dram_tensor("memory", [DM, DM], F32, kind="ExternalInput").ap()
    memnorm = nc.dram_tensor("memnorm", [1, DM], F32, kind="ExternalInput").ap()
    router = nc.dram_tensor("router", [E, H], F32, kind="ExternalInput").ap()
    W1 = nc.dram_tensor("W1", [E, H, DFF], F32, kind="ExternalInput").ap()
    b1in = nc.dram_tensor("b1in", [E, DFF], F32, kind="ExternalInput").ap()
    W2 = nc.dram_tensor("W2", [E, DFF, H], F32, kind="ExternalInput").ap()
    b2in = nc.dram_tensor("b2in", [E, H], F32, kind="ExternalInput").ap()
    identf = nc.dram_tensor("identf", [128, 128], F32, kind="ExternalInput").ap()
    identb = nc.dram_tensor("identb", [128, 128], BF16, kind="ExternalInput").ap()
    trimask_in = nc.dram_tensor("trimask_in", [128, 4 * 512], BF16, kind="ExternalInput").ap()
    iotac = nc.dram_tensor("iotac", [128, 16], F32, kind="ExternalInput").ap()
    out = nc.dram_tensor("out", [S, H], F32, kind="ExternalOutput").ap()
    x2_scr = nc.dram_tensor("x2_scr", [S, H], F32,
                            kind=("ExternalOutput" if DEV_EXPOSE else "Internal"))
    x2ln_dram32 = nc.dram_tensor("x2ln_dram", [KT, 128, S // 2], F32,
                                 kind=("ExternalOutput" if DEV_EXPOSE else "Internal"))
    x2ln_dram = x2ln_dram32.bitcast(BF16)
    W1bf = nc.dram_tensor("W1bf", [E, NFT, KT, 128, 128], BF16, kind="Internal")
    W2bf = nc.dram_tensor("W2bf", [E, NFT, 128, H], BF16, kind="Internal")

    _ctx1 = contextlib.ExitStack()
    id_f32 = _ctx1.enter_context(nc.sbuf_tensor("id_f32", [128, 128], F32))
    id_bf = _ctx1.enter_context(nc.sbuf_tensor("id_bf", [128, 128], BF16))
    iota8 = _ctx1.enter_context(nc.sbuf_tensor("iota8", [128, 16], F32))
    ones_row = _ctx1.enter_context(nc.sbuf_tensor("ones_row", [1, 128], F32))
    ones_col_bf = _ctx1.enter_context(nc.sbuf_tensor("ones_col_bf", [128, 1], BF16))
    b1s = _ctx1.enter_context(nc.sbuf_tensor("b1s", [128, 256], F32))
    b2s = _ctx1.enter_context(nc.sbuf_tensor("b2s", [8, H], F32))
    ohbc = _ctx1.enter_context(nc.sbuf_tensor("ohbc", [128, 8], F32))
    epsc = _ctx1.enter_context(nc.sbuf_tensor("epsc", [128, 1], F32))
    P_all = _ctx1.enter_context(nc.sbuf_tensor("P_all", [128, 256], F32))
    E_all = _ctx1.enter_context(nc.sbuf_tensor("E_all", [128, 256], F32))
    s_pre = _ctx1.enter_context(nc.semaphore("s_pre"))
    s_set = _ctx1.enter_context(nc.semaphore("s_set"))
    s_xld = _ctx1.enter_context(nc.semaphore("s_xld"))
    s_cv = _ctx1.enter_context(nc.semaphore("s_cv"))
    s_ln = _ctx1.enter_context(nc.semaphore("s_ln"))
    s_tp = _ctx1.enter_context(nc.semaphore("s_tp"))
    s_ev = _ctx1.enter_context(nc.semaphore("s_ev"))
    s_vx = _ctx1.enter_context(nc.semaphore("s_vx"))
    s_sc = _ctx1.enter_context(nc.semaphore("s_sc"))
    s_srch = _ctx1.enter_context(nc.semaphore("s_srch"))
    s_fw = _ctx1.enter_context(nc.semaphore("s_fw"))
    s_fx = _ctx1.enter_context(nc.semaphore("s_fx"))
    s_l1 = _ctx1.enter_context(nc.semaphore("s_l1"))
    s_rl = _ctx1.enter_context(nc.semaphore("s_rl"))
    s_yd = _ctx1.enter_context(nc.semaphore("s_yd"))
    s_yc = _ctx1.enter_context(nc.semaphore("s_yc"))
    s_ac = _ctx1.enter_context(nc.semaphore("s_ac"))
    s_st = _ctx1.enter_context(nc.semaphore("s_st"))
    s_flv = _ctx1.enter_context(nc.semaphore("s_flv"))
    s_fls = _ctx1.enter_context(nc.semaphore("s_fls"))
    _fv = [0]
    def vfl(v):
        _fv[0] += 1
        v.drain().then_inc(s_flv, 1)
        v.wait_ge(s_flv, _fv[0])
    _fs = [0]
    def sfl(s):
        _fs[0] += 1
        s.drain().then_inc(s_fls, 1)
        s.wait_ge(s_fls, _fs[0])
    block = _ctx1.enter_context(nc.Block())
    _ctx2 = contextlib.ExitStack()
    stage = _ctx2.enter_context(nc.sbuf_tensor("stage", [128, 4096], F32))
    gbc = _ctx2.enter_context(nc.sbuf_tensor("gbc", [128, 2], F32))
    rwT = _ctx2.enter_context(nc.sbuf_tensor("rwT", [128, 64], F32))
    maug = _ctx2.enter_context(nc.sbuf_tensor("maug", [128, 2 * 257], BF16))
    woeh = _ctx2.enter_context(nc.sbuf_tensor("woeh", [128, 2048], BF16))
    wvT = _ctx2.enter_context(nc.sbuf_tensor("wvT", [128, 2048], BF16))
    wkT = _ctx2.enter_context(nc.sbuf_tensor("wkT", [128, 2048], BF16))
    wqT = _ctx2.enter_context(nc.sbuf_tensor("wqT", [128, 2048], BF16))
    trimask = _ctx2.enter_context(nc.sbuf_tensor("trimask", [128, 4 * 512], BF16))
    cvA = _ctx2.enter_context(nc.sbuf_tensor("cvA", [128, 2048], BF16))
    cvB = _ctx2.enter_context(nc.sbuf_tensor("cvB", [128, 2048], BF16))
    xin = _ctx2.enter_context(nc.sbuf_tensor("xin", [128, 2 * 4096], F32))
    xln_tm = _ctx2.enter_context(nc.sbuf_tensor("xln_tm", [128, 4096], BF16))
    xlnT = _ctx2.enter_context(nc.sbuf_tensor("xlnT", [128, KT * 512], BF16))
    x2_t = _ctx2.enter_context(nc.sbuf_tensor("x2_t", [128, 4096], F32))
    x2ln_f = _ctx2.enter_context(nc.sbuf_tensor("x2ln_f", [128, 4096], F32))
    x2lnT_f = _ctx2.enter_context(nc.sbuf_tensor("x2lnT_f", [128, KT * 512], F32))
    sqT = _ctx2.enter_context(nc.sbuf_tensor("sqT", [128, 2 * 512], BF16))
    skT = _ctx2.enter_context(nc.sbuf_tensor("skT", [128, 2 * 512], BF16))
    sk_tm = _ctx2.enter_context(nc.sbuf_tensor("sk_tm", [128, 1024], BF16))
    v_aug = _ctx2.enter_context(nc.sbuf_tensor("v_aug", [128, 4 * 257], BF16))
    scm = _ctx2.enter_context(nc.sbuf_tensor("scm", [128, 4 * 512], BF16))
    eluR = _ctx2.enter_context(nc.sbuf_tensor("eluR", [128, 4 * 512], F32))
    x2sel = _ctx2.enter_context(nc.sbuf_tensor("x2sel", [128, KT * 512], BF16))
    nev = _ctx2.enter_context(nc.sbuf_tensor("nev", [128, 2 * 257], F32))
    mev = _ctx2.enter_context(nc.sbuf_tensor("mev", [128, 2 * 257], F32))
    tden = _ctx2.enter_context(nc.sbuf_tensor("tden", [128, 8], F32))
    comb = _ctx2.enter_context(nc.sbuf_tensor("comb", [128, 4 * 256], BF16))
    combT = _ctx2.enter_context(nc.sbuf_tensor("combT", [128, 1024], BF16))
    S_f = _ctx2.enter_context(nc.sbuf_tensor("S_f", [128, 2 * 257], F32))
    S_bf = _ctx2.enter_context(nc.sbuf_tensor("S_bf", [128, 2 * 257], BF16))
    Sup = _ctx2.enter_context(nc.sbuf_tensor("Sup", [128, 2 * 257], F32))
    lgT_sb = _ctx2.enter_context(nc.sbuf_tensor("lgT_sb", [8, 512], F32))
    lg_tm = _ctx2.enter_context(nc.sbuf_tensor("lg_tm", [128, 32], F32))
    stat_b = _ctx2.enter_context(nc.sbuf_tensor("stat_b", [128, 4], F32))
    stat_s = _ctx2.enter_context(nc.sbuf_tensor("stat_s", [128, 4], F32))
    t8a = _ctx2.enter_context(nc.sbuf_tensor("t8a", [128, 8], F32))
    sst = _ctx2.enter_context(nc.sbuf_tensor("sst", [128, 8], F32))
    vjunk = _ctx2.enter_context(nc.sbuf_tensor("vjunk", [128, 8], F32))
    t8b = _ctx2.enter_context(nc.sbuf_tensor("t8b", [128, 8], F32))
    t1a = _ctx2.enter_context(nc.sbuf_tensor("t1a", [128, 4], F32))
    t1b = _ctx2.enter_context(nc.sbuf_tensor("t1b", [128, 4], F32))
    TPP1 = _ctx2.enter_context(nc.psum_tensor("TPP1", [128, 1024], BF16))
    TPP2 = _ctx2.enter_context(nc.psum_tensor("TPP2", [128, 1024], BF16))
    PQ = _ctx2.enter_context(nc.psum_tensor("PQ", [128, 512], F32))
    PS = _ctx2.enter_context(nc.psum_tensor("PS", [128, 512], F32))
    PN = _ctx2.enter_context(nc.psum_tensor("PN", [128, 512], F32))
    PM = _ctx2.enter_context(nc.psum_tensor("PM", [128, 512], F32))
    PA = _ctx2.enter_context(nc.psum_tensor("PA", [128, 1024], F32))
    @block.sync
    def _(sync):
        sync.wait_ge(s_set, SET_TOTAL)
        for i in range(NCHT):
            b, c = divmod(i, NCH)
            half = i % 2
            if i >= 2:
                sync.wait_ge(s_vx, (i - 2) * VX_A + VX_X2LN)
            sync.dma_start(
                out=xin[:, half * 4096:(half + 1) * 4096]
                    .rearrange("p (n h) -> p n h", n=NT),
                in_=x_all[b, c * CHUNK:(c + 1) * CHUNK, :]
                    .rearrange("(n p) h -> p n h", p=128),
            ).then_inc(s_xld, 16)

    @block.gpsimd
    def _(g):
        pre = 0
        def gd(dst, src):
            nonlocal pre
            g.dma_start(out=dst, in_=src).then_inc(s_pre, 16)
            pre += 1
        gd(id_f32[:], identf[:])
        gd(id_bf[:], identb[:])
        gd(trimask[:], trimask_in[:])
        gd(iota8[:], iotac[:])
        gd(b2s[:], b2in[:, :])
        gd(stage[:, 0:1024], w_q[0:128, :])
        gd(stage[:, 1024:2048], w_q[128:256, :])
        gd(stage[:, 2048:3072], w_k[0:128, :])
        gd(stage[:, 3072:4096], w_k[128:256, :])
        g.wait_ge(s_set, 1)
        gd(stage[:, 0:1024], w_v[0:128, :])
        gd(stage[:, 1024:2048], w_v[128:256, :])
        for hb in range(KT):
            gd(stage[:, 2048 + hb * 256:2048 + (hb + 1) * 256],
               w_o[hb * 128:(hb + 1) * 128, :])
        g.wait_ge(s_set, 2)
        gd(stage[:, 0:256], memory[0:128, :])
        gd(stage[:, 256:512], memory[128:256, :])
        gd(stage[0:1, 512:768], memnorm[:, :])
        gd(stage[:, 768:769], memnorm[0, 0:128].rearrange("(p a) -> p a", a=1))
        gd(stage[:, 769:770], memnorm[0, 128:256].rearrange("(p a) -> p a", a=1))
        gd(stage[0:1, 770:771], gate[:, :])
        gd(stage[0:1, 772:780], onehot[:, :])
        gd(stage[0:8, 1024:2048], router[:, :])
        gd(stage[0:8, 2048:4096], b1in[:, 0:2048])
        g.wait_ge(s_tp, 10)
        gd(stage[0:8, 2048:4096], b1in[:, 2048:4096])
        g.wait_ge(s_set, 5)
        cvn = 0
        if DEV_SKIP_CONV:
            g.nop().then_inc(s_cv, CV_TOTAL)
        for e in range(E if not DEV_SKIP_CONV else 0):
            for hb in range(KT):
                for fh in range(2):
                    buf = cvA if cvn % 4 < 2 else cvB
                    g.dma_start(out=buf[:],
                                in_=W1[e, hb * 128:(hb + 1) * 128,
                                       fh * 2048:(fh + 1) * 2048]).then_inc(s_cv, 16)
                    cvn += 1
                    g.wait_ge(s_cv, cvn * 16)
                    g.dma_start(
                        out=W1bf[e, fh * 16:(fh + 1) * 16, hb, :, :]
                            .rearrange("ft r c -> r ft c"),
                        in_=buf[:].rearrange("r (ft c) -> r ft c", c=128),
                    ).then_inc(s_cv, 16)
                    cvn += 1
        for e in range(E if not DEV_SKIP_CONV else 0):
            for qb in range(16):
                buf = cvA if qb % 2 == 0 else cvB
                g.dma_start(
                    out=buf[:].rearrange("r (q m) -> r q m", q=2),
                    in_=W2[e, qb * 256:(qb + 1) * 256, :]
                        .rearrange("(q r) m -> r q m", r=128),
                ).then_inc(s_cv, 16)
                cvn += 1
                g.wait_ge(s_cv, cvn * 16)
                g.dma_start(
                    out=W2bf[e, qb * 2:(qb + 1) * 2, :, :].rearrange("q r m -> r q m"),
                    in_=buf[:].rearrange("r (q m) -> r q m", q=2),
                ).then_inc(s_cv, 16)
                cvn += 1

    @block.gpsimd
    def _(g):
        for i in range(NCHT):
            b, c = divmod(i, NCH)
            g.wait_ge(s_vx, i * VX_A + VX_OWN)
            g.nop(cycle_cnt=SDEL)
            g.dma_start(
                out=x2_scr[c * CHUNK:(c + 1) * CHUNK, :]
                    .rearrange("(n p) h -> p n h", p=128),
                in_=x2_t[:].rearrange("p (n h) -> p n h", n=NT),
                accum_op=(ALU.bypass if b == 0 else ALU.add),
            ).then_inc(s_sc, 16)
            g.dma_start(
                out=x2ln_dram[:, :, c * CHUNK:(c + 1) * CHUNK]
                    .rearrange("kt p t -> p kt t"),
                in_=x2sel[:].rearrange("p (kt t) -> p kt t", kt=KT),
                accum_op=(ALU.bypass if b == 0 else ALU.add),
            ).then_inc(s_sc, 16)

    if DEV_EXPOSE:
        _dumps = {}
        for _nm, _w in [
            ("d_xln_tm", 4096), ("d_xlnT", 4096), ("d_sqT", 1024), ("d_skT", 1024),
            ("d_sk_tm", 1024), ("d_v_aug", 1028), ("d_scm", 2048), ("d_comb", 1024),
            ("d_combT", 1024), ("d_x2_t", 4096), ("d_x2ln_f", 4096),
            ("d_x2lnT_f", 4096), ("d_eluR", 2048), ("d_Sf", 514),
            ("d_lgtm", 32), ("d_stats", 16), ("d_t8a", 16), ("d_nev", 514),
            ("d_mev", 514), ("d_gbc", 4), ("d_stg", 64),
        ]:
            _dumps[_nm] = nc.dram_tensor(_nm, [128, _w], F32, kind="ExternalOutput").ap()

        @block.gpsimd
        def _(g):
            g.wait_ge(s_vx, NCHT * VX_A)
            g.wait_ge(s_ev, SETUP_EV + NCHT * EV_A)
            g.nop(cycle_cnt=4096)
            _dc = [0]
            def dmp(nm, srcap, isbf, w):
                dst = _dumps[nm]
                g.dma_start(out=dst[:, 0:w], in_=srcap).then_inc(s_st, 16)
                _dc[0] += 1
            dmp("d_xln_tm", xln_tm[:], True, 4096)
            dmp("d_xlnT", xlnT[:], True, 4096)
            dmp("d_sqT", sqT[:], True, 1024)
            dmp("d_skT", skT[:], True, 1024)
            dmp("d_sk_tm", sk_tm[:], True, 1024)
            dmp("d_v_aug", v_aug[:], True, 1028)
            dmp("d_scm", scm[:], True, 2048)
            dmp("d_comb", comb[:], True, 1024)
            dmp("d_combT", combT[:], True, 1024)
            dmp("d_x2_t", x2_t[:], False, 4096)
            dmp("d_x2ln_f", x2ln_f[:], False, 4096)
            dmp("d_x2lnT_f", x2lnT_f[:], False, 4096)
            dmp("d_eluR", eluR[:], False, 2048)
            dmp("d_Sf", S_f[:], False, 514)
            dmp("d_lgtm", lg_tm[:], False, 32)
            dmp("d_stats", stat_b[:], False, 4)
            dmp("d_nev", nev[:], False, 514)
            dmp("d_mev", mev[:], False, 514)
            g.dma_start(out=_dumps["d_t8a"][:, 0:8], in_=t8a[:]).then_inc(s_st, 16)
            g.dma_start(out=_dumps["d_gbc"][:, 0:2], in_=gbc[:]).then_inc(s_st, 16)
            g.dma_start(out=_dumps["d_stg"][0:1, 0:64],
                        in_=stage[0:1, 744:808]).then_inc(s_st, 16)
            g.dma_start(out=_dumps["d_stats"][:, 4:8], in_=stat_s[:]).then_inc(s_st, 16)
            g.dma_start(out=_dumps["d_stats"][:, 8:12], in_=t1a[:]).then_inc(s_st, 16)
            g.dma_start(out=_dumps["d_stats"][:, 12:16], in_=t1b[:]).then_inc(s_st, 16)
            g.wait_ge(s_st, (_dc[0] + 6) * 16)

    @block.scalar
    def _(s):
        tp = [0]
        def sevac(dst_ap, src_ap, delay=CDEL):
            tp[0] += 1
            s.wait_ge(s_tp, tp[0])
            s.nop(cycle_cnt=delay)
            s.copy(out=dst_ap, in_=src_ap)
            s.drain().then_inc(s_ev, 1)
        s.wait_ge(s_pre, 9 * 16)
        s.copy(out=cvA[:, 0:1024], in_=stage[:, 0:1024])
        s.copy(out=cvB[:, 0:1024], in_=stage[:, 1024:2048])
        s.copy(out=cvA[:, 1024:2048], in_=stage[:, 2048:3072])
        s.copy(out=cvB[:, 1024:2048], in_=stage[:, 3072:4096])
        s.drain().then_inc(s_set, 1)
        sevac(wqT[:, 0:1024], TPP1[:, 0:1024])
        sevac(wqT[:, 1024:2048], TPP2[:, 0:1024])
        sevac(wkT[:, 0:1024], TPP1[:, 0:1024])
        sevac(wkT[:, 1024:2048], TPP2[:, 0:1024])
        s.wait_ge(s_pre, 19 * 16)
        s.copy(out=cvA[:, 0:1024], in_=stage[:, 0:1024])
        s.copy(out=cvB[:, 0:1024], in_=stage[:, 1024:2048])
        for hb in range(KT):
            s.copy(out=cvA[:, 1024 + hb * 128:1024 + (hb + 1) * 128],
                   in_=stage[:, 2048 + hb * 256:2048 + hb * 256 + 128])
            s.copy(out=cvB[:, 1024 + hb * 128:1024 + (hb + 1) * 128],
                   in_=stage[:, 2048 + hb * 256 + 128:2048 + (hb + 1) * 256])
        s.drain().then_inc(s_set, 1)
        sevac(wvT[:, 0:1024], TPP1[:, 0:1024])
        sevac(wvT[:, 1024:2048], TPP2[:, 0:1024])
        sevac(woeh[:, 0:1024], TPP1[:, 0:1024])
        sevac(woeh[:, 1024:2048], TPP2[:, 0:1024])
        s.wait_ge(s_pre, 28 * 16)
        s.copy(out=maug[:, 0:256], in_=stage[:, 0:256])
        s.copy(out=maug[:, 257:513], in_=stage[:, 256:512])
        s.copy(out=maug[:, 256:257], in_=stage[:, 768:769])
        s.copy(out=maug[:, 513:514], in_=stage[:, 769:770])
        s.activation(out=stage[0:1, 771:772], in_=stage[0:1, 770:771],
                     func=AF.Sigmoid)
        s.drain().then_inc(s_set, 1)
        sevac(rwT[:].rearrange("p (kt e) -> p kt e", e=8),
              PQ[:, 0:128].rearrange("p (kt e) -> p kt e", e=16)[:, :, 0:8])
        tp[0] += 1
        s.wait_ge(s_tp, tp[0])
        s.drain().then_inc(s_ev, 1)
        sevac(b1s[:].rearrange("p (e ft) -> p ft e", ft=32),
              PS[:, 0:512].rearrange("p (ft e) -> p ft e", e=16)[:, :, 0:8])
        tp[0] += 1
        s.wait_ge(s_tp, tp[0])
        s.nop(cycle_cnt=CDEL)
        s.copy(out=ohbc[:], in_=PN[:, 0:8])
        s.copy(out=gbc[:], in_=PN[:, 16:18])
        s.drain().then_inc(s_ev, 1)
        assert tp[0] == SETUP_TP
        s.wait_ge(s_set, 5)
        s.drain().then_inc(s_set, 1)
        for i in range(NCHT):
            b, c = divmod(i, NCH)
            half = i % 2
            xh = xin[:, half * 4096:(half + 1) * 4096]
            TPB = SETUP_TP + i * TP_A
            EVB = SETUP_EV + i * EV_A
            VXB = i * VX_A
            LNB = i * LN_A
            s.wait_ge(s_vx, VXB + VX_LN1)
            s.activation(out=sst[:, 0:4], in_=stat_s[:], func=AF.Ln,
                         bias=epsc[:, 0:1])
            sfl(s)
            s.activation(out=stat_s[:], in_=sst[:, 0:4], func=AF.Exp, scale=-0.5)
            s.drain().then_inc(s_ev, 1)
            s.wait_ge(s_vx, VXB + VX_LNB1)
            for tt in range(NT):
                s.activation(out=xln_tm[:, tt * 1024:(tt + 1) * 1024],
                             in_=xh[:, tt * 1024:(tt + 1) * 1024],
                             func=AF.Identity,
                             bias=stat_b[:, tt:tt + 1], scale=stat_s[:, tt:tt + 1])
            s.drain().then_inc(s_ln, 1)
            for gq in range(4):
                TPx = TPP1 if gq % 2 == 0 else TPP2
                s.wait_ge(s_tp, TPB + TP_XLN0 + gq)
                s.nop(cycle_cnt=SDEL)
                s.copy(out=xlnT[:, gq * 1024:(gq + 1) * 1024], in_=TPx[:, 0:1024])
                s.drain().then_inc(s_ev, 1)
            s.wait_ge(s_tp, TPB + TP_QK)
            s.nop(cycle_cnt=CDEL)
            for j, P in enumerate((PQ, PS, PN, PM)):
                s.activation(out=eluR[:, j * 512:(j + 1) * 512], in_=P[:],
                             func=AF.Relu)
                s.activation(out=x2ln_f[:, j * 512:(j + 1) * 512], in_=P[:],
                             func=AF.Relu, scale=-1.0)
            for j in range(4):
                dst = sqT if j < 2 else skT
                s.activation(out=dst[:, (j % 2) * 512:(j % 2 + 1) * 512],
                             in_=x2ln_f[:, j * 512:(j + 1) * 512],
                             func=AF.Exp, scale=-1.0)
            s.nop(cycle_cnt=SDEL)
            s.drain().then_inc(s_ev, 1)
            s.wait_ge(s_tp, TPB + TP_V)
            s.nop(cycle_cnt=CDEL)
            for j, P in enumerate((PQ, PS, PN, PM)):
                s.copy(out=v_aug[:, j * 257:j * 257 + 256], in_=P[:, 0:256])
            s.drain().then_inc(s_ev, 1)
            s.wait_ge(s_tp, TPB + TP_SC)
            s.nop(cycle_cnt=CDEL)
            for j, P in enumerate((PQ, PS, PN, PM)):
                s.copy(out=scm[:, j * 512:(j + 1) * 512], in_=P[:])
            s.drain().then_inc(s_ev, 1)
            s.wait_ge(s_tp, TPB + TP_SKT)
            s.nop(cycle_cnt=SDEL)
            s.copy(out=sk_tm[:], in_=TPP1[:, 0:1024])
            s.drain().then_inc(s_ev, 1)
            for it in range(NT):
                s.wait_ge(s_tp, TPB + TP_NM0 + it)
                if it >= 2:
                    s.wait_ge(s_vx, VXB + VX_CMB0 + it - 2)
                s.nop(cycle_cnt=CDEL)
                hN = it % 2
                s.copy(out=nev[:, hN * 257:hN * 257 + 257],
                       in_=(PQ if it % 2 == 0 else PS)[:, 0:257])
                s.copy(out=mev[:, hN * 257:hN * 257 + 257],
                       in_=(PN if it % 2 == 0 else PM)[:, 0:257])
                s.drain().then_inc(s_ev, 1)
            s.wait_ge(s_tp, TPB + TP_CMBT)
            s.nop(cycle_cnt=SDEL)
            s.copy(out=combT[:], in_=TPP2[:, 0:1024])
            s.drain().then_inc(s_ev, 1)
            s.wait_ge(s_tp, TPB + TP_SUP)
            s.nop(cycle_cnt=CDEL)
            s.copy(out=Sup[:, 0:257], in_=PQ[:, 0:257])
            s.copy(out=Sup[:, 257:514], in_=PS[:, 0:257])
            s.drain().then_inc(s_ev, 1)
            for it in range(NT):
                s.wait_ge(s_tp, TPB + TP_AO0 + it)
                if i > 0 and it == 0:
                    s.wait_ge(s_vx, (i - 1) * VX_A + VX_OWN)
                    s.wait_ge(s_sc, i * 32)
                s.nop(cycle_cnt=CDEL)
                s.copy(out=x2_t[:, it * 1024:(it + 1) * 1024], in_=PA[:])
                s.drain().then_inc(s_ev, 1)
            s.wait_ge(s_vx, VXB + VX_X2LN)
            s.activation(out=sst[:, 0:4], in_=stat_s[:], func=AF.Ln,
                         bias=epsc[:, 0:1])
            sfl(s)
            s.activation(out=stat_s[:], in_=sst[:, 0:4], func=AF.Exp, scale=-0.5)
            s.drain().then_inc(s_ev, 1)
            s.wait_ge(s_vx, VXB + VX_LNB2)
            for tt in range(NT):
                s.activation(out=x2ln_f[:, tt * 1024:(tt + 1) * 1024],
                             in_=x2_t[:, tt * 1024:(tt + 1) * 1024],
                             func=AF.Identity,
                             bias=stat_b[:, tt:tt + 1], scale=stat_s[:, tt:tt + 1])
            s.drain().then_inc(s_ln, 1)
            for gq in range(8):
                s.wait_ge(s_tp, TPB + TP_X2T0 + gq)
                s.nop(cycle_cnt=SDEL)
                s.copy(out=x2lnT_f[:, gq * 512:(gq + 1) * 512],
                       in_=(PQ if gq % 2 == 0 else PS)[:])
                s.drain().then_inc(s_ev, 1)
            s.wait_ge(s_tp, TPB + TP_LG)
            s.nop(cycle_cnt=CDEL)
            s.copy(out=lgT_sb[:], in_=PN[0:8, 0:512])
            s.drain().then_inc(s_ev, 1)
            s.wait_ge(s_tp, TPB + TP_LGT)
            s.nop(cycle_cnt=CDEL)
            s.copy(out=lg_tm[:].rearrange("p (it e) -> p it e", e=8),
                   in_=PM[:, 0:64].rearrange("p (it e) -> p it e", e=16)[:, :, 0:8])
            s.drain().then_inc(s_ev, 1)
            s.wait_ge(s_vx, VXB + VX_TOP1)
            s.nop(cycle_cnt=SDEL)
            for it in range(NT):
                s.activation(out=t8a[:], in_=lg_tm[:, it * 8:(it + 1) * 8],
                             func=AF.Exp, bias=t1b[:, it:it + 1],
                             accum_out=tden[:, it:it + 1])
            s.drain().then_inc(s_ev, 1)

    @block.vector
    def _(v):
        v.wait_ge(s_set, 3)
        v.nop(cycle_cnt=SDEL)
        v.memset(vjunk[:], 0.0)
        v.memset(epsc[:], EPS_LN)
        v.reduce_sum(stage[0:1, 781:782], stage[0:1, 512:768], axis=AX.X)
        v.memset(ones_row[:], 1.0)
        vfl(v)
        v.tensor_scalar(stage[0:1, 781:782], stage[0:1, 781:782], EPS_ATT, None,
                        op0=ALU.is_ge)
        v.memset(ones_col_bf[:], 1.0)
        vfl(v)
        v.tensor_tensor(stage[0:1, 782:783], stage[0:1, 771:772],
                        stage[0:1, 781:782], ALU.mult)
        v.tensor_scalar(stage[0:1, 783:784], stage[0:1, 771:772], -1.0, 1.0,
                        op0=ALU.mult, op1=ALU.add)
        v.memset(S_f[:], 0.0)
        v.memset(S_bf[:], 0.0)
        v.memset(P_all[:], 0.0)
        v.memset(E_all[:], 0.0)
        for j in range(4):
            v.memset(v_aug[:, j * 257 + 256:j * 257 + 257], 1.0)
        v.drain().then_inc(s_set, 1)
        for i in range(NCHT):
            b, c = divmod(i, NCH)
            half = i % 2
            xh = xin[:, half * 4096:(half + 1) * 4096]
            EVB = SETUP_EV + i * EV_A
            VXB = i * VX_A
            col0 = b * 32 + c * 4
            v.wait_ge(s_xld, (i + 1) * 16)
            v.nop(cycle_cnt=SDEL)
            for tt in range(NT):
                xt_ = xh[:, tt * 1024:(tt + 1) * 1024]
                v.reduce_sum(t1a[:, tt:tt + 1], xt_, axis=AX.X)
                v.tensor_tensor(x2ln_f[:, tt * 1024:(tt + 1) * 1024], xt_, xt_,
                                ALU.mult)
                v.reduce_sum(t1b[:, tt:tt + 1],
                             x2ln_f[:, tt * 1024:(tt + 1) * 1024], axis=AX.X)
            vfl(v)
            v.tensor_scalar(t1a[:], t1a[:], 1.0 / H, None, op0=ALU.mult)
            v.tensor_scalar(t1b[:], t1b[:], 1.0 / H, None, op0=ALU.mult)
            vfl(v)
            v.tensor_tensor(t8a[:, 0:4], t1a[:], t1a[:], ALU.mult)
            vfl(v)
            v.scalar_tensor_tensor(out=stat_s[:], in0=t8a[:, 0:4], scalar=-1.0,
                                   in1=t1b[:], op0=ALU.mult, op1=ALU.add)
            v.drain().then_inc(s_vx, 1)
            v.wait_ge(s_ev, EVB + EV_RSTD1)
            v.scalar_tensor_tensor(out=stat_b[:], in0=t1a[:], scalar=-1.0,
                                   in1=stat_s[:], op0=ALU.mult, op1=ALU.mult)
            v.drain().then_inc(s_vx, 1)
            v.wait_ge(s_ev, EVB + EV_QK)
            v.nop(cycle_cnt=SDEL)
            for j in range(2):
                v.tensor_tensor(sqT[:, j * 512:(j + 1) * 512],
                                sqT[:, j * 512:(j + 1) * 512],
                                eluR[:, j * 512:(j + 1) * 512], ALU.add)
            for j in range(2):
                v.tensor_tensor(skT[:, j * 512:(j + 1) * 512],
                                skT[:, j * 512:(j + 1) * 512],
                                eluR[:, (2 + j) * 512:(3 + j) * 512], ALU.add)
            v.drain().then_inc(s_vx, 1)
            v.wait_ge(s_ev, EVB + EV_SCM)
            v.nop(cycle_cnt=SDEL)
            for j in range(4):
                v.tensor_tensor(scm[:, j * 512:(j + 1) * 512],
                                scm[:, j * 512:(j + 1) * 512],
                                trimask[:, j * 512:(j + 1) * 512], ALU.mult)
            v.drain().then_inc(s_vx, 1)
            for it in range(NT):
                v.wait_ge(s_ev, EVB + EV_NM0 + it)
                v.nop(cycle_cnt=SDEL)
                hN = it % 2
                nv = nev[:, hN * 257:hN * 257 + 257]
                mv = mev[:, hN * 257:hN * 257 + 257]
                v.tensor_scalar(t8a[:, 0:1], nv[:, 256:257], EPS_ATT, None,
                                op0=ALU.max)
                v.tensor_scalar(t8a[:, 2:3], mv[:, 256:257], EPS_ATT, None,
                                op0=ALU.max)
                vfl(v)
                v.reciprocal(t8a[:, 1:2], t8a[:, 0:1])
                v.reciprocal(t8a[:, 3:4], t8a[:, 2:3])
                vfl(v)
                v.tensor_tensor(t8a[:, 5:6], t8a[:, 3:4], gbc[:, 0:1], ALU.mult)
                v.tensor_tensor(t8a[:, 4:5], t8a[:, 1:2], gbc[:, 1:2], ALU.mult)
                vfl(v)
                v.tensor_scalar(x2ln_f[:, 0:256], mv[:, 0:256], t8a[:, 5:6],
                                None, op0=ALU.mult)
                v.scalar_tensor_tensor(out=comb[:, it * 256:(it + 1) * 256],
                                       in0=nv[:, 0:256], scalar=t8a[:, 4:5],
                                       in1=x2ln_f[:, 0:256],
                                       op0=ALU.mult, op1=ALU.add)
                v.drain().then_inc(s_vx, 1)
            v.wait_ge(s_ev, EVB + EV_SUP)
            v.nop(cycle_cnt=SDEL)
            if c == NCH - 1:
                v.memset(S_f[:], 0.0)
                v.memset(S_bf[:], 0.0)
            else:
                v.tensor_tensor(S_f[:], S_f[:], Sup[:], ALU.add)
                v.tensor_copy(out=S_bf[:], in_=S_f[:])
            v.drain().then_inc(s_vx, 1)
            v.wait_ge(s_ev, EVB + EV_AO0 + 3)
            v.nop(cycle_cnt=SDEL)
            for tt in range(NT):
                v.tensor_tensor(x2_t[:, tt * 1024:(tt + 1) * 1024],
                                x2_t[:, tt * 1024:(tt + 1) * 1024],
                                xh[:, tt * 1024:(tt + 1) * 1024], ALU.add)
            for tt in range(NT):
                xt_ = x2_t[:, tt * 1024:(tt + 1) * 1024]
                v.reduce_sum(t1a[:, tt:tt + 1], xt_, axis=AX.X)
                v.tensor_tensor(x2ln_f[:, tt * 1024:(tt + 1) * 1024], xt_, xt_,
                                ALU.mult)
                v.reduce_sum(t1b[:, tt:tt + 1],
                             x2ln_f[:, tt * 1024:(tt + 1) * 1024], axis=AX.X)
            vfl(v)
            v.tensor_scalar(t1a[:], t1a[:], 1.0 / H, None, op0=ALU.mult)
            v.tensor_scalar(t1b[:], t1b[:], 1.0 / H, None, op0=ALU.mult)
            vfl(v)
            v.tensor_tensor(t8a[:, 0:4], t1a[:], t1a[:], ALU.mult)
            vfl(v)
            v.scalar_tensor_tensor(out=stat_s[:], in0=t8a[:, 0:4], scalar=-1.0,
                                   in1=t1b[:], op0=ALU.mult, op1=ALU.add)
            v.drain().then_inc(s_vx, 1)
            v.wait_ge(s_ev, EVB + EV_RSTD2)
            v.scalar_tensor_tensor(out=stat_b[:], in0=t1a[:], scalar=-1.0,
                                   in1=stat_s[:], op0=ALU.mult, op1=ALU.mult)
            v.drain().then_inc(s_vx, 1)
            v.wait_ge(s_ev, EVB + EV_X2T0 + 7)
            if i > 0:
                v.wait_ge(s_sc, i * 32)
            v.nop(cycle_cnt=SDEL)
            for kt in range(KT):
                v.tensor_scalar(x2sel[:, kt * 512:(kt + 1) * 512],
                                x2lnT_f[:, kt * 512:(kt + 1) * 512],
                                ohbc[:, b:b + 1], None, op0=ALU.mult)
            for tt in range(NT):
                v.tensor_scalar(x2_t[:, tt * 1024:(tt + 1) * 1024],
                                x2_t[:, tt * 1024:(tt + 1) * 1024],
                                ohbc[:, b:b + 1], None, op0=ALU.mult)
            v.drain().then_inc(s_vx, 1)
            v.wait_ge(s_ev, EVB + EV_LGTM)
            v.nop(cycle_cnt=SDEL)
            for it in range(NT):
                v.reduce_max(t1a[:, it:it + 1], lg_tm[:, it * 8:(it + 1) * 8],
                             axis=AX.X)
            vfl(v)
            for it in range(NT):
                v.tensor_scalar(x2ln_f[:, it * 8:(it + 1) * 8],
                                lg_tm[:, it * 8:(it + 1) * 8],
                                t1a[:, it:it + 1], None, op0=ALU.is_ge)
            vfl(v)
            for it in range(NT):
                v.scalar_tensor_tensor(out=x2ln_f[:, 64 + it * 8:64 + (it + 1) * 8],
                                       in0=x2ln_f[:, it * 8:(it + 1) * 8],
                                       scalar=-1000.0, in1=iota8[:, 8:16],
                                       op0=ALU.mult, op1=ALU.add)
            vfl(v)
            for it in range(NT):
                v.tensor_reduce(E_all[:, col0 + it:col0 + it + 1],
                                x2ln_f[:, 64 + it * 8:64 + (it + 1) * 8],
                                axis=AX.X, op=ALU.min)
            v.tensor_scalar(t1b[:], t1a[:], -1.0, None, op0=ALU.mult)
            v.drain().then_inc(s_vx, 1)
            v.wait_ge(s_ev, EVB + EV_TEXP)
            v.nop(cycle_cnt=SDEL)
            v.reciprocal(P_all[:, col0:col0 + 4], tden[:, 0:4])
            v.drain().then_inc(s_vx, 1)

    @block.tensor
    def _(t):
        t.wait_ge(s_set, 1)
        gidx = 0
        for srcoff, TPx in ((0, TPP1), (0, TPP2), (1024, TPP1), (1024, TPP2)):
            srcbuf = cvA if TPx is TPP1 else cvB
            if gidx >= 2:
                t.wait_ge(s_ev, gidx - 1)
            for kt in range(KT):
                t.transpose(TPx[:, kt * 128:(kt + 1) * 128],
                            srcbuf[:, srcoff + kt * 128:srcoff + (kt + 1) * 128],
                            id_bf[:])
            t.drain().then_inc(s_tp, 1)
            gidx += 1
        t.wait_ge(s_set, 2)
        for srcoff, TPx in ((0, TPP1), (0, TPP2), (1024, TPP1), (1024, TPP2)):
            srcbuf = cvA if TPx is TPP1 else cvB
            t.wait_ge(s_ev, gidx - 1)
            for kt in range(KT):
                t.transpose(TPx[:, kt * 128:(kt + 1) * 128],
                            srcbuf[:, srcoff + kt * 128:srcoff + (kt + 1) * 128],
                            id_bf[:])
            t.drain().then_inc(s_tp, 1)
            gidx += 1
        t.wait_ge(s_set, 3)
        t.wait_ge(s_pre, 28 * 16)
        for kt in range(KT):
            t.transpose(PQ[:, kt * 16:kt * 16 + 8],
                        stage[0:8, 1024 + kt * 128:1024 + (kt + 1) * 128],
                        id_f32[0:8, 0:8])
        t.drain().then_inc(s_tp, 1)
        gidx += 1
        for ft in range(16):
            t.transpose(PS[:, ft * 16:ft * 16 + 8],
                        stage[0:8, 2048 + ft * 128:2048 + (ft + 1) * 128],
                        id_f32[0:8, 0:8])
        t.drain().then_inc(s_tp, 1)
        gidx += 1
        t.wait_ge(s_pre, 29 * 16)
        for ft in range(16):
            t.transpose(PS[:, (16 + ft) * 16:(16 + ft) * 16 + 8],
                        stage[0:8, 2048 + ft * 128:2048 + (ft + 1) * 128],
                        id_f32[0:8, 0:8])
        t.drain().then_inc(s_tp, 1)
        gidx += 1
        t.wait_ge(s_set, 4)
        t.matmul(PN[:, 0:8], ones_row[0:1, :], stage[0:1, 772:780],
                 start=True, stop=True)
        t.matmul(PN[:, 16:18], ones_row[0:1, :], stage[0:1, 782:784],
                 start=True, stop=True)
        t.drain().then_inc(s_tp, 1)
        gidx += 1
        assert gidx == SETUP_TP
        t.drain().then_inc(s_set, 1)
        t.wait_ge(s_set, SET_TOTAL)
        for _ in range(64):
            t.matmul(PA[:, 0:128], id_bf[:], id_bf[:], start=True, stop=True)
        for i in range(NCHT):
            b, c = divmod(i, NCH)
            half = i % 2
            TPB = SETUP_TP + i * TP_A
            EVB = SETUP_EV + i * EV_A
            EVBp = SETUP_EV + (i - 1) * EV_A
            VXB = i * VX_A
            LNB = i * LN_A
            t.wait_ge(s_ln, LNB + 1)
            if i > 0:
                t.wait_ge(s_ev, EVBp + EV_CMBT)
            for gq in range(4):
                TPx = TPP1 if gq % 2 == 0 else TPP2
                if gq >= 2:
                    t.wait_ge(s_ev, EVB + EV_XLNT0 + gq - 2)
                for ktp in range(2):
                    kt = gq * 2 + ktp
                    for tt in range(NT):
                        t.transpose(TPx[:, ktp * 512 + tt * 128:ktp * 512 + (tt + 1) * 128],
                                    xln_tm[:, tt * 1024 + kt * 128:tt * 1024 + (kt + 1) * 128],
                                    id_bf[:])
                t.drain().then_inc(s_tp, 1)
            t.wait_ge(s_ev, EVB + EV_XLNT0 + 3)
            if i > 0:
                t.wait_ge(s_ev, EVBp + EV_LGTM)
            for dh, P in ((0, PQ), (1, PS)):
                for kt in range(KT):
                    t.matmul(P[:], wqT[:, dh * 1024 + kt * 128:dh * 1024 + (kt + 1) * 128],
                             xlnT[:, kt * 512:(kt + 1) * 512],
                             start=(kt == 0), stop=(kt == KT - 1))
                    t.nop(cycle_cnt=PACE2)
            for dh, P in ((0, PN), (1, PM)):
                for kt in range(KT):
                    t.matmul(P[:], wkT[:, dh * 1024 + kt * 128:dh * 1024 + (kt + 1) * 128],
                             xlnT[:, kt * 512:(kt + 1) * 512],
                             start=(kt == 0), stop=(kt == KT - 1))
                    t.nop(cycle_cnt=PACE2)
            t.drain().then_inc(s_tp, 1)
            t.wait_ge(s_ev, EVB + EV_QK)
            for jt, P in enumerate((PQ, PS, PN, PM)):
                for kt in range(KT):
                    t.matmul(P[:, 0:256],
                             xlnT[:, kt * 512 + jt * 128:kt * 512 + (jt + 1) * 128],
                             wvT[:].rearrange("p (dh kt2 c) -> p kt2 dh c",
                                              dh=2, c=128)[:, kt, :, :],
                             start=(kt == 0), stop=(kt == KT - 1))
                    t.nop(cycle_cnt=PACE1)
            t.drain().then_inc(s_tp, 1)
            t.wait_ge(s_vx, VXB + VX_SIG)
            t.wait_ge(s_ev, EVB + EV_V)
            t.nop(cycle_cnt=SDEL)
            for jt, P in enumerate((PQ, PS, PN, PM)):
                for dh in range(2):
                    t.matmul(P[:],
                             skT[:, dh * 512 + jt * 128:dh * 512 + (jt + 1) * 128],
                             sqT[:, dh * 512:(dh + 1) * 512],
                             start=(dh == 0), stop=(dh == 1))
                    t.nop(cycle_cnt=PACE2)
            t.drain().then_inc(s_tp, 1)
            for jt in range(NT):
                for dh in range(2):
                    t.transpose(TPP1[:, jt * 256 + dh * 128:jt * 256 + (dh + 1) * 128],
                                skT[:, dh * 512 + jt * 128:dh * 512 + (jt + 1) * 128],
                                id_bf[:])
            t.drain().then_inc(s_tp, 1)
            t.wait_ge(s_vx, VXB + VX_MSK)
            if i > 0:
                t.wait_ge(s_vx, (i - 1) * VX_A + VX_SREF)
            t.wait_ge(s_ev, EVB + EV_SCM)
            for it in range(NT):
                if it >= 2:
                    t.wait_ge(s_ev, EVB + EV_NM0 + it - 2)
                Pn = PQ if it % 2 == 0 else PS
                Pm = PN if it % 2 == 0 else PM
                for jt in range(NT):
                    t.matmul(Pn[:, 0:257],
                             scm[:, jt * 512 + it * 128:jt * 512 + (it + 1) * 128],
                             v_aug[:, jt * 257:(jt + 1) * 257],
                             start=(jt == 0), stop=False)
                    t.nop(cycle_cnt=PACE1)
                for dh in range(2):
                    t.matmul(Pn[:, 0:257],
                             sqT[:, dh * 512 + it * 128:dh * 512 + (it + 1) * 128],
                             S_bf[:, dh * 257:(dh + 1) * 257],
                             start=False, stop=(dh == 1))
                    t.nop(cycle_cnt=PACE1)
                for dh in range(2):
                    t.matmul(Pm[:, 0:257],
                             sqT[:, dh * 512 + it * 128:dh * 512 + (it + 1) * 128],
                             maug[:, dh * 257:(dh + 1) * 257],
                             start=(dh == 0), stop=(dh == 1))
                    t.nop(cycle_cnt=PACE1)
                t.drain().then_inc(s_tp, 1)
            t.wait_ge(s_vx, VXB + VX_CMB0 + 3)
            for it in range(NT):
                for eh in range(2):
                    t.transpose(TPP2[:, it * 256 + eh * 128:it * 256 + (eh + 1) * 128],
                                comb[:, it * 256 + eh * 128:it * 256 + (eh + 1) * 128],
                                id_bf[:])
            t.drain().then_inc(s_tp, 1)
            t.wait_ge(s_ev, EVB + EV_SKTM)
            t.wait_ge(s_ev, EVB + EV_NM0 + 3)
            for dh, P in ((0, PQ), (1, PS)):
                for jt in range(NT):
                    t.matmul(P[:, 0:257],
                             sk_tm[:, jt * 256 + dh * 128:jt * 256 + (dh + 1) * 128],
                             v_aug[:, jt * 257:(jt + 1) * 257],
                             start=(jt == 0), stop=(jt == NT - 1))
                    t.nop(cycle_cnt=PACE1)
            t.drain().then_inc(s_tp, 1)
            t.wait_ge(s_ev, EVB + EV_CMBT)
            for it in range(NT):
                if it >= 1:
                    t.wait_ge(s_ev, EVB + EV_AO0 + it - 1)
                for hhalf in range(2):
                    for eh in range(2):
                        t.matmul(PA[:, hhalf * 512:(hhalf + 1) * 512],
                                 combT[:, it * 256 + eh * 128:it * 256 + (eh + 1) * 128],
                                 woeh[:, eh * 1024 + hhalf * 512:eh * 1024 + (hhalf + 1) * 512],
                                 start=(eh == 0), stop=(eh == 1))
                        t.nop(cycle_cnt=PACE2)
                t.drain().then_inc(s_tp, 1)
            t.wait_ge(s_ln, LNB + 2)
            t.wait_ge(s_ev, EVB + EV_SUP)
            for gq in range(KT):
                P = PQ if gq % 2 == 0 else PS
                if gq >= 2:
                    t.wait_ge(s_ev, EVB + EV_X2T0 + gq - 2)
                for tt in range(NT):
                    t.transpose(P[:, tt * 128:(tt + 1) * 128],
                                x2ln_f[:, tt * 1024 + gq * 128:tt * 1024 + (gq + 1) * 128],
                                id_f32[:])
                t.drain().then_inc(s_tp, 1)
            t.wait_ge(s_ev, EVB + EV_X2T0 + 7)
            for kt in range(KT):
                t.matmul(PN[0:8, 0:512], rwT[:, kt * 8:(kt + 1) * 8],
                         x2lnT_f[:, kt * 512:(kt + 1) * 512],
                         start=(kt == 0), stop=(kt == KT - 1))
                t.nop(cycle_cnt=PACEF)
            t.drain().then_inc(s_tp, 1)
            t.wait_ge(s_ev, EVB + EV_LGT)
            t.nop(cycle_cnt=SDEL)
            for it in range(NT):
                t.transpose(PM[:, it * 16:it * 16 + 8],
                            lgT_sb[0:8, it * 128:(it + 1) * 128],
                            id_f32[0:8, 0:8])
            t.drain().then_inc(s_tp, 1)

    _ctx2.close()
    _ctx3 = contextlib.ExitStack()
    Pe = _ctx3.enter_context(nc.sbuf_tensor("Pe", [128, 2048], F32))
    geb = _ctx3.enter_context(nc.sbuf_tensor("geb", [128, 2048], BF16))
    cnt_sb = _ctx3.enter_context(nc.sbuf_tensor("cnt_sb", [1, 2048], F32))
    th = _ctx3.enter_context(nc.sbuf_tensor("th", [128, 32], F32))
    lohi = _ctx3.enter_context(nc.sbuf_tensor("lohi", [1, 32], F32))
    cnts = _ctx3.enter_context(nc.sbuf_tensor("cnts", [1, 16], F32))
    neg1 = _ctx3.enter_context(nc.sbuf_tensor("neg1", [128, 256], F32))
    eqtmp = _ctx3.enter_context(nc.sbuf_tensor("eqtmp", [128, 256], F32))
    keep = _ctx3.enter_context(nc.sbuf_tensor("keep", [128, 256], F32))
    fE = _ctx3.enter_context(nc.sbuf_tensor("fE", [128, 256], F32))
    own_e = _ctx3.enter_context(nc.sbuf_tensor("own_e", [128, 32], F32))
    masksC = _ctx3.enter_context(nc.sbuf_tensor("masksC", [128, 256], F32))
    eqb = _ctx3.enter_context(nc.sbuf_tensor("eqb", [128, 256], F32))
    onehotT = _ctx3.enter_context(nc.sbuf_tensor("onehotT", [8, 4096], F32))
    w1s = _ctx3.enter_context(nc.sbuf_tensor("w1s", [128, 8192], BF16))
    x2u = _ctx3.enter_context(nc.sbuf_tensor("x2u", [128, 2 * 3072], BF16))
    w2s = _ctx3.enter_context(nc.sbuf_tensor("w2s", [128, 8192], BF16))
    htr = _ctx3.enter_context(nc.sbuf_tensor("htr", [128, 6 * 384], BF16))
    ystage = _ctx3.enter_context(nc.sbuf_tensor("ystage", [128, 3072], F32))
    acc = _ctx3.enter_context(nc.sbuf_tensor("acc", [128, 3072], F32))
    xres = _ctx3.enter_context(nc.sbuf_tensor("xres", [128, 3072], F32))
    fpa = _ctx3.enter_context(nc.psum_tensor("fpa", [128, 512], F32))
    fpb = _ctx3.enter_context(nc.psum_tensor("fpb", [128, 512], F32))
    fpy = _ctx3.enter_context(nc.psum_tensor("fpy", [128, 3072], F32))
    SRCH_STEPS = []
    if not DEV_SKIP_SEARCH:
        SRCH_STEPS.append("v_init")
    if not DEV_SKIP_SEARCH:
        for k in range(SEARCH_ITERS):
            SRCH_STEPS += ["v_mid", "t_bc", "s_th", "v_ge", "t_cnt", "s_cnt", "v_upd"]
        SRCH_STEPS += ["v_midhi", "t_bc", "s_th", "v_fin"]
        for grp in range(8):
            SRCH_STEPS += ["t_eqT%d" % grp, "s_eqv%d" % grp]
    SRCH_TOTAL = len(SRCH_STEPS)

    @block.sync
    def _(sync):
        sync.wait_ge(s_cv, CV_TOTAL)
        sync.wait_ge(s_sc, NCHT * 32)
        sync.wait_ge(s_vx, NCHT * VX_A)
        sync.wait_ge(s_ev, SETUP_EV + NCHT * EV_A)
        for u in range(NU):
            ci, e = divmod(u, E)
            tk0u, nttu = FFN_CHUNKS[ci]
            if u >= 2:
                sync.wait_ge(s_l1, (u - 2) * NFT + NFT + 2)
            sync.dma_start(
                out=x2u[:, (u % 2) * 3072:(u % 2) * 3072 + KT * nttu * 128]
                    .rearrange("r (kt t) -> r kt t", kt=KT),
                in_=x2ln_dram[:, :, tk0u:tk0u + nttu * 128]
                    .rearrange("kt p t -> p kt t"),
            ).then_inc(s_fw, 16)
            for p in range(8):
                G = u * 8 + p
                if G >= 2:
                    sync.wait_ge(s_l1, 4 * G - 2)
                sync.dma_start(
                    out=w1s[:, (G % 2) * 4096:(G % 2 + 1) * 4096]
                        .rearrange("r (ft kt c) -> r ft kt c", ft=4, kt=KT),
                    in_=W1bf[e, p * 4:(p + 1) * 4, :, :, :]
                        .rearrange("ft kt r c -> r ft kt c"),
                ).then_inc(s_fw, 16)
                if G >= 2:
                    sync.wait_ge(s_l1, 4 * G)
                sync.dma_start(
                    out=w2s[:, (G % 2) * 4096:(G % 2 + 1) * 4096]
                        .rearrange("r (ft m) -> r ft m", ft=4),
                    in_=W2bf[e, p * 4:(p + 1) * 4, :, :]
                        .rearrange("ft r m -> r ft m"),
                ).then_inc(s_fw, 16)

    @block.gpsimd
    def _(g):
        g.wait_ge(s_vx, NCHT * VX_A)
        g.wait_ge(s_ev, SETUP_EV + NCHT * EV_A)
        g.wait_ge(s_cv, CV_TOTAL)
        for ci in range(NCI):
            tk0, ntt = FFN_CHUNKS[ci]
            if ci >= 1:
                g.wait_ge(s_ac, (ci - 1) * 9 + 1)
            g.dma_start(
                out=xres[:, 0:ntt * 1024].rearrange("p (n h) -> p n h", n=ntt),
                in_=x2_scr[tk0:tk0 + ntt * 128, :]
                    .rearrange("(n p) h -> p n h", p=128),
            ).then_inc(s_fx, 16)
            g.wait_ge(s_ac, ci * 9 + 9)
            g.nop(cycle_cnt=2048)
            g.dma_start(
                out=out[tk0:tk0 + ntt * 128, :]
                    .rearrange("(n p) h -> p n h", p=128),
                in_=acc[:, 0:ntt * 1024].rearrange("p (n h) -> p n h", n=ntt),
            ).then_inc(s_st, 16)
        g.wait_ge(s_st, NCI * 16)

    @block.scalar
    def _(s):
        s.wait_ge(s_cv, CV_TOTAL)
        s.wait_ge(s_vx, NCHT * VX_A)
        s.wait_ge(s_sc, NCHT * 32)
        for k, step in enumerate(SRCH_STEPS):
            if step == "s_th":
                s.wait_ge(s_srch, k)
                s.nop(cycle_cnt=CDEL)
                s.copy(out=th[:, 0:8], in_=fpa[:, 0:8])
                s.drain().then_inc(s_srch, 1)
            elif step == "s_cnt":
                s.wait_ge(s_srch, k)
                s.nop(cycle_cnt=CDEL)
                s.copy(out=cnt_sb[:], in_=fpy[0:1, 0:2048])
                s.drain().then_inc(s_srch, 1)
            elif step.startswith("s_eqv"):
                grp = int(step[5:])
                s.wait_ge(s_srch, k)
                s.nop(cycle_cnt=CDEL)
                s.copy(out=onehotT[0:8, grp * 512:(grp + 1) * 512],
                       in_=(fpa if grp % 2 == 0 else fpb)[0:8, 0:512])
                s.drain().then_inc(s_srch, 1)
        for u in range(NU):
            ci, e = divmod(u, E)
            tk0, ntt = FFN_CHUNKS[ci]
            if e == 0:
                s.wait_ge(s_yd, ci * 9 + 1)
                if ci > 0:
                    s.wait_ge(s_ac, ci * 9)
                s.nop(cycle_cnt=CDEL)
                for tt in range(ntt):
                    s.copy(out=ystage[:, tt * 1024:(tt + 1) * 1024],
                           in_=fpy[:, tt * 1024:(tt + 1) * 1024])
                s.drain().then_inc(s_yc, 1)
            for f in range(NFT):
                fg = u * NFT + f
                s.wait_ge(s_l1, fg + 1)
                s.nop(cycle_cnt=SDEL)
                s.activation(out=htr[:, (fg % 6) * 384:(fg % 6) * 384 + ntt * 128],
                             in_=(fpa if fg % 2 == 0 else fpb)[:, 0:ntt * 128],
                             func=AF.Relu, bias=b1s[:, e * 32 + f:e * 32 + f + 1])
                s.nop(cycle_cnt=512)
                s.drain().then_inc(s_rl, 1)
            s.wait_ge(s_yd, ci * 9 + 1 + e + 1)
            s.wait_ge(s_ac, ci * 9 + 1 + e)
            s.nop(cycle_cnt=CDEL)
            for tt in range(ntt):
                s.copy(out=ystage[:, tt * 1024:(tt + 1) * 1024],
                       in_=fpy[:, tt * 1024:(tt + 1) * 1024])
            s.drain().then_inc(s_yc, 1)

    @block.vector
    def _(v):
        v.wait_ge(s_cv, CV_TOTAL)
        v.wait_ge(s_ev, SETUP_EV + NCHT * EV_A)
        v.wait_ge(s_sc, NCHT * 32)
        for k, step in enumerate(SRCH_STEPS):
            if not step.startswith("v"):
                continue
            v.wait_ge(s_srch, k)
            v.nop(cycle_cnt=SDEL)
            if step == "v_init":
                v.memset(neg1[:], -1.0)
                v.memset(lohi[0:1, 0:8], 0.05)
                v.memset(lohi[0:1, 8:16], 1.01)
                v.memset(own_e[:], 0.0)
                v.tensor_scalar(keep[:], P_all[:], 1.0, None, op0=ALU.add)
                for e_ in range(E):
                    v.tensor_scalar(eqtmp[:], E_all[:], float(e_), None,
                                    op0=ALU.is_equal)
                    v.tensor_tensor(eqtmp[:], eqtmp[:], keep[:], ALU.mult)
                    v.tensor_scalar(Pe[:, e_ * 256:(e_ + 1) * 256], eqtmp[:], 1.0,
                                    None, op0=ALU.subtract)
            elif step == "v_mid":
                v.tensor_tensor(cnts[0:1, 8:16], lohi[0:1, 0:8],
                                lohi[0:1, 8:16], ALU.add)
                vfl(v)
                v.tensor_scalar(lohi[0:1, 16:24], cnts[0:1, 8:16], 0.5, None,
                                op0=ALU.mult)
            elif step == "v_ge":
                for e_ in range(E):
                    v.tensor_scalar(geb[:, e_ * 256:(e_ + 1) * 256],
                                    Pe[:, e_ * 256:(e_ + 1) * 256],
                                    th[:, e_:e_ + 1], None, op0=ALU.is_ge)
            elif step == "v_upd":
                for e_ in range(E):
                    v.reduce_sum(cnts[0:1, e_:e_ + 1],
                                 cnt_sb[0:1, e_ * 256:(e_ + 1) * 256], axis=AX.X)
                vfl(v)
                v.tensor_scalar(cnts[0:1, 8:16], cnts[0:1, 0:8], CAP, None,
                                op0=ALU.is_gt)
                vfl(v)
                v.scalar_tensor_tensor(out=th[0:1, 0:8], in0=lohi[0:1, 0:8],
                                       scalar=-1.0, in1=lohi[0:1, 16:24],
                                       op0=ALU.mult, op1=ALU.add)
                v.scalar_tensor_tensor(out=th[0:1, 8:16], in0=lohi[0:1, 8:16],
                                       scalar=-1.0, in1=lohi[0:1, 16:24],
                                       op0=ALU.mult, op1=ALU.add)
                v.tensor_scalar(cnts[0:1, 0:8], cnts[0:1, 8:16], -1.0, 1.0,
                                op0=ALU.mult, op1=ALU.add)
                vfl(v)
                v.tensor_tensor(th[0:1, 16:24], th[0:1, 0:8], cnts[0:1, 8:16],
                                ALU.mult)
                v.tensor_tensor(th[0:1, 24:32], th[0:1, 8:16], cnts[0:1, 0:8],
                                ALU.mult)
                vfl(v)
                v.tensor_tensor(lohi[0:1, 0:8], lohi[0:1, 0:8], th[0:1, 16:24],
                                ALU.add)
                v.tensor_tensor(lohi[0:1, 8:16], lohi[0:1, 8:16], th[0:1, 24:32],
                                ALU.add)
            elif step == "v_midhi":
                v.tensor_copy(out=lohi[0:1, 16:24], in_=lohi[0:1, 8:16])
            elif step == "v_fin":
                for e_ in range(E):
                    if e_ == 0:
                        v.tensor_scalar(keep[:], Pe[:, 0:256], th[:, 0:1],
                                        None, op0=ALU.is_ge)
                    else:
                        v.tensor_scalar(eqtmp[:], Pe[:, e_ * 256:(e_ + 1) * 256],
                                        th[:, e_:e_ + 1], None, op0=ALU.is_ge)
                        v.tensor_tensor(keep[:], keep[:], eqtmp[:], ALU.add)
                v.tensor_tensor(fE[:], E_all[:], keep[:], ALU.mult)
                for b_ in range(B):
                    v.scalar_tensor_tensor(out=own_e[:],
                                           in0=fE[:, b_ * 32:(b_ + 1) * 32],
                                           scalar=ohbc[:, b_:b_ + 1],
                                           in1=own_e[:], op0=ALU.mult,
                                           op1=ALU.add)
                    vfl(v)
                for e_ in range(E):
                    v.tensor_scalar(masksC[:, e_ * 32:(e_ + 1) * 32], own_e[:],
                                    float(e_), None, op0=ALU.is_equal)
                for tg in range(32):
                    v.tensor_scalar(eqb[:, tg * 8:(tg + 1) * 8], iota8[:, 0:8],
                                    own_e[:, tg:tg + 1], None, op0=ALU.is_equal)
            v.drain().then_inc(s_srch, 1)
        for ci in range(NCI):
            tk0, ntt = FFN_CHUNKS[ci]
            v.wait_ge(s_yc, ci * 9 + 1)
            v.wait_ge(s_fx, (ci + 1) * 16)
            if ci > 0:
                v.wait_ge(s_st, ci * 16)
            v.nop(cycle_cnt=SDEL)
            for tt in range(ntt):
                v.tensor_tensor(acc[:, tt * 1024:(tt + 1) * 1024],
                                ystage[:, tt * 1024:(tt + 1) * 1024],
                                xres[:, tt * 1024:(tt + 1) * 1024], ALU.add)
            v.drain().then_inc(s_ac, 1)
            for e in range(E):
                v.wait_ge(s_yc, ci * 9 + 2 + e)
                v.nop(cycle_cnt=SDEL)
                for tt in range(ntt):
                    v.scalar_tensor_tensor(
                        out=acc[:, tt * 1024:(tt + 1) * 1024],
                        in0=ystage[:, tt * 1024:(tt + 1) * 1024],
                        scalar=masksC[:, e * 32 + tk0 // 128 + tt:
                                      e * 32 + tk0 // 128 + tt + 1],
                        in1=acc[:, tt * 1024:(tt + 1) * 1024],
                        op0=ALU.mult, op1=ALU.add)
                v.drain().then_inc(s_ac, 1)

    @block.tensor
    def _(t):
        t.wait_ge(s_ev, SETUP_EV + NCHT * EV_A)
        t.wait_ge(s_vx, NCHT * VX_A)
        t.wait_ge(s_cv, CV_TOTAL)
        t.wait_ge(s_sc, NCHT * 32)
        for k, step in enumerate(SRCH_STEPS):
            if not step.startswith("t"):
                continue
            t.wait_ge(s_srch, k)
            if step == "t_bc":
                t.matmul(fpa[:, 0:8], ones_row[0:1, :], lohi[0:1, 16:24],
                         start=True, stop=True)
            elif step == "t_cnt":
                for e_ in range(E):
                    t.matmul(fpy[0:1, e_ * 256:(e_ + 1) * 256],
                             ones_col_bf[:], geb[:, e_ * 256:(e_ + 1) * 256],
                             start=True, stop=True)
                    t.nop(cycle_cnt=PACE1)
            elif step.startswith("t_eqT"):
                grp = int(step[5:])
                P = fpa if grp % 2 == 0 else fpb
                for tgl in range(4):
                    tg = grp * 4 + tgl
                    t.transpose(P[0:8, tgl * 128:(tgl + 1) * 128],
                                eqb[:, tg * 8:(tg + 1) * 8], id_f32[:])
            t.drain().then_inc(s_srch, 1)
        t.wait_ge(s_srch, SRCH_TOTAL)
        for _ in range(64):
            t.matmul(fpa[:, 0:128], id_bf[:], id_bf[:], start=True, stop=True)
        def emit_l2(u, ci, e, tk0, ntt, fl):
            fgl = u * NFT + fl
            if fl % 4 == 0:
                t.wait_ge(s_fw, (u * 16 + 2 * (fl // 4) + 2) * 16)
            if fl == 0:
                t.wait_ge(s_yc, ci * 9 + 1 + e)
            t.wait_ge(s_rl, min(u * NFT + fl + 2, u * NFT + NFT))
            slot = fgl % 6
            pw = (fl // 4) % 2
            for tt in range(ntt):
                for half in range(2):
                    t.matmul(fpy[:, tt * 1024 + half * 512:tt * 1024 + (half + 1) * 512],
                             htr[:, slot * 384 + tt * 128:slot * 384 + (tt + 1) * 128],
                             w2s[:, pw * 4096 + (fl % 4) * 1024 + half * 512:
                                 pw * 4096 + (fl % 4) * 1024 + (half + 1) * 512],
                             start=(fl == 0), stop=(fl == NFT - 1))
                    t.nop(cycle_cnt=PACE2)
        for u in range(NU):
            ci, e = divmod(u, E)
            tk0, ntt = FFN_CHUNKS[ci]
            if e == 0:
                if ci > 0:
                    t.wait_ge(s_yc, ci * 9)
                for tt in range(ntt):
                    tg = tk0 // 128 + tt
                    for half in range(2):
                        t.matmul(fpy[:, tt * 1024 + half * 512:tt * 1024 + (half + 1) * 512],
                                 onehotT[0:8, tg * 128:(tg + 1) * 128],
                                 b2s[0:8, half * 512:(half + 1) * 512],
                                 start=True, stop=True)
                        t.nop(cycle_cnt=PACEF)
                t.drain().then_inc(s_yd, 1)
            for f in range(NFT):
                fg = u * NFT + f
                if f == 0:
                    t.wait_ge(s_fw, (u * 17 + 1) * 16)
                if f % 4 == 0:
                    t.wait_ge(s_fw, (u * 17 + 2 * (f // 4) + 2) * 16)
                bank = fpa if fg % 2 == 0 else fpb
                if fg >= 2:
                    t.wait_ge(s_rl, fg - 1)
                pw = (f // 4) % 2
                for kt in range(KT):
                    t.matmul(bank[:, 0:ntt * 128],
                             w1s[:, pw * 4096 + (f % 4) * 1024 + kt * 128:
                                 pw * 4096 + (f % 4) * 1024 + (kt + 1) * 128],
                             x2u[:, (u % 2) * 3072 + kt * ntt * 128:
                                 (u % 2) * 3072 + (kt + 1) * ntt * 128],
                             start=(kt == 0), stop=(kt == KT - 1))
                    t.nop(cycle_cnt=PACE1)
                t.drain().then_inc(s_l1, 1)
                if f >= 2:
                    emit_l2(u, ci, e, tk0, ntt, f - 2)
            emit_l2(u, ci, e, tk0, ntt, NFT - 2)
            emit_l2(u, ci, e, tk0, ntt, NFT - 1)
            t.drain().then_inc(s_yd, 1)
    _ctx3.close()
    _ctx1.close()
    return nc


_CONSTS = None


def _host_consts():
    global _CONSTS
    if _CONSTS is None:
        identf = np.eye(128, dtype=np.float32)
        identb = np.eye(128, dtype=np.float32).astype(ml_dtypes.bfloat16)
        tri = np.zeros((128, 4 * 512), dtype=np.float32)
        for jt in range(4):
            p = np.arange(128)[:, None] + jt * 128
            i = np.arange(512)[None, :]
            tri[:, jt * 512:(jt + 1) * 512] = (p <= i).astype(np.float32)
        trimask = tri.astype(ml_dtypes.bfloat16)
        iotac = np.zeros((128, 16), dtype=np.float32)
        iotac[:, 0:8] = np.arange(8)[None, :]
        iotac[:, 8:16] = np.arange(8)[None, :] + 1000.0
        _CONSTS = (identf, identb, trimask, iotac)
    return _CONSTS


_NC = None


def _get_nc():
    global _NC
    if _NC is None:
        _NC = build_nc()
    return _NC


def kernel(L0_attn_hidden_states, L0_attn_w_q, L0_attn_w_k, L0_attn_w_v, L0_attn_w_o,
           L0_attn_gate, L0_attn_memory, L0_attn_memory_norm, L0_ffn_router_w,
           L0_ffn_W1, L0_ffn_b1, L0_ffn_W2, L0_ffn_b2):
    identf, identb, trimask, iotac = _host_consts()
    x = np.ascontiguousarray(np.asarray(L0_attn_hidden_states, dtype=np.float32))
    base = {
        "x_all": x,
        "w_q": np.ascontiguousarray(np.asarray(L0_attn_w_q, dtype=np.float32)),
        "w_k": np.ascontiguousarray(np.asarray(L0_attn_w_k, dtype=np.float32)),
        "w_v": np.ascontiguousarray(np.asarray(L0_attn_w_v, dtype=np.float32)),
        "w_o": np.ascontiguousarray(np.asarray(L0_attn_w_o, dtype=np.float32)),
        "gate": np.asarray(L0_attn_gate, dtype=np.float32).reshape(1, 1),
        "memory": np.ascontiguousarray(np.asarray(L0_attn_memory, dtype=np.float32)),
        "memnorm": np.asarray(L0_attn_memory_norm, dtype=np.float32).reshape(1, DM),
        "router": np.ascontiguousarray(np.asarray(L0_ffn_router_w, dtype=np.float32)),
        "W1": np.ascontiguousarray(np.asarray(L0_ffn_W1, dtype=np.float32)),
        "b1in": np.ascontiguousarray(np.asarray(L0_ffn_b1, dtype=np.float32)),
        "W2": np.ascontiguousarray(np.asarray(L0_ffn_W2, dtype=np.float32)),
        "b2in": np.ascontiguousarray(np.asarray(L0_ffn_b2, dtype=np.float32)),
        "identf": identf,
        "identb": identb,
        "trimask_in": trimask,
        "iotac": iotac,
    }
    in_maps = []
    for c in range(B):
        m = dict(base)
        oh = np.zeros((1, B), dtype=np.float32)
        oh[0, c] = 1.0
        m["onehot"] = oh
        in_maps.append(m)
    nc = _get_nc()
    res = run_bass_kernel_spmd(nc, in_maps, core_ids=list(range(B)))
    return np.stack([res.results[c]["out"] for c in range(B)], axis=0)
